# version 14
# baseline (speedup 1.0000x reference)
import numpy as np
from contextlib import ExitStack

import concourse.bass as bass
import concourse.mybir as mybir
from concourse.bass_utils import run_bass_kernel_spmd
import ml_dtypes

F32 = mybir.dt.float32
BF16 = mybir.dt.bfloat16
AF = mybir.ActivationFunctionType
ALU = mybir.AluOpType
AX = mybir.AxisListType.X
f32 = np.float32
bf16 = ml_dtypes.bfloat16

P = 128
D = 2048
DC = D // P
E = 8
H = 16
T = 2048
SEQ = 2
TOK = SEQ * T
STRIPE = 1024
NSTR = TOK // STRIPE
NCH = STRIPE // P
NTC = TOK // P
OQKV = 3 * D
NSH = SEQ * H
LN_EPS = 1e-5
RMS_EPS_QK = 1.1920929e-07
RMS_EPS_FFN = 1e-6
CSHIFT = -12.0
NEG = -1e9
HALF_LN128 = 0.5 * float(np.log(128.0))

ENGINES = ("sync", "gpsimd", "tensor", "scalar", "vector")


class _FakeInstr:
    def then_inc(self, *a, **k):
        return self


class _FakeEng:
    def __getattr__(self, name):
        return lambda *a, **k: _FakeInstr()


class EM:
    def __init__(self):
        self.mode = "plan"
        self.cnt = {}
        self.plan = {}
        self.sems = {}

    def inc(self, sem, instr, label=None, k=1):
        if self.mode == "plan":
            self.cnt[sem] = self.cnt.get(sem, 0) + k
            if label is not None:
                assert label not in self.plan, f"dup label {label}"
                self.plan[label] = (sem, self.cnt[sem])
        else:
            s = self.sems[sem]
            instr.then_inc(s.h, k)
            s.n += k
            if label is not None:
                assert self.plan[label] == (sem, s.n), (label, self.plan[label], sem, s.n)

    def wait(self, eng, label):
        if self.mode == "emit":
            sem, cnt = self.plan[label]
            eng.wait_ge(self.sems[sem].h, cnt)


class Sem:
    def __init__(self, ctx, nc, name):
        self.h = ctx.enter_context(nc.semaphore(name))
        self.n = 0


def build_program(upto="full", tap=None):
    nc = bass.Bass("TRN2", target_bir_lowering=False)

    d_x = nc.dram_tensor("x_in", [TOK, D], F32, kind="ExternalInput").ap()
    d_wqkv = nc.dram_tensor("w_qkv", [OQKV, D], F32, kind="ExternalInput").ap()
    d_wo = nc.dram_tensor("w_o", [D, D], F32, kind="ExternalInput").ap()
    d_rw = nc.dram_tensor("router_w", [D, E], F32, kind="ExternalInput").ap()
    d_rb = nc.dram_tensor("router_b", [1, E], F32, kind="ExternalInput").ap()
    d_ew = nc.dram_tensor("expert_w", [E, D, D], F32, kind="ExternalInput").ap()
    d_ebl = nc.dram_tensor("expert_b_laid", [P, DC * E], F32, kind="ExternalInput").ap()
    d_pw = nc.dram_tensor("proj_w", [D, D], F32, kind="ExternalInput").ap()
    d_pb = nc.dram_tensor("proj_b", [1, D], F32, kind="ExternalInput").ap()
    d_nw = nc.dram_tensor("norm_w", [1, D], F32, kind="ExternalInput").ap()
    d_ident = nc.dram_tensor("c_ident", [P, P], BF16, kind="ExternalInput").ap()
    d_pm = nc.dram_tensor("c_rope_pm", [P, P], BF16, kind="ExternalInput").ap()
    d_cos = nc.dram_tensor("c_cos", [P, TOK], BF16, kind="ExternalInput").ap()
    d_sin = nc.dram_tensor("c_sin", [P, TOK], BF16, kind="ExternalInput").ap()
    d_maskt = nc.dram_tensor("c_maskt", [P, P], F32, kind="ExternalInput").ap()
    d_out = nc.dram_tensor("x_out", [TOK, D], F32, kind="ExternalOutput").ap()

    d_ewb = nc.dram_tensor("s_ewb", [E, D, D], BF16, kind="Internal").ap()
    d_pwb = nc.dram_tensor("s_pwb", [D, D], BF16, kind="Internal").ap()
    d_qkvb = nc.dram_tensor("s_qkvb", [OQKV, D], BF16, kind="Internal").ap()
    d_qkvT = nc.dram_tensor("s_qkvT", [D, OQKV], BF16, kind="Internal").ap()
    d_wob = nc.dram_tensor("s_wob", [D, D], BF16, kind="Internal").ap()
    d_x1 = nc.dram_tensor("s_x1", [TOK, D], F32, kind="Internal").ap()
    d_x2 = nc.dram_tensor("s_x2", [TOK, D], F32, kind="Internal").ap()
    d_qt = nc.dram_tensor("s_qt", [D, TOK], BF16, kind="Internal").ap()
    d_kt = nc.dram_tensor("s_kt", [D, TOK], BF16, kind="Internal").ap()
    d_v = nc.dram_tensor("s_v", [TOK, D], BF16, kind="Internal").ap()
    d_at = nc.dram_tensor("s_at", [D, TOK], BF16, kind="Internal").ap()

    ctx = ExitStack()
    with ctx:
        def sbuf(name, shape, dt=F32):
            return ctx.enter_context(nc.sbuf_tensor(name, shape, dt))

        xt_sb = sbuf("xt_sb", [P, DC * STRIPE], BF16)
        comb_sb = sbuf("comb_sb", [P, DC * STRIPE], BF16)
        wbuf = sbuf("wbuf", [P, 2 * DC * STRIPE], BF16)
        xch = sbuf("xch", [P, 3 * D])
        xln_sb = sbuf("xln_sb", [P, 2 * D], BF16)
        wbc_sb = sbuf("wbc_sb", [P, E * STRIPE], BF16)
        nwb_sb = sbuf("nwb_sb", [P, D], BF16)
        sq_sb = sbuf("sq_sb", [P, 512], BF16)
        y_sb = sbuf("y_sb", [P, D])
        tmp_bf = sbuf("tmp_bf", [P, 2048], BF16)
        qo2_sb = sbuf("qo2_sb", [P, 512], BF16)
        qp2_sb = sbuf("qp2_sb", [P, 512], BF16)
        ident = sbuf("ident", [P, P], BF16)
        pm_sb = sbuf("pm_sb", [P, P], BF16)
        maskt_sb = sbuf("maskt_sb", [P, P])
        ones_col = sbuf("ones_col", [P, 1], BF16)
        ones_row = sbuf("ones_row", [1, P], BF16)
        bias_sb = sbuf("bias_sb", [P, DC * E])
        rb_bc = sbuf("rb_bc", [P, E])
        rw_sb = sbuf("rw_sb", [P, DC * E], BF16)
        pbrow = sbuf("pbrow", [1, D], BF16)
        wrow_sb = sbuf("wrow_sb", [1, E * P], BF16)
        rms_rowb = sbuf("rms_rowb", [1, 512], BF16)
        stats = sbuf("stats", [P, 4 * NCH])
        logit_sb = sbuf("logit_sb", [P, E])
        m1_sb = sbuf("m1_sb", [P, 1])
        l2_sb = sbuf("l2_sb", [P, E])
        m2_sb = sbuf("m2_sb", [P, 1])
        nm1_sb = sbuf("nm1_sb", [P, 1])
        keep_sb = sbuf("keep_sb", [P, E])
        es_sb = sbuf("es_sb", [P, E])
        sk_sb = sbuf("sk_sb", [P, E])
        w8_sb = sbuf("w8_sb", [P, E], BF16)
        red_sb = sbuf("red_sb", [P, 16])
        rinv_sb = sbuf("rinv_sb", [P, 2])

        pb2 = [ctx.enter_context(nc.psum_tensor(f"pb2_{i}", [P, 1024], F32)) for i in range(4)]

        class _PB:
            def __getitem__(self, i):
                return pb2[i // 2][:, (i % 2) * 512:(i % 2 + 1) * 512]

        pbk = _PB()

        def pbb(i):
            return pb2[i // 2].bitcast(BF16)[:, (i % 2) * 1024:(i % 2 + 1) * 1024]

        em = EM()
        SEM_NAMES = ["s_cast", "s_ld", "s_xl", "s_st", "s_ast", "s_mm", "s_pe2", "s_wfree",
                     "s_act", "s_act2", "s_ve", "s_ve2", "s_pfree", "s_xfa", "s_xfb"]
        for sn in SEM_NAMES:
            em.sems[sn] = Sem(ctx, nc, sn)

        def wslot(j2):
            return wbuf[:, j2 * 16384:(j2 + 1) * 16384]

        def a1slot(j4):
            return wbuf[:, j4 * 8192:(j4 + 1) * 8192]

        COS_OFF = 0
        SIN_OFF = TOK

        QK_TILES = []
        V_TILES = []
        for _s in range(NSTR):
            for _o in range(12):
                if _o < 8:
                    for _oc in range(4):
                        for _th in range(2):
                            QK_TILES.append((_s, _o, _oc, _th))
                else:
                    for _tc in range(NCH):
                        V_TILES.append((_s, _o, _tc))
        QK_IDX = {t: i for i, t in enumerate(QK_TILES)}
        V_IDX = {t: i for i, t in enumerate(V_TILES)}

        def phase_prep(which, eng):
            if which != "gpsimd":
                return
            g = eng
            em.inc("s_cast", g.dma_start(
                out=rw_sb[:, :].rearrange("p (c e) -> p c e", e=E),
                in_=d_rw.rearrange("(c p) e -> p c e", p=P)), None, 16)
            em.inc("s_cast", g.dma_start(out=pbrow[:, :], in_=d_pb), None, 16)
            em.inc("s_cast", g.dma_start(
                out=nwb_sb[:, :], in_=d_nw[:, :].partition_broadcast(P)[:, 0, :]), None, 16)
            g.memset(ones_col[:, :], 1.0)
            g.memset(ones_row[:, :], 1.0)
            em.inc("s_cast", g.dma_start(out=ident[:, :], in_=d_ident), None, 16)
            em.inc("s_cast", g.dma_start(out=pm_sb[:, :], in_=d_pm), None, 16)
            em.inc("s_cast", g.dma_start(out=maskt_sb[:, :], in_=d_maskt), None, 16)
            em.inc("s_cast", g.dma_start(out=bias_sb[:, :], in_=d_ebl), None, 16)
            em.inc("s_cast", g.dma_start(
                out=rb_bc[:, :], in_=d_rb[:, :].partition_broadcast(P)[:, 0, :]),
                ("prep_done",), 16)
            em.inc("s_cast", g.dma_start(out=d_qkvb, in_=d_wqkv), ("cast_qkv",), 16)
            em.inc("s_cast", g.dma_start(out=d_wob, in_=d_wo), ("cast_wo",), 16)

        def phase_F(which, eng, fp, xin, xout, parts=("pro", "exp", "proj"), nstr=NSTR,
                    sync_hook=None):
            L = lambda *a: ("F", fp) + a
            NSLOT = 18

            def SO(s):
                return (s % 2) * 2 * NCH

            if which == "gpsimd":
                g_ = eng
                for s_ in range(nstr):
                    for e_ in range(E):
                        for fh_ in range(2):
                            j_ = e_ * 2 + fh_
                            prev_ = s_ * NSLOT + j_ - 2
                            if prev_ >= 0:
                                em.wait(g_, L("wfree", prev_ // NSLOT, prev_ % NSLOT))
                            elif fp > 0:
                                em.wait(g_, ("A3", "mm", NTC - 1, 3))
                            em.inc("s_cast", g_.dma_start(
                                out=wslot(j_ % 2).rearrange("p (c f) -> p c f", f=1024),
                                in_=d_ew[e_][:, fh_ * 1024:(fh_ + 1) * 1024]
                                .rearrange("(c p) f -> p c f", p=P)), L("wl", s_, j_), 16)
                    for fh_ in range(2):
                        j_ = 16 + fh_
                        prev_ = s_ * NSLOT + j_ - 2
                        em.wait(g_, L("wfree", prev_ // NSLOT, prev_ % NSLOT))
                        em.inc("s_cast", g_.dma_start(
                            out=wslot(j_ % 2).rearrange("p (c f) -> p c f", f=1024),
                            in_=d_pw[:, fh_ * 1024:(fh_ + 1) * 1024]
                            .rearrange("(c p) f -> p c f", p=P)), L("wl", s_, j_), 16)
                if tap == "comb" and fp == 0:
                    em.wait(g_, ("F", 0, "comb", 0, 255))
                    em.wait(g_, ("F", 0, "st", 0, NCH - 1))
                    for r_ in range(8):
                        em.inc("s_cast", g_.dma_start(
                            out=xout[r_ * P:(r_ + 1) * P, :],
                            in_=comb_sb[:, r_ * 2048:(r_ + 1) * 2048]), ("tapc", r_), 16)
                return

            if which == "sync":
                sy = eng

                def sy_xl(s):
                    for tc in range(NCH):
                        g = s * NCH + tc
                        if g >= 2:
                            em.wait(sy, L("xfa", (g - 2) // NCH, (g - 2) % NCH))
                            em.wait(sy, L("sq", (g - 2) // NCH, (g - 2) % NCH))
                        elif fp > 0:
                            em.wait(sy, ("A3", "x1free", NTC - 1))
                        em.inc("s_xl", sy.dma_start(
                            out=xch[:, (tc % 2) * D:(tc % 2 + 1) * D],
                            in_=xin[(s * NCH + tc) * P:(s * NCH + tc + 1) * P, :]),
                            L("xl", s, tc), 16)

                def sy_ep(s):
                    for tc in range(NCH):
                        g = s * NCH + tc
                        if g >= 1:
                            em.wait(sy, L("ep_done", (g - 1) // NCH, (g - 1) % NCH))
                        elif fp > 0:
                            em.wait(sy, ("F", 0, "ep_done", NSTR - 1, NCH - 1))
                        em.inc("s_xl", sy.dma_start(
                            out=xch[:, 2 * D:3 * D],
                            in_=xin[(s * NCH + tc) * P:(s * NCH + tc + 1) * P, :]),
                            L("xe", s, tc), 16)
                        em.wait(sy, L("ep_done", s, tc))
                        em.inc("s_st", sy.dma_start(
                            out=xout[(s * NCH + tc) * P:(s * NCH + tc + 1) * P, :],
                            in_=y_sb[:, :]), L("st", s, tc), 16)

                for s in range(nstr):
                    sy_xl(s)
                    if s == 0 and sync_hook is not None:
                        sync_hook()
                    if s >= 1:
                        sy_ep(s - 1)
                sy_ep(nstr - 1)

            elif which == "tensor":
                t = eng
                if fp == 0:
                    em.wait(t, ("prep_done",))
                else:
                    em.wait(t, ("A3", "done_ve"))
                    em.wait(t, ("A2", "exp", NSH - 1, 23))
                    em.wait(t, ("A2", "avdone", NSH - 1, DC - 1))
                    em.wait(t, ("A2", "avtpc", NSH - 1, DC - 1))

                def t_tp(s):
                    for tc in range(NCH):
                        em.wait(t, L("xln", s, tc))
                        for dc in range(DC):
                            k = tc * DC + dc
                            kk = s * NCH * DC + k
                            if kk >= 2:
                                em.wait(t, L("xtc", (kk - 2) // (NCH * DC), (kk - 2) % (NCH * DC)))
                            elif fp > 0:
                                em.wait(t, ("A1", "xtc", NSTR - 1, NCH * DC - 2 + kk))
                            em.inc("s_pe2", t.transpose(
                                pbb(2 + k % 2)[:, 0:P],
                                xln_sb[:, (tc % 2) * D + dc * P:(tc % 2) * D + (dc + 1) * P],
                                ident[:, :]), L("tp", s, k))

                def t_rt(s):
                    if s >= 1:
                        em.wait(t, L("pfree", s - 1, 31))
                    for tc in range(NCH):
                        em.wait(t, L("xtc", s, tc * DC + DC - 1))
                        gg = s * NCH + tc - 1
                        if gg >= 0:
                            em.wait(t, L("rtfree", gg // NCH, gg % NCH))
                        for dc in range(DC):
                            ins = t.matmul(pbk[4][:, 0:8],
                                           xt_sb[:, dc * STRIPE + tc * P:dc * STRIPE + (tc + 1) * P],
                                           rw_sb[:, dc * E:(dc + 1) * E],
                                           start=(dc == 0), stop=(dc == DC - 1))
                        em.inc("s_mm", ins, L("rt_mm", s, tc))
                        em.wait(t, L("w8", s, tc))
                        if gg >= 0:
                            em.wait(t, L("w8free", gg // NCH, gg % NCH))
                        for e in range(E):
                            ins = t.transpose(pbb(5)[0:1, e * P:(e + 1) * P], w8_sb[:, e:e + 1],
                                              ident[:, :])
                        em.inc("s_pe2", ins, L("w8tp", s, tc))
                        em.wait(t, L("wrow", s, tc))
                        for e in range(E):
                            iw = (s * NCH + tc) * E + e
                            if iw >= 2:
                                iwp = iw - 2
                                em.wait(t, L("wbcc", iwp // (NCH * E), (iwp % (NCH * E)) // E,
                                             iwp % E))
                            em.inc("s_pe2", t.matmul(
                                pbk[6 + e % 2][:, 0:P], ones_row[:, :],
                                wrow_sb[0:1, e * P:(e + 1) * P], start=True, stop=True),
                                L("wbc_mm", s, tc, e))

                def t_exp(s):
                    for e in range(E):
                        for fh in range(2):
                            j = e * 2 + fh
                            em.wait(t, L("wl", s, j))
                            for fc in range(NCH):
                                for th in range(2):
                                    i = (j * NCH + fc) * 2 + th
                                    gi = s * 256 + i
                                    bank = gi % 2
                                    if i == 0:
                                        em.wait(t, L("xtc", s, NCH * DC - 1))
                                        em.wait(t, L("wbcc", s, NCH - 1, E - 1))
                                    if gi >= 2:
                                        gp = gi - 2
                                        em.wait(t, L("silu", gp // 256, gp % 256))
                                    for dc in range(DC):
                                        ins = t.matmul(
                                            pbk[bank][:, :],
                                            wslot(j % 2)[:, dc * 1024 + fc * P:dc * 1024 + (fc + 1) * P],
                                            xt_sb[:, dc * STRIPE + th * 512:dc * STRIPE + (th + 1) * 512],
                                            start=(dc == 0), stop=(dc == DC - 1))
                                    em.inc("s_mm", ins, L("exp_mm", s, i))
                                    if fc == NCH - 1 and th == 1:
                                        em.inc("s_wfree", t.nop(), L("wfree", s, j))

                def t_proj(s):
                    em.wait(t, L("wl", s, 16))
                    em.wait(t, L("wl", s, 17))
                    em.wait(t, L("comb", s, 255))
                    for tc in range(NCH):
                        for foq in range(4):
                            jj = tc * 4 + foq
                            gj = s * 32 + jj
                            bank = 4 + jj % 4
                            gp = gj - 4
                            if gp >= 0:
                                em.wait(t, L("pfree", gp // 32, gp % 32))
                            slot = foq // 2
                            for fgc in range(DC):
                                t.matmul(pbk[bank][:, :],
                                         comb_sb[:, fgc * STRIPE + tc * P:fgc * STRIPE + (tc + 1) * P],
                                         wslot(slot)[:, fgc * 1024 + (foq % 2) * 512:
                                                     fgc * 1024 + (foq % 2) * 512 + 512],
                                         start=(fgc == 0), stop=False)
                            ins = t.matmul(pbk[bank][:, :], ones_row[:, :],
                                           pbrow[0:1, foq * 512:(foq + 1) * 512],
                                           start=False, stop=True)
                            em.inc("s_mm", ins, L("proj_mm", s, jj))
                            if jj == 30:
                                em.inc("s_wfree", t.nop(), L("wfree", s, 16))
                            if jj == 31:
                                em.inc("s_wfree", t.nop(), L("wfree", s, 17))

                t_tp(0)
                t_rt(0)
                for s in range(nstr):
                    t_exp(s)
                    if s + 1 < nstr:
                        t_tp(s + 1)
                    t_proj(s)
                    if s + 1 < nstr:
                        t_rt(s + 1)

            elif which == "scalar":
                sc = eng
                if fp > 0:
                    em.wait(sc, ("A2", "atst", NSH - 1, DC - 1))
                    em.wait(sc, ("A2", "avtp", NSH - 1, DC - 1))

                def sc_ln(s):
                    for tc in range(NCH):
                        g = s * NCH + tc
                        pv = tc % 2
                        em.wait(sc, L("xl", s, tc))
                        if g >= 1:
                            em.wait(sc, L("var", (g - 1) // NCH, (g - 1) % NCH))
                        elif fp > 0:
                            em.wait(sc, ("A1", "var", NSTR - 1, NCH - 1))
                        for q4 in range(4):
                            ins = sc.activation(
                                sq_sb[:, :],
                                xch[:, (tc % 2) * D + q4 * 512:(tc % 2) * D + (q4 + 1) * 512],
                                AF.Square, accum_out=red_sb[:, 8 + q4:9 + q4])
                        em.inc("s_act2", ins, L("sq", s, tc))
                        em.wait(sc, L("var", s, tc))
                        em.inc("s_act2", sc.activation(red_sb[:, 2 + pv:3 + pv], red_sb[:, pv:pv + 1],
                                                       AF.Ln, bias=f32(LN_EPS)), L("lnv", s, tc))
                        em.wait(sc, L("lnv", s, tc))
                        em.inc("s_act2", sc.activation(stats[:, SO(s) + 2 * tc + 1:SO(s) + 2 * tc + 2],
                                                       red_sb[:, 2 + pv:3 + pv], AF.Exp, scale=f32(-0.5)),
                               L("istd", s, tc))
                        for dc in range(DC):
                            k = tc * DC + dc
                            em.wait(sc, L("tp", s, k))
                            em.inc("s_act2", sc.activation(
                                xt_sb[:, dc * STRIPE + tc * P:dc * STRIPE + (tc + 1) * P],
                                pbb(2 + k % 2)[:, 0:P], AF.Copy), L("xtc", s, k))

                def sc_es(s):
                    for tc in range(NCH):
                        gg = s * NCH + tc - 1
                        if gg >= 0:
                            em.wait(sc, L("sk", gg // NCH, gg % NCH))
                        em.wait(sc, L("nm1", s, tc))
                        em.inc("s_act2", sc.activation(es_sb[:, :], logit_sb[:, :], AF.Exp,
                                                       bias=nm1_sb[:, :]), L("es", s, tc))

                def sc_silu(s):
                    for e in range(E):
                        for fh in range(2):
                            for fc in range(NCH):
                                for th in range(2):
                                    i = ((e * 2 + fh) * NCH + fc) * 2 + th
                                    gi = s * 256 + i
                                    bank = gi % 2
                                    fgc = fh * NCH + fc
                                    em.wait(sc, L("exp_mm", s, i))
                                    gp = gi - 2
                                    if gp >= 0:
                                        em.wait(sc, L("comb", gp // 256, gp % 256))
                                    em.inc("s_act", sc.activation(
                                        tmp_bf[:, (gi % 2) * 512:(gi % 2 + 1) * 512],
                                        pbk[bank][:, :], AF.Silu,
                                        bias=bias_sb[:, fgc * E + e:fgc * E + e + 1]),
                                        L("silu", s, i))

                def sc_ep(s):
                    for tc in range(NCH):
                        g = s * NCH + tc
                        em.wait(sc, L("ycomp", s, tc))
                        if g >= 1:
                            em.wait(sc, L("yps", (g - 1) // NCH, (g - 1) % NCH))
                        elif fp > 0:
                            em.wait(sc, ("F", 0, "yps", NSTR - 1, NCH - 1))
                        for q4 in range(4):
                            ins = sc.activation(sq_sb[:, :], y_sb[:, q4 * 512:(q4 + 1) * 512],
                                                AF.Square, accum_out=red_sb[:, 12 + q4:13 + q4])
                        em.inc("s_act2", ins, L("sqy", s, tc))
                        em.wait(sc, L("yps", s, tc))
                        em.inc("s_act2", sc.activation(red_sb[:, 5:6], red_sb[:, 4:5], AF.Ln,
                                                       scale=f32(1.0 / D), bias=f32(RMS_EPS_FFN)),
                               L("lny", s, tc))
                        em.wait(sc, L("lny", s, tc))
                        if g >= 1:
                            em.wait(sc, L("contrib1", (g - 1) // NCH, (g - 1) % NCH))
                        elif fp > 0:
                            em.wait(sc, ("F", 0, "contrib1", NSTR - 1, NCH - 1))
                        em.inc("s_act2", sc.activation(red_sb[:, 6:7], red_sb[:, 5:6], AF.Exp,
                                                       scale=f32(-0.5)), L("irms", s, tc))

                sc_ln(0)
                sc_es(0)
                for s in range(nstr):
                    sc_silu(s)
                    if s + 1 < nstr:
                        sc_ln(s + 1)
                    sc_ep(s)
                    if s + 1 < nstr:
                        sc_es(s + 1)

            elif which == "vector":
                v = eng

                def v_proln(s):
                    for tc in range(NCH):
                        slot = (tc % 2) * D
                        g = s * NCH + tc
                        pv = tc % 2
                        if fp > 0 and s == 0 and tc < 2:
                            em.wait(v, ("A1", "tp", NSTR - 1, NCH * DC - 1))
                            if tc == 0:
                                em.wait(v, ("A3", "mm", NTC - 1, 3))
                        em.wait(v, L("xl", s, tc))
                        em.inc("s_ve2", v.reduce_sum(red_sb[:, 7:8], xch[:, slot:slot + D],
                                                     axis=AX, negate=True), L("nsum", s, tc))
                        em.wait(v, L("nsum", s, tc))
                        em.inc("s_ve2", v.tensor_scalar(stats[:, SO(s) + 2 * tc:SO(s) + 2 * tc + 1],
                                                        red_sb[:, 7:8], 1.0 / D, None, ALU.mult),
                               L("nmean", s, tc))
                        em.wait(v, L("sq", s, tc))
                        em.wait(v, L("nmean", s, tc))
                        v.tensor_add(red_sb[:, 8:9], red_sb[:, 8:9], red_sb[:, 9:10])
                        em.inc("s_ve2", v.tensor_add(red_sb[:, 10:11], red_sb[:, 10:11],
                                                     red_sb[:, 11:12]), L("p2", s, tc))
                        em.wait(v, L("p2", s, tc))
                        em.inc("s_ve2", v.tensor_add(red_sb[:, 8:9], red_sb[:, 8:9],
                                                     red_sb[:, 10:11]), L("p4", s, tc))
                        if g >= 2:
                            em.wait(v, L("lnv", (g - 2) // NCH, (g - 2) % NCH))
                        elif fp > 0:
                            em.wait(v, ("A1", "lnv", NSTR - 1, NCH - 2 + g))
                        em.inc("s_ve2", v.tensor_tensor(red_sb[:, pv:pv + 1],
                                                        stats[:, SO(s) + 2 * tc:SO(s) + 2 * tc + 1],
                                                        stats[:, SO(s) + 2 * tc:SO(s) + 2 * tc + 1],
                                                        ALU.mult),
                               L("m2done", s, tc))
                        em.wait(v, L("p4", s, tc))
                        em.wait(v, L("m2done", s, tc))
                        em.inc("s_ve2", v.scalar_tensor_tensor(
                            red_sb[:, pv:pv + 1], red_sb[:, 8:9], 1.0 / D,
                            red_sb[:, pv:pv + 1], ALU.mult, ALU.subtract), L("var", s, tc))
                        em.wait(v, L("istd", s, tc))
                        if g >= 2:
                            em.wait(v, L("tp", (g - 2) // NCH, ((g - 2) % NCH) * DC + DC - 1))
                        elif fp > 0:
                            em.wait(v, ("A1", "tp", NSTR - 1, (NCH - 2 + g) * DC + DC - 1))
                        em.inc("s_ve2", v.tensor_scalar(
                            xln_sb[:, slot:slot + D], xch[:, slot:slot + D],
                            stats[:, SO(s) + 2 * tc:SO(s) + 2 * tc + 1],
                            stats[:, SO(s) + 2 * tc + 1:SO(s) + 2 * tc + 2],
                            ALU.add, ALU.mult), L("xln", s, tc))
                        em.inc("s_xfa", v.engine_nop(), L("xfa", s, tc))

                def v_prort(s):
                    for tc in range(NCH):
                        em.wait(v, L("rt_mm", s, tc))
                        gg = s * NCH + tc - 1
                        if gg >= 0:
                            em.wait(v, L("es", gg // NCH, gg % NCH))
                        elif fp > 0:
                            em.wait(v, ("F", 0, "es", NSTR - 1, NCH - 1))
                        em.inc("s_ve", v.tensor_add(logit_sb[:, :], pbk[4][:, 0:8], rb_bc[:, :]),
                               L("rtfree", s, tc))
                        em.wait(v, L("rtfree", s, tc))
                        em.inc("s_ve", v.reduce_max(m1_sb[:, :], logit_sb[:, :], axis=AX),
                               L("m1", s, tc))
                        em.wait(v, L("m1", s, tc))
                        em.inc("s_ve", v.tensor_scalar_mul(nm1_sb[:, :], m1_sb[:, :], -1.0),
                               L("nm1", s, tc))
                        em.inc("s_ve", v.tensor_scalar(l2_sb[:, :], logit_sb[:, :], m1_sb[:, :],
                                                       None, ALU.is_ge), L("lmask", s, tc))
                        em.wait(v, L("lmask", s, tc))
                        em.inc("s_ve", v.scalar_tensor_tensor(l2_sb[:, :], l2_sb[:, :], NEG,
                                                              logit_sb[:, :], ALU.mult, ALU.add),
                               L("l2", s, tc))
                        em.wait(v, L("l2", s, tc))
                        em.inc("s_ve", v.reduce_max(m2_sb[:, :], l2_sb[:, :], axis=AX),
                               L("m2", s, tc))
                        em.wait(v, L("m2", s, tc))
                        em.inc("s_ve", v.tensor_scalar(keep_sb[:, :], logit_sb[:, :], m2_sb[:, :],
                                                       None, ALU.is_ge), L("keep", s, tc))
                        em.wait(v, L("es", s, tc))
                        em.wait(v, L("keep", s, tc))
                        em.inc("s_ve", v.tensor_mul(sk_sb[:, :], es_sb[:, :], keep_sb[:, :]),
                               L("sk", s, tc))
                        em.wait(v, L("sk", s, tc))
                        em.inc("s_ve", v.reduce_sum(red_sb[:, 7:8], sk_sb[:, :], axis=AX),
                               L("den", s, tc))
                        em.wait(v, L("den", s, tc))
                        em.inc("s_ve", v.reciprocal(red_sb[:, 7:8], red_sb[:, 7:8]),
                               L("rden", s, tc))
                        em.wait(v, L("rden", s, tc))
                        em.inc("s_ve", v.tensor_scalar(w8_sb[:, :], sk_sb[:, :], red_sb[:, 7:8],
                                                       None, ALU.mult), L("w8", s, tc))
                        em.wait(v, L("w8tp", s, tc))
                        em.inc("s_ve", v.tensor_copy(wrow_sb[0:1, :], pbb(5)[0:1, 0:E * P]),
                               L("wrow", s, tc), k=1)
                        em.inc("s_ve", v.engine_nop(), L("w8free", s, tc))
                        for e in range(E):
                            em.wait(v, L("wbc_mm", s, tc, e))
                            em.inc("s_ve", v.tensor_copy(
                                wbc_sb[:, e * STRIPE + tc * P:e * STRIPE + (tc + 1) * P],
                                pbk[6 + e % 2][:, 0:P]), L("wbcc", s, tc, e))

                def v_comb(s):
                    if s >= 1:
                        em.wait(v, L("proj_mm", s - 1, 31))
                    elif fp > 0:
                        em.wait(v, ("A2", "av_mm", NSH - 1, DC - 1))
                    for e in range(E):
                        for fh in range(2):
                            for fc in range(NCH):
                                for th in range(2):
                                    i = ((e * 2 + fh) * NCH + fc) * 2 + th
                                    gi = s * 256 + i
                                    fgc = fh * NCH + fc
                                    ts = (gi % 2) * 512
                                    em.wait(v, L("silu", s, i))
                                    if e == 0:
                                        em.inc("s_ve", v.tensor_tensor(
                                            comb_sb[:, fgc * STRIPE + th * 512:
                                                    fgc * STRIPE + (th + 1) * 512],
                                            tmp_bf[:, ts:ts + 512],
                                            wbc_sb[:, th * 512:(th + 1) * 512],
                                            ALU.mult), L("comb", s, i))
                                    else:
                                        em.inc("s_ve", v.tensor_mul(
                                            tmp_bf[:, ts:ts + 512], tmp_bf[:, ts:ts + 512],
                                            wbc_sb[:, e * STRIPE + th * 512:
                                                   e * STRIPE + (th + 1) * 512]),
                                            L("wmul", s, i))
                                        em.wait(v, L("wmul", s, i))
                                        em.inc("s_ve", v.tensor_add(
                                            comb_sb[:, fgc * STRIPE + th * 512:
                                                    fgc * STRIPE + (th + 1) * 512],
                                            comb_sb[:, fgc * STRIPE + th * 512:
                                                    fgc * STRIPE + (th + 1) * 512],
                                            tmp_bf[:, ts:ts + 512]), L("comb", s, i))

                def v_ep(s):
                    for tc in range(NCH):
                        em.wait(v, L("xe", s, tc))
                        g = s * NCH + tc
                        if fp == 0 and g == 0:
                            em.wait(v, ("prep_done",))
                        if g >= 1:
                            em.wait(v, L("st", (g - 1) // NCH, (g - 1) % NCH))
                        elif fp > 0:
                            em.wait(v, ("A3", "st", NTC - 1))
                        em.inc("s_ve2", v.tensor_scalar(
                            y_sb[:, :], xch[:, 2 * D:3 * D],
                            stats[:, SO(s) + 2 * tc:SO(s) + 2 * tc + 1],
                            stats[:, SO(s) + 2 * tc + 1:SO(s) + 2 * tc + 2],
                            ALU.add, ALU.mult),
                            L("yln", s, tc))
                        for foq in range(4):
                            jj = tc * 4 + foq
                            bank = 4 + jj % 4
                            em.wait(v, L("proj_mm", s, jj))
                            if foq == 0:
                                em.wait(v, L("yln", s, tc))
                            em.inc("s_pfree", v.tensor_add(y_sb[:, foq * 512:(foq + 1) * 512],
                                                           y_sb[:, foq * 512:(foq + 1) * 512],
                                                           pbk[bank][:, :]), L("pfree", s, jj))
                        em.inc("s_ve2", v.engine_nop(), L("ycomp", s, tc))
                        em.wait(v, L("sqy", s, tc))
                        if g >= 1:
                            em.wait(v, L("lny", (g - 1) // NCH, (g - 1) % NCH))
                        elif fp > 0:
                            em.wait(v, ("F", 0, "lny", NSTR - 1, NCH - 1))
                        v.tensor_add(red_sb[:, 12:13], red_sb[:, 12:13], red_sb[:, 13:14])
                        em.inc("s_ve2", v.tensor_add(red_sb[:, 14:15], red_sb[:, 14:15],
                                                     red_sb[:, 15:16]), L("yp2", s, tc))
                        em.wait(v, L("yp2", s, tc))
                        em.inc("s_ve2", v.tensor_add(red_sb[:, 4:5], red_sb[:, 12:13],
                                                     red_sb[:, 14:15]), L("yps", s, tc))
                        em.wait(v, L("irms", s, tc))
                        em.inc("s_ve2", v.tensor_scalar(y_sb[:, :], y_sb[:, :], red_sb[:, 6:7],
                                                        0.5, ALU.mult, ALU.mult),
                               L("contrib1", s, tc))
                        em.wait(v, L("contrib1", s, tc))
                        em.inc("s_ve2", v.tensor_tensor(y_sb[:, :], y_sb[:, :], nwb_sb[:, :],
                                                        ALU.mult), L("contrib2", s, tc))
                        em.wait(v, L("contrib2", s, tc))
                        em.inc("s_ve2", v.tensor_add(y_sb[:, :], y_sb[:, :], xch[:, 2 * D:3 * D]),
                               L("ep_done", s, tc))
                        em.inc("s_xfb", v.engine_nop(), L("xfb", s, tc))

                v_proln(0)
                v_prort(0)
                for s in range(nstr):
                    if tap == "comb" and fp == 0 and s == 1:
                        em.wait(v, ("tapc", 7))
                    v_comb(s)
                    if s + 1 < nstr:
                        v_proln(s + 1)
                    v_ep(s)
                    if s + 1 < nstr:
                        v_prort(s + 1)

                if tap == "logit" and fp == 0:
                    pass

        def phase_qkvT(which, eng):
            if which != "sync":
                return
            sy = eng
            em.wait(sy, ("cast_qkv",))
            slots = [qo2_sb[:, :], qp2_sb[:, :]]
            nidx = 0
            for orange in range(12):
                for cc in range(DC):
                    sl = slots[nidx % 2]
                    if nidx >= 2:
                        em.wait(sy, ("qkvT_st", nidx - 2))
                    em.inc("s_ld", sy.dma_start_transpose(
                        sl, d_qkvb[orange * 512:(orange + 1) * 512, cc * P:(cc + 1) * P]),
                        ("qkvT_ld", nidx), 16)
                    em.wait(sy, ("qkvT_ld", nidx))
                    em.inc("s_st", sy.dma_start(
                        out=d_qkvT[cc * P:(cc + 1) * P, orange * 512:(orange + 1) * 512],
                        in_=sl), ("qkvT_st", nidx), 16)
                    nidx += 1

        def phase_A1(which, eng):
            L = lambda *a: ("A1",) + a

            if which == "sync":
                sy = eng
                em.wait(sy, ("F", 0, "proj_mm", NSTR - 1, 31))
                em.wait(sy, ("F", 0, "comb", NSTR - 1, 255))
                em.inc("s_ld", sy.dma_start(out=wbc_sb[:, COS_OFF:COS_OFF + TOK], in_=d_cos),
                       None, 16)
                em.inc("s_ld", sy.dma_start(out=wbc_sb[:, SIN_OFF:SIN_OFF + TOK], in_=d_sin),
                       L("cs"), 16)
                for s in range(NSTR):
                    for tc in range(NCH):
                        g = s * NCH + tc
                        if g >= 2:
                            em.wait(sy, L("xfa", (g - 2) // NCH, (g - 2) % NCH))
                            em.wait(sy, L("sq", (g - 2) // NCH, (g - 2) % NCH))
                        else:
                            em.wait(sy, ("F", 0, "xfa", NSTR - 1, NCH - 2 + g))
                            em.wait(sy, ("F", 0, "sq", NSTR - 1, NCH - 2 + g))
                        em.wait(sy, ("F", 0, "st", s, tc))
                        em.inc("s_xl", sy.dma_start(
                            out=xch[:, (tc % 2) * D:(tc % 2 + 1) * D],
                            in_=d_x1[(s * NCH + tc) * P:(s * NCH + tc + 1) * P, :]),
                            L("xl", s, tc), 16)
                    def a1_store(orange):
                        if orange < 8:
                            for oc in range(4):
                                for th in range(2):
                                    o_abs = orange * 512 + oc * P
                                    is_q = o_abs < D
                                    hh = (o_abs % D) // P
                                    dst = d_qt if is_q else d_kt
                                    em.wait(sy, L("qkout", s, orange, oc, th))
                                    i = QK_IDX[(s, orange, oc, th)]
                                    srcb = tmp_bf[:, 1536:2048] if i % 2 == 0 else qo2_sb[:, :]
                                    em.inc("s_st", sy.dma_start(
                                        out=dst[hh * P:(hh + 1) * P,
                                                s * STRIPE + th * 512:s * STRIPE + (th + 1) * 512],
                                        in_=srcb), L("qkst", s, orange, oc, th), 16)
                        else:
                            for tc in range(NCH):
                                em.wait(sy, L("vout", s, orange, tc))
                                em.inc("s_st", sy.dma_start(
                                    out=d_v[(s * NCH + tc) * P:(s * NCH + tc + 1) * P,
                                            (orange - 8) * 512:(orange - 7) * 512],
                                    in_=tmp_bf[:, (tc % 2) * 512:(tc % 2 + 1) * 512]),
                                    L("vst", s, orange, tc), 16)

                    for orange in range(12):
                        gs = s * 12 + orange
                        if gs == 0:
                            em.wait(sy, ("qkvT_st", 191))
                        if gs >= 4:
                            em.wait(sy, L("wfree", (gs - 4) // 12, (gs - 4) % 12))
                        elif gs in (0, 1):
                            em.wait(sy, ("F", 0, "wfree", NSTR - 1, 16))
                        else:
                            em.wait(sy, ("F", 0, "wfree", NSTR - 1, 17))
                        em.inc("s_ld", sy.dma_start(
                            out=a1slot(orange % 4).rearrange("p (c o) -> p c o", o=512),
                            in_=d_qkvT[:, orange * 512:(orange + 1) * 512]
                            .rearrange("(c p) o -> p c o", p=P)), L("wl", s, orange), 16)
                        if orange >= 1:
                            a1_store(orange - 1)
                    a1_store(11)

            elif which == "tensor":
                t = eng
                em.wait(t, ("F", 0, "pfree", NSTR - 1, 31))
                em.wait(t, ("F", 0, "silu", NSTR - 1, 255))
                em.wait(t, ("F", 0, "xtc", NSTR - 1, NCH * DC - 1))
                em.wait(t, ("F", 0, "rtfree", NSTR - 1, NCH - 1))
                em.wait(t, ("F", 0, "w8free", NSTR - 1, NCH - 1))
                em.wait(t, ("F", 0, "wbcc", NSTR - 1, NCH - 1, E - 1))
                for s in range(NSTR):
                    for tc in range(NCH):
                        em.wait(t, L("xln", s, tc))
                        if tc == 0 and s >= 1:
                            em.wait(t, L("qkout", *QK_TILES[s * 64 - 1]))
                        for dc in range(DC):
                            k = tc * DC + dc
                            kk = s * NCH * DC + k
                            if kk >= 2:
                                em.wait(t, L("xtc", (kk - 2) // (NCH * DC), (kk - 2) % (NCH * DC)))
                            else:
                                em.wait(t, ("F", 0, "xtc", NSTR - 1, NCH * DC - 2 + kk))
                            em.inc("s_pe2", t.transpose(
                                pbb(2 + k % 2)[:, 0:P],
                                xln_sb[:, (tc % 2) * D + dc * P:(tc % 2) * D + (dc + 1) * P],
                                ident[:, :]), L("tp", s, k))
                    qk_local = [(o_, oc_, th_) for o_ in range(8)
                                for oc_ in range(4) for th_ in range(2)]

                    def a1_stage1(tile):
                        orange, oc, th = tile
                        i = QK_IDX[(s, orange, oc, th)]
                        bank = i % 3
                        if oc == 0 and th == 0:
                            em.wait(t, L("wl", s, orange))
                        if i - s * 64 < 3 and bank == 2:
                            em.wait(t, L("xtc", s, NCH * DC - 2))
                        if i - s * 64 < 1:
                            em.wait(t, L("xtc", s, NCH * DC - 1))
                        if i >= 3:
                            em.wait(t, L("qb", *QK_TILES[i - 3]))
                        for dc in range(DC):
                            ins = t.matmul(
                                pbk[bank][:, :],
                                a1slot(orange % 4)[:, dc * 512 + oc * P:dc * 512 + (oc + 1) * P],
                                xt_sb[:, dc * STRIPE + th * 512:dc * STRIPE + (th + 1) * 512],
                                start=(dc == 0), stop=(dc == DC - 1))
                        em.inc("s_mm", ins, L("qk_mm", s, orange, oc, th))
                        if oc == 3 and th == 1:
                            em.inc("s_wfree", t.nop(), L("wfree", s, orange))

                    def a1_stage2(tile):
                        orange, oc, th = tile
                        i = QK_IDX[(s, orange, oc, th)]
                        em.wait(t, L("qbc", s, orange, oc, th))
                        if i - s * 64 < 1:
                            em.wait(t, L("xtc", s, NCH * DC - 1))
                        if i >= 1:
                            em.wait(t, L("qsin", *QK_TILES[i - 1]))
                        em.inc("s_pe2", t.matmul(pbk[3][:, :], pm_sb[:, :],
                                                 tmp_bf[:, 0:512], start=True, stop=True),
                               L("rot_mm", s, orange, oc, th))

                    def a1_stage3(tile):
                        orange, oc, th = tile
                        i = QK_IDX[(s, orange, oc, th)]
                        em.wait(t, L("qsq", s, orange, oc, th))
                        if i >= 1:
                            em.wait(t, L("rowb", *QK_TILES[i - 1]))
                        em.inc("s_pe2", t.matmul(pbk[4][0:1, :], ones_col[:, :],
                                                 sq_sb[:, 0:512], start=True, stop=True),
                               L("rms_mm", s, orange, oc, th))

                    def a1_stage4(tile):
                        orange, oc, th = tile
                        i = QK_IDX[(s, orange, oc, th)]
                        em.wait(t, L("rowb", s, orange, oc, th))
                        if i >= 1:
                            em.wait(t, L("qkout", *QK_TILES[i - 1]))
                        em.inc("s_pe2", t.matmul(pbk[5][:, :], ones_row[:, :],
                                                 rms_rowb[0:1, :], start=True, stop=True),
                               L("bc_mm", s, orange, oc, th))

                    for step in range(len(qk_local) + 3):
                        if 3 <= step and step - 3 < len(qk_local):
                            a1_stage4(qk_local[step - 3])
                        if 2 <= step and step - 2 < len(qk_local):
                            a1_stage3(qk_local[step - 2])
                        if 1 <= step and step - 1 < len(qk_local):
                            a1_stage2(qk_local[step - 1])
                        if step < len(qk_local):
                            a1_stage1(qk_local[step])
                    for orange in range(8, 12):
                        em.wait(t, L("wl", s, orange))
                        if True:
                            for tc in range(NCH):
                                iv = V_IDX[(s, orange, tc)]
                                bank = iv % 2 + 6
                                if iv >= 2:
                                    em.wait(t, L("vc", *V_TILES[iv - 2]))
                                elif fpv0 := (iv < 2):
                                    em.wait(t, ("F", 0, "wbcc", NSTR - 1, NCH - 1, E - 2 + iv))
                                for dc in range(DC):
                                    ins = t.matmul(
                                        pbk[bank][:, :],
                                        xt_sb[:, dc * STRIPE + tc * P:dc * STRIPE + (tc + 1) * P],
                                        a1slot(orange % 4)[:, dc * 512:(dc + 1) * 512],
                                        start=(dc == 0), stop=(dc == DC - 1))
                                em.inc("s_mm", ins, L("v_mm", s, orange, tc))
                                if tc == NCH - 1:
                                    em.inc("s_wfree", t.nop(), L("wfree", s, orange))

            elif which == "scalar":
                sc = eng
                for s in range(NSTR):
                    for tc in range(NCH):
                        g = s * NCH + tc
                        pv = tc % 2
                        em.wait(sc, L("xl", s, tc))
                        if s >= 1 and tc == 0:
                            em.wait(sc, L("rms_mm", s - 1, 7, 3, 1))
                        if g >= 1:
                            em.wait(sc, L("var", (g - 1) // NCH, (g - 1) % NCH))
                        else:
                            em.wait(sc, ("F", 0, "var", NSTR - 1, NCH - 1))
                        for q4 in range(4):
                            ins = sc.activation(
                                sq_sb[:, :],
                                xch[:, (tc % 2) * D + q4 * 512:(tc % 2) * D + (q4 + 1) * 512],
                                AF.Square, accum_out=red_sb[:, 8 + q4:9 + q4])
                        em.inc("s_act2", ins, L("sq", s, tc))
                        em.wait(sc, L("var", s, tc))
                        em.inc("s_act2", sc.activation(red_sb[:, 2 + pv:3 + pv], red_sb[:, pv:pv + 1],
                                                       AF.Ln, bias=f32(LN_EPS)), L("lnv", s, tc))
                        em.wait(sc, L("lnv", s, tc))
                        em.inc("s_act2", sc.activation(stats[:, 2 * tc + 1:2 * tc + 2],
                                                       red_sb[:, 2 + pv:3 + pv], AF.Exp, scale=f32(-0.5)),
                               L("istd", s, tc))
                    qk_local = [(o_, oc_, th_) for o_ in range(8)
                                for oc_ in range(4) for th_ in range(2)]

                    def a1s_front(tile):
                        orange, oc, th = tile
                        i = QK_IDX[(s, orange, oc, th)]
                        em.wait(sc, L("qrope", s, orange, oc, th))
                        if i >= 1:
                            em.wait(sc, L("rms_mm", *QK_TILES[i - 1]))
                        qsrc = tmp_bf[:, 512:1024] if i % 2 == 0 else qp2_sb[:, :]
                        em.inc("s_act2", sc.activation(sq_sb[:, 0:512], qsrc,
                                                       AF.Square), L("qsq", s, orange, oc, th))

                    def a1s_back(tile):
                        orange, oc, th = tile
                        o_abs = orange * 512 + oc * P
                        is_q = o_abs < D
                        i = QK_IDX[(s, orange, oc, th)]
                        em.wait(sc, L("rms_mm", s, orange, oc, th))
                        if i >= 1:
                            em.wait(sc, L("rowb", *QK_TILES[i - 1]))
                        em.inc("s_act2", sc.activation(
                            pbk[4][0:1, :], pbk[4][0:1, :], AF.Ln,
                            scale=f32(1.0 / P), bias=f32(RMS_EPS_QK)),
                            L("lnr", s, orange, oc, th))
                        em.wait(sc, L("lnr", s, orange, oc, th))
                        em.inc("s_act2", sc.activation(
                            pbk[4][0:1, :], pbk[4][0:1, :], AF.Exp, scale=f32(-0.5),
                            bias=f32(-HALF_LN128) if is_q else f32(0.0)),
                            L("expr", s, orange, oc, th))

                    for step in range(len(qk_local) + 1):
                        if step >= 1:
                            a1s_back(qk_local[step - 1])
                        if step < len(qk_local):
                            a1s_front(qk_local[step])

            elif which == "vector":
                v = eng
                em.wait(v, L("cs"))
                for s in range(NSTR):
                    for tc in range(NCH):
                        slot = (tc % 2) * D
                        g = s * NCH + tc
                        pv = tc % 2
                        em.wait(v, L("xl", s, tc))
                        em.inc("s_ve2", v.reduce_sum(red_sb[:, 7:8], xch[:, slot:slot + D],
                                                     axis=AX, negate=True), L("nsum", s, tc))
                        em.wait(v, L("nsum", s, tc))
                        em.inc("s_ve2", v.tensor_scalar(stats[:, 2 * tc:2 * tc + 1],
                                                        red_sb[:, 7:8], 1.0 / D, None, ALU.mult),
                               L("nmean", s, tc))
                        em.wait(v, L("sq", s, tc))
                        em.wait(v, L("nmean", s, tc))
                        v.tensor_add(red_sb[:, 8:9], red_sb[:, 8:9], red_sb[:, 9:10])
                        em.inc("s_ve2", v.tensor_add(red_sb[:, 10:11], red_sb[:, 10:11],
                                                     red_sb[:, 11:12]), L("p2", s, tc))
                        em.wait(v, L("p2", s, tc))
                        em.inc("s_ve2", v.tensor_add(red_sb[:, 8:9], red_sb[:, 8:9],
                                                     red_sb[:, 10:11]), L("p4", s, tc))
                        if g >= 2:
                            em.wait(v, L("lnv", (g - 2) // NCH, (g - 2) % NCH))
                        else:
                            em.wait(v, ("F", 0, "lnv", NSTR - 1, NCH - 2 + g))
                        em.inc("s_ve2", v.tensor_tensor(red_sb[:, pv:pv + 1], stats[:, 2 * tc:2 * tc + 1],
                                                        stats[:, 2 * tc:2 * tc + 1], ALU.mult),
                               L("m2done", s, tc))
                        em.wait(v, L("p4", s, tc))
                        em.wait(v, L("m2done", s, tc))
                        em.inc("s_ve2", v.scalar_tensor_tensor(
                            red_sb[:, pv:pv + 1], red_sb[:, 8:9], 1.0 / D,
                            red_sb[:, pv:pv + 1], ALU.mult, ALU.subtract), L("var", s, tc))
                        em.wait(v, L("istd", s, tc))
                        if g >= 2:
                            em.wait(v, L("tp", (g - 2) // NCH, ((g - 2) % NCH) * DC + DC - 1))
                        else:
                            em.wait(v, ("F", 0, "tp", NSTR - 1, (NCH - 2 + g) * DC + DC - 1))
                        em.inc("s_ve2", v.tensor_scalar(
                            xln_sb[:, slot:slot + D], xch[:, slot:slot + D],
                            stats[:, 2 * tc:2 * tc + 1], stats[:, 2 * tc + 1:2 * tc + 2],
                            ALU.add, ALU.mult), L("xln", s, tc))
                        em.inc("s_xfa", v.engine_nop(), L("xfa", s, tc))
                        for dc in range(DC):
                            k = tc * DC + dc
                            em.wait(v, L("tp", s, k))
                            em.inc("s_ve", v.tensor_copy(
                                xt_sb[:, dc * STRIPE + tc * P:dc * STRIPE + (tc + 1) * P],
                                pbb(2 + k % 2)[:, 0:P]), L("xtc", s, k))
                    qk_local = [(o_, oc_, th_) for o_ in range(8)
                                for oc_ in range(4) for th_ in range(2)]

                    def qp(i):
                        return tmp_bf[:, 512:1024] if i % 2 == 0 else qp2_sb[:, :]

                    def a1v_front(tile):
                        orange, oc, th = tile
                        tok0 = s * STRIPE + th * 512
                        i = QK_IDX[(s, orange, oc, th)]
                        bank = i % 3
                        em.wait(v, L("qk_mm", s, orange, oc, th))
                        if i >= 1:
                            em.wait(v, L("rot_mm", *QK_TILES[i - 1]))
                        em.inc("s_ve", v.tensor_copy(tmp_bf[:, 0:512], pbk[bank][:, :]),
                               L("qb", s, orange, oc, th))
                        em.inc("s_ve", v.engine_nop(), L("qbc", s, orange, oc, th))
                        em.wait(v, L("qbc", s, orange, oc, th))
                        if i >= 2:
                            em.wait(v, L("qsq", *QK_TILES[i - 2]))
                        em.inc("s_ve", v.tensor_tensor(
                            qp(i), tmp_bf[:, 0:512],
                            wbc_sb[:, COS_OFF + tok0:COS_OFF + tok0 + 512], ALU.mult),
                            L("qcos", s, orange, oc, th))
                        em.wait(v, L("rot_mm", s, orange, oc, th))
                        em.inc("s_ve", v.tensor_tensor(
                            tmp_bf[:, 1024:1536], pbk[3][:, :],
                            wbc_sb[:, SIN_OFF + tok0:SIN_OFF + tok0 + 512], ALU.mult),
                            L("qsin", s, orange, oc, th))
                        em.wait(v, L("qcos", s, orange, oc, th))
                        em.wait(v, L("qsin", s, orange, oc, th))
                        em.inc("s_ve", v.tensor_add(
                            qp(i), qp(i),
                            tmp_bf[:, 1024:1536]), L("qrope", s, orange, oc, th))

                    def a1v_back(tile):
                        orange, oc, th = tile
                        i = QK_IDX[(s, orange, oc, th)]
                        em.wait(v, L("expr", s, orange, oc, th))
                        if i >= 1:
                            em.wait(v, L("bc_mm", *QK_TILES[i - 1]))
                        em.inc("s_ve", v.tensor_copy(rms_rowb[0:1, :], pbk[4][0:1, :]),
                               L("rowb", s, orange, oc, th))
                        em.wait(v, L("bc_mm", s, orange, oc, th))
                        if i == 1:
                            em.wait(v, ("qkvT_st", 191))
                        if i >= 2:
                            em.wait(v, L("qkst", *QK_TILES[i - 2]))
                        dstq = tmp_bf[:, 1536:2048] if i % 2 == 0 else qo2_sb[:, :]
                        em.inc("s_ve", v.tensor_tensor(
                            dstq, qp(i), pbk[5][:, :], ALU.mult),
                            L("qkout", s, orange, oc, th))

                    for step in range(len(qk_local) + 2):
                        if step >= 2:
                            a1v_back(qk_local[step - 2])
                        if step < len(qk_local):
                            a1v_front(qk_local[step])
                    for orange in range(8, 12):
                        if True:
                            for tc in range(NCH):
                                iv = V_IDX[(s, orange, tc)]
                                bank = iv % 2 + 6
                                em.wait(v, L("v_mm", s, orange, tc))
                                if iv >= 2:
                                    em.wait(v, L("vst", *V_TILES[iv - 2]))
                                em.inc("s_ve", v.tensor_copy(
                                    tmp_bf[:, (tc % 2) * 512:(tc % 2 + 1) * 512], pbk[bank][:, :]),
                                    L("vout", s, orange, tc))
                                em.inc("s_ve", v.engine_nop(), L("vc", s, orange, tc))

        def phase_A2(which, eng):
            L = lambda *a: ("A2",) + a

            ST = []
            SIDX0 = {}
            for kcb in range(DC):
                nq = T - kcb * P
                SIDX0[kcb] = len(ST)
                for t2 in range((nq + 1023) // 1024):
                    ST.append((kcb, t2, min(1024, nq - t2 * 1024)))
            NST = len(ST)

            def qtbuf(b):
                return xt_sb[:, b * 2048:(b + 1) * 2048]

            def ktbuf(b):
                return xt_sb[:, 4096 + b * 2048:4096 + (b + 1) * 2048]

            def vbuf(b):
                return comb_sb[:, b * 2112:(b + 1) * 2112]

            def stbuf(sidx):
                return pb2[0] if sidx % 2 == 0 else pb2[3]

            if which == "sync":
                sy = eng
                for sh in range(NSH):
                    s, hh = sh // H, sh % H
                    b = sh % 2
                    if sh >= 2:
                        em.wait(sy, L("shdone", sh - 2))
                    else:
                        em.wait(sy, L("vones", b))
                        em.wait(sy, ("A1", "v_mm", NSTR - 1, 11, NCH - 1))
                        em.wait(sy, ("A1", "qkst", 2 * s + 1, 7, 3, 1))
                        em.wait(sy, ("A1", "vst", 2 * s + 1, 11, NCH - 1))
                    em.inc("s_xl", sy.dma_start(
                        out=qtbuf(b), in_=d_qt[hh * P:(hh + 1) * P, s * T:(s + 1) * T]), None, 16)
                    em.inc("s_xl", sy.dma_start(
                        out=ktbuf(b), in_=d_kt[hh * P:(hh + 1) * P, s * T:(s + 1) * T]), None, 16)
                    em.inc("s_xl", sy.dma_start(
                        out=vbuf(b).rearrange("p (kc c) -> p kc c", c=132)[:, :, 0:P],
                        in_=d_v[s * T:(s + 1) * T, hh * P:(hh + 1) * P]
                        .rearrange("(kc p) c -> p kc c", p=P)), L("qkv", sh), 16)

            elif which == "tensor":
                t = eng
                em.wait(t, ("A1", "qb", NSTR - 1, 7, 3, 1))
                em.wait(t, ("A1", "qsin", NSTR - 1, 7, 3, 1))
                em.wait(t, ("A1", "rowb", NSTR - 1, 7, 3, 1))
                em.wait(t, ("A1", "qkout", NSTR - 1, 7, 3, 1))
                em.wait(t, ("A1", "vc", NSTR - 1, 11, NCH - 1))
                for sh in range(NSH):
                    b = sh % 2
                    em.wait(t, L("qkv", sh))
                    for step in range(DC + 2):
                        if step < DC:
                            kcb = step
                            q0 = kcb * P
                            for t2 in range((T - q0 + 1023) // 1024):
                                sidx = SIDX0[kcb] + t2
                                w2 = ST[sidx][2]
                                gsid = sh * NST + sidx
                                if gsid >= 2:
                                    gp = gsid - 2
                                    em.wait(t, L("exp", gp // NST, gp % NST))
                                nsub = (w2 + 511) // 512
                                for sub in range(nsub):
                                    ws = min(512, w2 - sub * 512)
                                    ins = t.matmul(
                                        stbuf(sidx)[:, sub * 512:sub * 512 + ws],
                                        ktbuf(b)[:, kcb * P:(kcb + 1) * P],
                                        qtbuf(b)[:, q0 + t2 * 1024 + sub * 512:
                                                 q0 + t2 * 1024 + sub * 512 + ws],
                                        start=True, stop=True)
                                em.inc("s_mm", ins, L("sc_mm", sh, sidx))
                        if step >= 1 and step - 1 < DC:
                            j = step - 1
                            avoff = (j % 2) * 512
                            em.wait(t, L("exp", sh, SIDX0[j]))
                            gav = sh * DC + j
                            if gav >= 2:
                                gp = gav - 2
                                em.wait(t, L("avdone", gp // DC, gp % DC))
                            for kcb2 in range(j + 1):
                                ins = t.matmul(
                                    pb2[1][:, avoff:avoff + 129],
                                    wbuf[:, kcb2 * T + j * P:kcb2 * T + (j + 1) * P],
                                    vbuf(b)[:, kcb2 * 132:kcb2 * 132 + 129],
                                    start=(kcb2 == 0), stop=(kcb2 == j))
                            em.inc("s_mm", ins, L("av_mm", sh, j))
                        if step >= 2 and step - 2 < DC:
                            j2 = step - 2
                            em.wait(t, L("avsc", sh, j2))
                            gav = sh * DC + j2
                            if gav >= 2:
                                gp = gav - 2
                                em.wait(t, L("avst", gp // DC, gp % DC))
                            em.inc("s_pe2", t.transpose(
                                pb2[2].bitcast(BF16)[:, (j2 % 2) * 1024:(j2 % 2) * 1024 + P],
                                qo2_sb[:, (j2 % 2) * P:(j2 % 2 + 1) * P],
                                ident[:, :]), L("avtp", sh, j2))

            elif which == "scalar":
                sc = eng
                em.wait(sc, ("A1", "qkst", NSTR - 1, 7, 3, 1))
                def at_store(sh_, j2_):
                    s2, hh2 = sh_ // H, sh_ % H
                    em.wait(sc, L("avst", sh_, j2_))
                    em.inc("s_ast", sc.dma_start(
                        out=d_at[hh2 * P:(hh2 + 1) * P,
                                 s2 * T + j2_ * P:s2 * T + (j2_ + 1) * P],
                        in_=tmp_bf[:, j2_ * P:(j2_ + 1) * P]),
                        L("atst", sh_, j2_), 16)

                for sh in range(NSH):
                    for sidx in range(NST):
                        kcb, t2, w2 = ST[sidx]
                        q0 = kcb * P
                        if sh >= 1 and 4 <= sidx < 4 + DC:
                            at_store(sh - 1, sidx - 4)
                        em.wait(sc, L("sc_mm", sh, sidx))
                        if t2 == 0:
                            em.wait(sc, L("mask", sh, kcb))
                        if sh >= 1:
                            j_last = (kcb * P + t2 * 1024 + w2 - 1) // P
                            em.wait(sc, L("avdone", sh - 1, min(j_last, DC - 1)))
                        em.inc("s_act", sc.activation(
                            wbuf[:, kcb * T + q0 + t2 * 1024:kcb * T + q0 + t2 * 1024 + w2],
                            stbuf(sidx)[:, 0:w2], AF.Exp, bias=f32(CSHIFT)),
                            L("exp", sh, sidx))
                for j2_ in range(DC):
                    at_store(NSH - 1, j2_)

            elif which == "vector":
                v = eng
                for b in range(2):
                    em.wait(v, ("F", 0, "proj_mm", NSTR - 1, 31))
                    em.inc("s_ve2", v.memset(vbuf(b)[:, :], 1.0), L("vones", b))
                em.wait(v, ("A1", "vst", NSTR - 1, 11, NCH - 1))
                em.wait(v, ("A1", "qkst", NSTR - 1, 7, 3, 1))
                for sh in range(NSH):
                    b = sh % 2
                    for step in range(DC + 2):
                        if step < DC:
                            kcb = step
                            sidx = SIDX0[kcb]
                            em.wait(v, L("sc_mm", sh, sidx))
                            em.inc("s_ve", v.tensor_add(stbuf(sidx)[:, 0:P], stbuf(sidx)[:, 0:P],
                                                        maskt_sb[:, :]), L("mask", sh, kcb))
                        if step >= 1 and step - 1 < DC:
                            j = step - 1
                            avoff = (j % 2) * 512
                            em.wait(v, L("av_mm", sh, j))
                            em.inc("s_ve", v.reciprocal(rinv_sb[:, j % 2:j % 2 + 1],
                                                        pb2[1][:, avoff + 128:avoff + 129]),
                                   L("rinv", sh, j))
                            em.wait(v, L("rinv", sh, j))
                            gav = sh * DC + j
                            if gav >= 2:
                                gp = gav - 2
                                em.wait(v, L("avtpc", gp // DC, gp % DC))
                            em.inc("s_ve", v.tensor_scalar(
                                qo2_sb[:, (j % 2) * P:(j % 2 + 1) * P],
                                pb2[1][:, avoff:avoff + P], rinv_sb[:, j % 2:j % 2 + 1], None,
                                ALU.mult), L("avsc", sh, j))
                            em.inc("s_ve", v.engine_nop(), L("avdone", sh, j))
                        if step >= 2 and step - 2 < DC:
                            j2 = step - 2
                            em.wait(v, L("avtp", sh, j2))
                            if sh >= 1:
                                em.wait(v, L("atst", sh - 1, j2))
                            em.inc("s_ve", v.tensor_copy(
                                tmp_bf[:, j2 * P:(j2 + 1) * P],
                                pb2[2].bitcast(BF16)[:, (j2 % 2) * 1024:(j2 % 2) * 1024 + P]),
                                L("avst", sh, j2))
                            em.inc("s_ve", v.engine_nop(), L("avtpc", sh, j2))
                    em.inc("s_ve", v.engine_nop(), L("shdone", sh))

        def phase_A3(which, eng, xo):
            L = lambda *a: ("A3",) + a

            if which == "sync":
                sy = eng
                em.wait(sy, ("cast_wo",))
                em.wait(sy, ("A2", "shdone", NSH - 1))
                n = 0
                for oc in range(DC):
                    for cr in range(4):
                        em.inc("s_ld", sy.dma_start_transpose(
                            wbuf[:, oc * 2048 + cr * 512:oc * 2048 + (cr + 1) * 512],
                            d_wob[cr * 512:(cr + 1) * 512, oc * P:(oc + 1) * P]),
                            L("wo", n), 16)
                        n += 1
                for tc in range(NTC):
                    if tc >= 2:
                        em.wait(sy, L("atfree", tc - 2))
                    else:
                        em.wait(sy, ("A2", "av_mm", NSH - 1, DC - 1))
                    sh_last = (tc * P) // T * H + H - 1
                    em.wait(sy, ("A2", "atst", sh_last, tc % (T // P)))
                    em.inc("s_xl", sy.dma_start(
                        out=xt_sb[:, (tc % 2) * 2048:(tc % 2 + 1) * 2048]
                        .rearrange("p (c t) -> p c t", t=P),
                        in_=d_at[:, tc * P:(tc + 1) * P].rearrange("(c p) t -> p c t", p=P)),
                        L("at", tc), 16)
                    if tc >= 2:
                        em.wait(sy, L("x1free", tc - 2))
                    else:
                        em.wait(sy, ("A1", "xfa", NSTR - 1, NCH - 2 + tc))
                        em.wait(sy, ("A1", "sq", NSTR - 1, NCH - 2 + tc))
                    em.inc("s_xl", sy.dma_start(
                        out=xch[:, (tc % 2) * D:(tc % 2 + 1) * D],
                        in_=d_x1[tc * P:(tc + 1) * P, :]), L("x1", tc), 16)
                    em.wait(sy, L("x2done", tc))
                    em.inc("s_st", sy.dma_start(
                        out=xo[tc * P:(tc + 1) * P, :], in_=y_sb[:, :]), L("st", tc), 16)

            elif which == "tensor":
                t = eng
                for n in range(64):
                    em.wait(t, L("wo", n))
                em.wait(t, ("A2", "avtpc", NSH - 1, DC - 1))
                for tc in range(NTC):
                    em.wait(t, L("at", tc))
                    for foq in range(4):
                        jj = tc * 4 + foq
                        bank = 4 + jj % 4
                        if jj >= 4:
                            em.wait(t, L("pfree", jj - 4))
                        for oc in range(DC):
                            ins = t.matmul(
                                pbk[bank][:, :],
                                xt_sb[:, (tc % 2) * 2048 + oc * P:(tc % 2) * 2048 + (oc + 1) * P],
                                wbuf[:, oc * 2048 + foq * 512:oc * 2048 + (foq + 1) * 512],
                                start=(oc == 0), stop=(oc == DC - 1))
                        em.inc("s_mm", ins, L("mm", tc, foq))
                        if foq == 3:
                            em.inc("s_pe2", t.nop(), L("atfree", tc))

            elif which == "vector":
                v = eng
                for tc in range(NTC):
                    em.wait(v, L("x1", tc))
                    if tc >= 1:
                        em.wait(v, L("st", tc - 1))
                    else:
                        em.wait(v, ("F", 0, "st", NSTR - 1, NCH - 1))
                    for foq in range(4):
                        em.wait(v, L("mm", tc, foq))
                        em.inc("s_ve2", v.tensor_add(
                            y_sb[:, foq * 512:(foq + 1) * 512],
                            xch[:, (tc % 2) * D + foq * 512:(tc % 2) * D + (foq + 1) * 512],
                            pbk[4 + (tc * 4 + foq) % 4][:, :]), L("pfree", tc * 4 + foq))
                    em.inc("s_ve2", v.engine_nop(), L("x2done", tc))
                    em.inc("s_ve2", v.engine_nop(), L("x1free", tc))
                em.inc("s_ve2", eng.engine_nop(), ("A3", "done_ve"))

        def phase_final(which, eng):
            if which != "sync" or em.mode != "emit":
                return
            for sn in SEM_NAMES:
                cnt = em.cnt.get(sn, 0)
                if cnt > 0:
                    eng.wait_ge(em.sems[sn].h, cnt)

        def run_phases(which, eng):
            phase_prep(which, eng)
            if upto.startswith("dbg_"):
                m = upto[4:]
                ns = NSTR
                if "_s" in m:
                    m, _, nss = m.partition("_s")
                    ns = int(nss)
                pmap = {"ln": (), "tp": ("tp",), "rt": ("tp", "rt"),
                        "w8": ("tp", "rt", "w8c"), "wbc": ("tp", "rt", "w8c", "wbc"),
                        "pro": ("pro",), "exp": ("pro", "exp")}
                phase_F(which, eng, 0, d_x, d_out, parts=pmap[m], nstr=ns)
                return
            if upto == "f1":
                phase_F(which, eng, 0, d_x, d_out)
                return
            phase_F(which, eng, 0, d_x, d_x1,
                    sync_hook=(lambda: phase_qkvT(which, eng)) if which == "sync" else None)
            phase_A1(which, eng)
            phase_A2(which, eng)
            if upto == "a3":
                phase_A3(which, eng, d_out)
                return
            phase_A3(which, eng, d_x2)
            phase_F(which, eng, 1, d_x2, d_out)

        em.mode = "plan"
        fake = _FakeEng()
        for which in ENGINES:
            run_phases(which, fake)
        total_plan = dict(em.cnt)

        em.mode = "emit"
        em.cnt = total_plan
        global _LAST_EM
        _LAST_EM = em

        block = ctx.enter_context(nc.Block())

        @block.sync
        def _(sy):
            run_phases("sync", sy)
            phase_final("sync", sy)

        @block.gpsimd
        def _(g):
            run_phases("gpsimd", g)

        @block.tensor
        def _(t):
            run_phases("tensor", t)

        @block.scalar
        def _(sc):
            run_phases("scalar", sc)

        @block.vector
        def _(v):
            run_phases("vector", v)

    return nc



def _host_constants():
    ident = np.eye(P, dtype=bf16)
    pm = np.zeros((P, P), dtype=np.float32)
    for i in range(P // 2):
        pm[2 * i + 1, 2 * i] = -1.0
        pm[2 * i, 2 * i + 1] = 1.0
    pm = pm.astype(bf16)
    inv_freq = (1.0 / 10000.0 ** (np.arange(0, P, 2, dtype=np.float64) / P))
    pos = (np.arange(TOK) % T).astype(np.float64)
    ang = np.repeat(inv_freq, 2)[:, None] * pos[None, :]
    cos = np.cos(ang).astype(np.float32).astype(bf16)
    sin = np.sin(ang).astype(np.float32).astype(bf16)
    kl = np.arange(P)[:, None]
    ql = np.arange(P)[None, :]
    maskt = np.where(kl <= ql, 0.0, NEG).astype(np.float32)
    return ident, pm, cos, sin, maskt


_UPTO = "full"
_TAP = None
_LAST_EM = None


def kernel(L0_attn_x, L0_attn_W_qkv, L0_attn_W_o, L0_ffn_router_w, L0_ffn_router_b,
           L0_ffn_expert_w, L0_ffn_expert_b, L0_ffn_proj_w, L0_ffn_proj_b, L0_ffn_norm_w):
    x = np.ascontiguousarray(np.asarray(L0_attn_x, dtype=np.float32))
    wqkv = np.ascontiguousarray(np.asarray(L0_attn_W_qkv, dtype=np.float32))
    wo = np.ascontiguousarray(np.asarray(L0_attn_W_o, dtype=np.float32))
    rw = np.ascontiguousarray(np.asarray(L0_ffn_router_w, dtype=np.float32))
    rb = np.ascontiguousarray(np.asarray(L0_ffn_router_b, dtype=np.float32)).reshape(1, E)
    ew = np.ascontiguousarray(np.asarray(L0_ffn_expert_w, dtype=np.float32))
    eb = np.ascontiguousarray(np.asarray(L0_ffn_expert_b, dtype=np.float32))
    pw = np.ascontiguousarray(np.asarray(L0_ffn_proj_w, dtype=np.float32))
    pb = np.ascontiguousarray(np.asarray(L0_ffn_proj_b, dtype=np.float32)).reshape(1, D)
    nw = np.ascontiguousarray(np.asarray(L0_ffn_norm_w, dtype=np.float32)).reshape(1, D)

    ebl = np.ascontiguousarray(eb.reshape(E, DC, P).transpose(2, 1, 0).reshape(P, DC * E))
    ident, pm, cos, sin, maskt = _host_constants()

    nc = build_program(_UPTO, _TAP)
    in_maps = []
    for c in range(8):
        xs = np.ascontiguousarray(x[SEQ * c:SEQ * (c + 1)].reshape(TOK, D))
        in_maps.append({
            "x_in": xs, "w_qkv": wqkv, "w_o": wo, "router_w": rw, "router_b": rb,
            "expert_w": ew, "expert_b_laid": ebl, "proj_w": pw, "proj_b": pb,
            "norm_w": nw, "c_ident": ident, "c_rope_pm": pm, "c_cos": cos,
            "c_sin": sin, "c_maskt": maskt,
        })
    res = run_bass_kernel_spmd(nc, in_maps, core_ids=list(range(8)))
    out = np.stack([np.asarray(r["x_out"]).reshape(SEQ, T, D) for r in res.results], axis=0)
    return out.reshape(16, T, D)


# version 15 (final)
# speedup vs baseline: 1.0251x; 1.0251x over previous
import numpy as np
from contextlib import ExitStack

import concourse.bass as bass
import concourse.mybir as mybir
from concourse.bass_utils import run_bass_kernel_spmd
import ml_dtypes

F32 = mybir.dt.float32
BF16 = mybir.dt.bfloat16
AF = mybir.ActivationFunctionType
ALU = mybir.AluOpType
AX = mybir.AxisListType.X
f32 = np.float32
bf16 = ml_dtypes.bfloat16

P = 128
D = 2048
DC = D // P
E = 8
H = 16
T = 2048
SEQ = 2
TOK = SEQ * T
STRIPE = 1024
NSTR = TOK // STRIPE
NCH = STRIPE // P
NTC = TOK // P
OQKV = 3 * D
NSH = SEQ * H
LN_EPS = 1e-5
RMS_EPS_QK = 1.1920929e-07
RMS_EPS_FFN = 1e-6
CSHIFT = -12.0
NEG = -1e9
HALF_LN128 = 0.5 * float(np.log(128.0))

ENGINES = ("sync", "gpsimd", "tensor", "scalar", "vector")


class _FakeInstr:
    def then_inc(self, *a, **k):
        return self


class _FakeEng:
    def __getattr__(self, name):
        return lambda *a, **k: _FakeInstr()


class EM:
    def __init__(self):
        self.mode = "plan"
        self.cnt = {}
        self.plan = {}
        self.sems = {}

    def inc(self, sem, instr, label=None, k=1):
        if self.mode == "plan":
            self.cnt[sem] = self.cnt.get(sem, 0) + k
            if label is not None:
                assert label not in self.plan, f"dup label {label}"
                self.plan[label] = (sem, self.cnt[sem])
        else:
            s = self.sems[sem]
            instr.then_inc(s.h, k)
            s.n += k
            if label is not None:
                assert self.plan[label] == (sem, s.n), (label, self.plan[label], sem, s.n)

    def wait(self, eng, label):
        if self.mode == "emit":
            sem, cnt = self.plan[label]
            eng.wait_ge(self.sems[sem].h, cnt)


class Sem:
    def __init__(self, ctx, nc, name):
        self.h = ctx.enter_context(nc.semaphore(name))
        self.n = 0


def build_program(upto="full", tap=None):
    nc = bass.Bass("TRN2", target_bir_lowering=False)

    d_x = nc.dram_tensor("x_in", [TOK, D], F32, kind="ExternalInput").ap()
    d_wqkv = nc.dram_tensor("w_qkv", [OQKV, D], F32, kind="ExternalInput").ap()
    d_wo = nc.dram_tensor("w_o", [D, D], F32, kind="ExternalInput").ap()
    d_rw = nc.dram_tensor("router_w", [D, E], F32, kind="ExternalInput").ap()
    d_rb = nc.dram_tensor("router_b", [1, E], F32, kind="ExternalInput").ap()
    d_ew = nc.dram_tensor("expert_w", [E, D, D], F32, kind="ExternalInput").ap()
    d_ebl = nc.dram_tensor("expert_b_laid", [P, DC * E], F32, kind="ExternalInput").ap()
    d_pw = nc.dram_tensor("proj_w", [D, D], F32, kind="ExternalInput").ap()
    d_pb = nc.dram_tensor("proj_b", [1, D], F32, kind="ExternalInput").ap()
    d_nw = nc.dram_tensor("norm_w", [1, D], F32, kind="ExternalInput").ap()
    d_ident = nc.dram_tensor("c_ident", [P, P], BF16, kind="ExternalInput").ap()
    d_pm = nc.dram_tensor("c_rope_pm", [P, P], BF16, kind="ExternalInput").ap()
    d_cos = nc.dram_tensor("c_cos", [P, TOK], BF16, kind="ExternalInput").ap()
    d_sin = nc.dram_tensor("c_sin", [P, TOK], BF16, kind="ExternalInput").ap()
    d_maskt = nc.dram_tensor("c_maskt", [P, P], F32, kind="ExternalInput").ap()
    d_out = nc.dram_tensor("x_out", [TOK, D], F32, kind="ExternalOutput").ap()

    d_ewb = nc.dram_tensor("s_ewb", [E, D, D], BF16, kind="Internal").ap()
    d_pwb = nc.dram_tensor("s_pwb", [D, D], BF16, kind="Internal").ap()
    d_qkvb = nc.dram_tensor("s_qkvb", [OQKV, D], BF16, kind="Internal").ap()
    d_qkvT = nc.dram_tensor("s_qkvT", [D, OQKV], BF16, kind="Internal").ap()
    d_wob = nc.dram_tensor("s_wob", [D, D], BF16, kind="Internal").ap()
    d_x1 = nc.dram_tensor("s_x1", [TOK, D], F32, kind="Internal").ap()
    d_x2 = nc.dram_tensor("s_x2", [TOK, D], F32, kind="Internal").ap()
    d_qt = nc.dram_tensor("s_qt", [D, TOK], BF16, kind="Internal").ap()
    d_kt = nc.dram_tensor("s_kt", [D, TOK], BF16, kind="Internal").ap()
    d_v = nc.dram_tensor("s_v", [TOK, D], BF16, kind="Internal").ap()
    d_at = nc.dram_tensor("s_at", [D, TOK], BF16, kind="Internal").ap()

    ctx = ExitStack()
    with ctx:
        def sbuf(name, shape, dt=F32):
            return ctx.enter_context(nc.sbuf_tensor(name, shape, dt))

        xt_sb = sbuf("xt_sb", [P, DC * STRIPE], BF16)
        comb_sb = sbuf("comb_sb", [P, DC * STRIPE], BF16)
        wbuf = sbuf("wbuf", [P, 2 * DC * STRIPE], BF16)
        xch = sbuf("xch", [P, 3 * D])
        xln_sb = sbuf("xln_sb", [P, 2 * D], BF16)
        wbc_sb = sbuf("wbc_sb", [P, E * STRIPE], BF16)
        nwb_sb = sbuf("nwb_sb", [P, D], BF16)
        sq_sb = sbuf("sq_sb", [P, 512], BF16)
        y_sb = sbuf("y_sb", [P, D])
        tmp_bf = sbuf("tmp_bf", [P, 2048], BF16)
        qo2_sb = sbuf("qo2_sb", [P, 512], BF16)
        qp2_sb = sbuf("qp2_sb", [P, 512], BF16)
        ident = sbuf("ident", [P, P], BF16)
        pm_sb = sbuf("pm_sb", [P, P], BF16)
        maskt_sb = sbuf("maskt_sb", [P, P])
        ones_col = sbuf("ones_col", [P, 1], BF16)
        ones_row = sbuf("ones_row", [1, P], BF16)
        bias_sb = sbuf("bias_sb", [P, DC * E])
        rb_bc = sbuf("rb_bc", [P, E])
        rw_sb = sbuf("rw_sb", [P, DC * E], BF16)
        pbrow = sbuf("pbrow", [1, D], BF16)
        wrow_sb = sbuf("wrow_sb", [1, E * P], BF16)
        rms_rowb = sbuf("rms_rowb", [1, 512], BF16)
        stats = sbuf("stats", [P, 4 * NCH])
        logit_sb = sbuf("logit_sb", [P, E])
        m1_sb = sbuf("m1_sb", [P, 1])
        l2_sb = sbuf("l2_sb", [P, E])
        m2_sb = sbuf("m2_sb", [P, 1])
        nm1_sb = sbuf("nm1_sb", [P, 1])
        keep_sb = sbuf("keep_sb", [P, E])
        es_sb = sbuf("es_sb", [P, E])
        sk_sb = sbuf("sk_sb", [P, E])
        w8_sb = sbuf("w8_sb", [P, E], BF16)
        red_sb = sbuf("red_sb", [P, 16])
        rinv_sb = sbuf("rinv_sb", [P, 2])

        pb2 = [ctx.enter_context(nc.psum_tensor(f"pb2_{i}", [P, 1024], F32)) for i in range(4)]

        class _PB:
            def __getitem__(self, i):
                return pb2[i // 2][:, (i % 2) * 512:(i % 2 + 1) * 512]

        pbk = _PB()

        def pbb(i):
            return pb2[i // 2].bitcast(BF16)[:, (i % 2) * 1024:(i % 2 + 1) * 1024]

        em = EM()
        SEM_NAMES = ["s_cast", "s_ld", "s_xl", "s_st", "s_ast", "s_mm", "s_pe2", "s_wfree",
                     "s_act", "s_act2", "s_ve", "s_ve2", "s_pfree", "s_xfa", "s_xfb"]
        for sn in SEM_NAMES:
            em.sems[sn] = Sem(ctx, nc, sn)

        def wslot(j2):
            return wbuf[:, j2 * 16384:(j2 + 1) * 16384]

        def a1slot(j4):
            return wbuf[:, j4 * 8192:(j4 + 1) * 8192]

        COS_OFF = 0
        SIN_OFF = TOK

        QK_TILES = []
        V_TILES = []
        for _s in range(NSTR):
            for _o in range(12):
                if _o < 8:
                    for _oc in range(4):
                        for _th in range(2):
                            QK_TILES.append((_s, _o, _oc, _th))
                else:
                    for _tc in range(NCH):
                        V_TILES.append((_s, _o, _tc))
        QK_IDX = {t: i for i, t in enumerate(QK_TILES)}
        V_IDX = {t: i for i, t in enumerate(V_TILES)}

        def phase_prep(which, eng):
            if which != "gpsimd":
                return
            g = eng
            em.inc("s_cast", g.dma_start(
                out=rw_sb[:, :].rearrange("p (c e) -> p c e", e=E),
                in_=d_rw.rearrange("(c p) e -> p c e", p=P)), None, 16)
            em.inc("s_cast", g.dma_start(out=pbrow[:, :], in_=d_pb), None, 16)
            em.inc("s_cast", g.dma_start(
                out=nwb_sb[:, :], in_=d_nw[:, :].partition_broadcast(P)[:, 0, :]), None, 16)
            g.memset(ones_col[:, :], 1.0)
            g.memset(ones_row[:, :], 1.0)
            em.inc("s_cast", g.dma_start(out=ident[:, :], in_=d_ident), None, 16)
            em.inc("s_cast", g.dma_start(out=pm_sb[:, :], in_=d_pm), None, 16)
            em.inc("s_cast", g.dma_start(out=maskt_sb[:, :], in_=d_maskt), None, 16)
            em.inc("s_cast", g.dma_start(out=bias_sb[:, :], in_=d_ebl), None, 16)
            em.inc("s_cast", g.dma_start(
                out=rb_bc[:, :], in_=d_rb[:, :].partition_broadcast(P)[:, 0, :]),
                ("prep_done",), 16)
            em.inc("s_cast", g.dma_start(out=d_qkvb, in_=d_wqkv), ("cast_qkv",), 16)
            em.inc("s_cast", g.dma_start(out=d_wob, in_=d_wo), ("cast_wo",), 16)

        def phase_F(which, eng, fp, xin, xout, parts=("pro", "exp", "proj"), nstr=NSTR,
                    sync_hook=None):
            L = lambda *a: ("F", fp) + a
            NSLOT = 18

            def SO(s):
                return (s % 2) * 2 * NCH

            if which == "gpsimd":
                g_ = eng
                for s_ in range(nstr):
                    for e_ in range(E):
                        for fh_ in range(2):
                            j_ = e_ * 2 + fh_
                            prev_ = s_ * NSLOT + j_ - 2
                            if prev_ >= 0:
                                em.wait(g_, L("wfree", prev_ // NSLOT, prev_ % NSLOT))
                            elif fp > 0:
                                em.wait(g_, ("A3", "mm", NTC - 1, 3))
                            em.inc("s_cast", g_.dma_start(
                                out=wslot(j_ % 2).rearrange("p (c f) -> p c f", f=1024),
                                in_=d_ew[e_][:, fh_ * 1024:(fh_ + 1) * 1024]
                                .rearrange("(c p) f -> p c f", p=P)), L("wl", s_, j_), 16)
                    for fh_ in range(2):
                        j_ = 16 + fh_
                        prev_ = s_ * NSLOT + j_ - 2
                        em.wait(g_, L("wfree", prev_ // NSLOT, prev_ % NSLOT))
                        em.inc("s_cast", g_.dma_start(
                            out=wslot(j_ % 2).rearrange("p (c f) -> p c f", f=1024),
                            in_=d_pw[:, fh_ * 1024:(fh_ + 1) * 1024]
                            .rearrange("(c p) f -> p c f", p=P)), L("wl", s_, j_), 16)
                if tap == "comb" and fp == 0:
                    em.wait(g_, ("F", 0, "comb", 0, 255))
                    em.wait(g_, ("F", 0, "st", 0, NCH - 1))
                    for r_ in range(8):
                        em.inc("s_cast", g_.dma_start(
                            out=xout[r_ * P:(r_ + 1) * P, :],
                            in_=comb_sb[:, r_ * 2048:(r_ + 1) * 2048]), ("tapc", r_), 16)
                return

            if which == "sync":
                sy = eng

                def sy_xl(s):
                    for tc in range(NCH):
                        g = s * NCH + tc
                        if g >= 2:
                            em.wait(sy, L("xfa", (g - 2) // NCH, (g - 2) % NCH))
                            em.wait(sy, L("sq", (g - 2) // NCH, (g - 2) % NCH))
                        elif fp > 0:
                            em.wait(sy, ("A3", "x1free", NTC - 1))
                        em.inc("s_xl", sy.dma_start(
                            out=xch[:, (tc % 2) * D:(tc % 2 + 1) * D],
                            in_=xin[(s * NCH + tc) * P:(s * NCH + tc + 1) * P, :]),
                            L("xl", s, tc), 16)

                def sy_ep(s):
                    for tc in range(NCH):
                        g = s * NCH + tc
                        if g >= 1:
                            em.wait(sy, L("ep_done", (g - 1) // NCH, (g - 1) % NCH))
                        elif fp > 0:
                            em.wait(sy, ("F", 0, "ep_done", NSTR - 1, NCH - 1))
                        em.inc("s_xl", sy.dma_start(
                            out=xch[:, 2 * D:3 * D],
                            in_=xin[(s * NCH + tc) * P:(s * NCH + tc + 1) * P, :]),
                            L("xe", s, tc), 16)
                        em.wait(sy, L("ep_done", s, tc))
                        em.inc("s_st", sy.dma_start(
                            out=xout[(s * NCH + tc) * P:(s * NCH + tc + 1) * P, :],
                            in_=y_sb[:, :]), L("st", s, tc), 16)

                for s in range(nstr):
                    sy_xl(s)
                    if s == 0 and sync_hook is not None:
                        sync_hook()
                    if s >= 1:
                        sy_ep(s - 1)
                sy_ep(nstr - 1)

            elif which == "tensor":
                t = eng
                if fp == 0:
                    em.wait(t, ("prep_done",))
                else:
                    em.wait(t, ("A3", "done_ve"))
                    em.wait(t, ("A2", "exp", NSH - 1, 23))
                    em.wait(t, ("A2", "avdone", NSH - 1, DC - 1))
                    em.wait(t, ("A2", "avtpc", NSH - 1, DC - 1))

                def t_tp(s):
                    for tc in range(NCH):
                        em.wait(t, L("xln", s, tc))
                        for dc in range(DC):
                            k = tc * DC + dc
                            kk = s * NCH * DC + k
                            if kk >= 2:
                                em.wait(t, L("xtc", (kk - 2) // (NCH * DC), (kk - 2) % (NCH * DC)))
                            elif fp > 0:
                                em.wait(t, ("A1", "xtc", NSTR - 1, NCH * DC - 2 + kk))
                            em.inc("s_pe2", t.transpose(
                                pbb(2 + k % 2)[:, 0:P],
                                xln_sb[:, (tc % 2) * D + dc * P:(tc % 2) * D + (dc + 1) * P],
                                ident[:, :]), L("tp", s, k))

                def t_rt(s):
                    if s >= 1:
                        em.wait(t, L("pfree", s - 1, 31))
                    for tc in range(NCH):
                        em.wait(t, L("xtc", s, tc * DC + DC - 1))
                        gg = s * NCH + tc - 1
                        if gg >= 0:
                            em.wait(t, L("rtfree", gg // NCH, gg % NCH))
                        for dc in range(DC):
                            ins = t.matmul(pbk[4][:, 0:8],
                                           xt_sb[:, dc * STRIPE + tc * P:dc * STRIPE + (tc + 1) * P],
                                           rw_sb[:, dc * E:(dc + 1) * E],
                                           start=(dc == 0), stop=(dc == DC - 1))
                        em.inc("s_mm", ins, L("rt_mm", s, tc))
                        em.wait(t, L("w8", s, tc))
                        if gg >= 0:
                            em.wait(t, L("w8free", gg // NCH, gg % NCH))
                        for e in range(E):
                            ins = t.transpose(pbb(5)[0:1, e * P:(e + 1) * P], w8_sb[:, e:e + 1],
                                              ident[:, :])
                        em.inc("s_pe2", ins, L("w8tp", s, tc))
                        em.wait(t, L("wrow", s, tc))
                        for e in range(E):
                            iw = (s * NCH + tc) * E + e
                            if iw >= 2:
                                iwp = iw - 2
                                em.wait(t, L("wbcc", iwp // (NCH * E), (iwp % (NCH * E)) // E,
                                             iwp % E))
                            em.inc("s_pe2", t.matmul(
                                pbk[6 + e % 2][:, 0:P], ones_row[:, :],
                                wrow_sb[0:1, e * P:(e + 1) * P], start=True, stop=True),
                                L("wbc_mm", s, tc, e))

                def t_exp(s):
                    for e in range(E):
                        for fh in range(2):
                            j = e * 2 + fh
                            em.wait(t, L("wl", s, j))
                            for fc in range(NCH):
                                for th in range(2):
                                    i = (j * NCH + fc) * 2 + th
                                    gi = s * 256 + i
                                    bank = gi % 2
                                    if i == 0:
                                        em.wait(t, L("xtc", s, NCH * DC - 1))
                                        em.wait(t, L("wbcc", s, NCH - 1, E - 1))
                                    if gi >= 2:
                                        gp = gi - 2
                                        em.wait(t, L("silu", gp // 256, gp % 256))
                                    for dc in range(DC):
                                        ins = t.matmul(
                                            pbk[bank][:, :],
                                            wslot(j % 2)[:, dc * 1024 + fc * P:dc * 1024 + (fc + 1) * P],
                                            xt_sb[:, dc * STRIPE + th * 512:dc * STRIPE + (th + 1) * 512],
                                            start=(dc == 0), stop=(dc == DC - 1))
                                    em.inc("s_mm", ins, L("exp_mm", s, i))
                                    if fc == NCH - 1 and th == 1:
                                        em.inc("s_wfree", t.nop(), L("wfree", s, j))

                def t_proj(s):
                    em.wait(t, L("wl", s, 16))
                    em.wait(t, L("wl", s, 17))
                    em.wait(t, L("comb", s, 255))
                    for tc in range(NCH):
                        for foq in range(4):
                            jj = tc * 4 + foq
                            gj = s * 32 + jj
                            bank = 4 + jj % 4
                            gp = gj - 4
                            if gp >= 0:
                                em.wait(t, L("pfree", gp // 32, gp % 32))
                            slot = foq // 2
                            for fgc in range(DC):
                                t.matmul(pbk[bank][:, :],
                                         comb_sb[:, fgc * STRIPE + tc * P:fgc * STRIPE + (tc + 1) * P],
                                         wslot(slot)[:, fgc * 1024 + (foq % 2) * 512:
                                                     fgc * 1024 + (foq % 2) * 512 + 512],
                                         start=(fgc == 0), stop=False)
                            ins = t.matmul(pbk[bank][:, :], ones_row[:, :],
                                           pbrow[0:1, foq * 512:(foq + 1) * 512],
                                           start=False, stop=True)
                            em.inc("s_mm", ins, L("proj_mm", s, jj))
                            if jj == 30:
                                em.inc("s_wfree", t.nop(), L("wfree", s, 16))
                            if jj == 31:
                                em.inc("s_wfree", t.nop(), L("wfree", s, 17))

                t_tp(0)
                t_rt(0)
                for s in range(nstr):
                    t_exp(s)
                    if s + 1 < nstr:
                        t_tp(s + 1)
                    t_proj(s)
                    if s + 1 < nstr:
                        t_rt(s + 1)

            elif which == "scalar":
                sc = eng
                if fp > 0:
                    em.wait(sc, ("A2", "atst", NSH - 1, DC - 1))
                    em.wait(sc, ("A2", "avtp", NSH - 1, DC - 1))

                def sc_ln(s):
                    for tc in range(NCH):
                        g = s * NCH + tc
                        pv = tc % 2
                        em.wait(sc, L("xl", s, tc))
                        if g >= 1:
                            em.wait(sc, L("var", (g - 1) // NCH, (g - 1) % NCH))
                        elif fp > 0:
                            em.wait(sc, ("A1", "var", NSTR - 1, NCH - 1))
                        for q4 in range(4):
                            ins = sc.activation(
                                sq_sb[:, :],
                                xch[:, (tc % 2) * D + q4 * 512:(tc % 2) * D + (q4 + 1) * 512],
                                AF.Square, accum_out=red_sb[:, 8 + q4:9 + q4])
                        em.inc("s_act2", ins, L("sq", s, tc))
                        em.wait(sc, L("var", s, tc))
                        em.inc("s_act2", sc.activation(red_sb[:, 2 + pv:3 + pv], red_sb[:, pv:pv + 1],
                                                       AF.Ln, bias=f32(LN_EPS)), L("lnv", s, tc))
                        em.wait(sc, L("lnv", s, tc))
                        em.inc("s_act2", sc.activation(stats[:, SO(s) + 2 * tc + 1:SO(s) + 2 * tc + 2],
                                                       red_sb[:, 2 + pv:3 + pv], AF.Exp, scale=f32(-0.5)),
                               L("istd", s, tc))
                        for dc in range(DC):
                            k = tc * DC + dc
                            em.wait(sc, L("tp", s, k))
                            em.inc("s_act2", sc.activation(
                                xt_sb[:, dc * STRIPE + tc * P:dc * STRIPE + (tc + 1) * P],
                                pbb(2 + k % 2)[:, 0:P], AF.Copy), L("xtc", s, k))

                def sc_es(s):
                    for tc in range(NCH):
                        gg = s * NCH + tc - 1
                        if gg >= 0:
                            em.wait(sc, L("sk", gg // NCH, gg % NCH))
                        em.wait(sc, L("nm1", s, tc))
                        em.inc("s_act2", sc.activation(es_sb[:, :], logit_sb[:, :], AF.Exp,
                                                       bias=nm1_sb[:, :]), L("es", s, tc))

                def sc_silu(s):
                    for e in range(E):
                        for fh in range(2):
                            for fc in range(NCH):
                                for th in range(2):
                                    i = ((e * 2 + fh) * NCH + fc) * 2 + th
                                    gi = s * 256 + i
                                    bank = gi % 2
                                    fgc = fh * NCH + fc
                                    em.wait(sc, L("exp_mm", s, i))
                                    gp = gi - 2
                                    if gp >= 0:
                                        em.wait(sc, L("comb", gp // 256, gp % 256))
                                    em.inc("s_act", sc.activation(
                                        tmp_bf[:, (gi % 2) * 512:(gi % 2 + 1) * 512],
                                        pbk[bank][:, :], AF.Silu,
                                        bias=bias_sb[:, fgc * E + e:fgc * E + e + 1]),
                                        L("silu", s, i))

                def sc_ep(s):
                    for tc in range(NCH):
                        g = s * NCH + tc
                        em.wait(sc, L("ycomp", s, tc))
                        if g >= 1:
                            em.wait(sc, L("yps", (g - 1) // NCH, (g - 1) % NCH))
                        elif fp > 0:
                            em.wait(sc, ("F", 0, "yps", NSTR - 1, NCH - 1))
                        for q4 in range(4):
                            ins = sc.activation(sq_sb[:, :], y_sb[:, q4 * 512:(q4 + 1) * 512],
                                                AF.Square, accum_out=red_sb[:, 12 + q4:13 + q4])
                        em.inc("s_act2", ins, L("sqy", s, tc))
                        em.wait(sc, L("yps", s, tc))
                        em.inc("s_act2", sc.activation(red_sb[:, 5:6], red_sb[:, 4:5], AF.Ln,
                                                       scale=f32(1.0 / D), bias=f32(RMS_EPS_FFN)),
                               L("lny", s, tc))
                        em.wait(sc, L("lny", s, tc))
                        if g >= 1:
                            em.wait(sc, L("contrib1", (g - 1) // NCH, (g - 1) % NCH))
                        elif fp > 0:
                            em.wait(sc, ("F", 0, "contrib1", NSTR - 1, NCH - 1))
                        em.inc("s_act2", sc.activation(red_sb[:, 6:7], red_sb[:, 5:6], AF.Exp,
                                                       scale=f32(-0.5)), L("irms", s, tc))

                sc_ln(0)
                sc_es(0)
                for s in range(nstr):
                    sc_silu(s)
                    if s + 1 < nstr:
                        sc_ln(s + 1)
                    sc_ep(s)
                    if s + 1 < nstr:
                        sc_es(s + 1)

            elif which == "vector":
                v = eng

                def v_proln(s):
                    for tc in range(NCH):
                        slot = (tc % 2) * D
                        g = s * NCH + tc
                        pv = tc % 2
                        if fp > 0 and s == 0 and tc < 2:
                            em.wait(v, ("A1", "tp", NSTR - 1, NCH * DC - 1))
                            if tc == 0:
                                em.wait(v, ("A3", "mm", NTC - 1, 3))
                        em.wait(v, L("xl", s, tc))
                        em.inc("s_ve2", v.reduce_sum(red_sb[:, 7:8], xch[:, slot:slot + D],
                                                     axis=AX, negate=True), L("nsum", s, tc))
                        em.wait(v, L("nsum", s, tc))
                        em.inc("s_ve2", v.tensor_scalar(stats[:, SO(s) + 2 * tc:SO(s) + 2 * tc + 1],
                                                        red_sb[:, 7:8], 1.0 / D, None, ALU.mult),
                               L("nmean", s, tc))
                        em.wait(v, L("sq", s, tc))
                        em.wait(v, L("nmean", s, tc))
                        v.tensor_add(red_sb[:, 8:9], red_sb[:, 8:9], red_sb[:, 9:10])
                        em.inc("s_ve2", v.tensor_add(red_sb[:, 10:11], red_sb[:, 10:11],
                                                     red_sb[:, 11:12]), L("p2", s, tc))
                        em.wait(v, L("p2", s, tc))
                        em.inc("s_ve2", v.tensor_add(red_sb[:, 8:9], red_sb[:, 8:9],
                                                     red_sb[:, 10:11]), L("p4", s, tc))
                        if g >= 2:
                            em.wait(v, L("lnv", (g - 2) // NCH, (g - 2) % NCH))
                        elif fp > 0:
                            em.wait(v, ("A1", "lnv", NSTR - 1, NCH - 2 + g))
                        em.inc("s_ve2", v.tensor_tensor(red_sb[:, pv:pv + 1],
                                                        stats[:, SO(s) + 2 * tc:SO(s) + 2 * tc + 1],
                                                        stats[:, SO(s) + 2 * tc:SO(s) + 2 * tc + 1],
                                                        ALU.mult),
                               L("m2done", s, tc))
                        em.wait(v, L("p4", s, tc))
                        em.wait(v, L("m2done", s, tc))
                        em.inc("s_ve2", v.scalar_tensor_tensor(
                            red_sb[:, pv:pv + 1], red_sb[:, 8:9], 1.0 / D,
                            red_sb[:, pv:pv + 1], ALU.mult, ALU.subtract), L("var", s, tc))
                        em.wait(v, L("istd", s, tc))
                        if g >= 2:
                            em.wait(v, L("tp", (g - 2) // NCH, ((g - 2) % NCH) * DC + DC - 1))
                        elif fp > 0:
                            em.wait(v, ("A1", "tp", NSTR - 1, (NCH - 2 + g) * DC + DC - 1))
                        em.inc("s_ve2", v.tensor_scalar(
                            xln_sb[:, slot:slot + D], xch[:, slot:slot + D],
                            stats[:, SO(s) + 2 * tc:SO(s) + 2 * tc + 1],
                            stats[:, SO(s) + 2 * tc + 1:SO(s) + 2 * tc + 2],
                            ALU.add, ALU.mult), L("xln", s, tc))
                        em.inc("s_xfa", v.engine_nop(), L("xfa", s, tc))

                def v_prort(s):
                    for tc in range(NCH):
                        em.wait(v, L("rt_mm", s, tc))
                        gg = s * NCH + tc - 1
                        if gg >= 0:
                            em.wait(v, L("es", gg // NCH, gg % NCH))
                        elif fp > 0:
                            em.wait(v, ("F", 0, "es", NSTR - 1, NCH - 1))
                        em.inc("s_ve", v.tensor_add(logit_sb[:, :], pbk[4][:, 0:8], rb_bc[:, :]),
                               L("rtfree", s, tc))
                        em.wait(v, L("rtfree", s, tc))
                        em.inc("s_ve", v.reduce_max(m1_sb[:, :], logit_sb[:, :], axis=AX),
                               L("m1", s, tc))
                        em.wait(v, L("m1", s, tc))
                        em.inc("s_ve", v.tensor_scalar_mul(nm1_sb[:, :], m1_sb[:, :], -1.0),
                               L("nm1", s, tc))
                        em.inc("s_ve", v.tensor_scalar(l2_sb[:, :], logit_sb[:, :], m1_sb[:, :],
                                                       None, ALU.is_ge), L("lmask", s, tc))
                        em.wait(v, L("lmask", s, tc))
                        em.inc("s_ve", v.scalar_tensor_tensor(l2_sb[:, :], l2_sb[:, :], NEG,
                                                              logit_sb[:, :], ALU.mult, ALU.add),
                               L("l2", s, tc))
                        em.wait(v, L("l2", s, tc))
                        em.inc("s_ve", v.reduce_max(m2_sb[:, :], l2_sb[:, :], axis=AX),
                               L("m2", s, tc))
                        em.wait(v, L("m2", s, tc))
                        em.inc("s_ve", v.tensor_scalar(keep_sb[:, :], logit_sb[:, :], m2_sb[:, :],
                                                       None, ALU.is_ge), L("keep", s, tc))
                        em.wait(v, L("es", s, tc))
                        em.wait(v, L("keep", s, tc))
                        em.inc("s_ve", v.tensor_mul(sk_sb[:, :], es_sb[:, :], keep_sb[:, :]),
                               L("sk", s, tc))
                        em.wait(v, L("sk", s, tc))
                        em.inc("s_ve", v.reduce_sum(red_sb[:, 7:8], sk_sb[:, :], axis=AX),
                               L("den", s, tc))
                        em.wait(v, L("den", s, tc))
                        em.inc("s_ve", v.reciprocal(red_sb[:, 7:8], red_sb[:, 7:8]),
                               L("rden", s, tc))
                        em.wait(v, L("rden", s, tc))
                        em.inc("s_ve", v.tensor_scalar(w8_sb[:, :], sk_sb[:, :], red_sb[:, 7:8],
                                                       None, ALU.mult), L("w8", s, tc))
                        em.wait(v, L("w8tp", s, tc))
                        em.inc("s_ve", v.tensor_copy(wrow_sb[0:1, :], pbb(5)[0:1, 0:E * P]),
                               L("wrow", s, tc), k=1)
                        em.inc("s_ve", v.engine_nop(), L("w8free", s, tc))
                        for e in range(E):
                            em.wait(v, L("wbc_mm", s, tc, e))
                            em.inc("s_ve", v.tensor_copy(
                                wbc_sb[:, e * STRIPE + tc * P:e * STRIPE + (tc + 1) * P],
                                pbk[6 + e % 2][:, 0:P]), L("wbcc", s, tc, e))

                def v_comb(s):
                    if s >= 1:
                        em.wait(v, L("proj_mm", s - 1, 31))
                    elif fp > 0:
                        em.wait(v, ("A2", "av_mm", NSH - 1, DC - 1))
                    for e in range(E):
                        for fh in range(2):
                            for fc in range(NCH):
                                for th in range(2):
                                    i = ((e * 2 + fh) * NCH + fc) * 2 + th
                                    gi = s * 256 + i
                                    fgc = fh * NCH + fc
                                    ts = (gi % 2) * 512
                                    em.wait(v, L("silu", s, i))
                                    if e == 0:
                                        em.inc("s_ve", v.tensor_tensor(
                                            comb_sb[:, fgc * STRIPE + th * 512:
                                                    fgc * STRIPE + (th + 1) * 512],
                                            tmp_bf[:, ts:ts + 512],
                                            wbc_sb[:, th * 512:(th + 1) * 512],
                                            ALU.mult), L("comb", s, i))
                                    else:
                                        em.inc("s_ve", v.tensor_mul(
                                            tmp_bf[:, ts:ts + 512], tmp_bf[:, ts:ts + 512],
                                            wbc_sb[:, e * STRIPE + th * 512:
                                                   e * STRIPE + (th + 1) * 512]),
                                            L("wmul", s, i))
                                        em.wait(v, L("wmul", s, i))
                                        em.inc("s_ve", v.tensor_add(
                                            comb_sb[:, fgc * STRIPE + th * 512:
                                                    fgc * STRIPE + (th + 1) * 512],
                                            comb_sb[:, fgc * STRIPE + th * 512:
                                                    fgc * STRIPE + (th + 1) * 512],
                                            tmp_bf[:, ts:ts + 512]), L("comb", s, i))

                def v_ep(s):
                    for tc in range(NCH):
                        em.wait(v, L("xe", s, tc))
                        g = s * NCH + tc
                        if fp == 0 and g == 0:
                            em.wait(v, ("prep_done",))
                        if g >= 1:
                            em.wait(v, L("st", (g - 1) // NCH, (g - 1) % NCH))
                        elif fp > 0:
                            em.wait(v, ("A3", "st", NTC - 1))
                        em.inc("s_ve2", v.tensor_scalar(
                            y_sb[:, :], xch[:, 2 * D:3 * D],
                            stats[:, SO(s) + 2 * tc:SO(s) + 2 * tc + 1],
                            stats[:, SO(s) + 2 * tc + 1:SO(s) + 2 * tc + 2],
                            ALU.add, ALU.mult),
                            L("yln", s, tc))
                        for foq in range(4):
                            jj = tc * 4 + foq
                            bank = 4 + jj % 4
                            em.wait(v, L("proj_mm", s, jj))
                            if foq == 0:
                                em.wait(v, L("yln", s, tc))
                            em.inc("s_pfree", v.tensor_add(y_sb[:, foq * 512:(foq + 1) * 512],
                                                           y_sb[:, foq * 512:(foq + 1) * 512],
                                                           pbk[bank][:, :]), L("pfree", s, jj))
                        em.inc("s_ve2", v.engine_nop(), L("ycomp", s, tc))
                        em.wait(v, L("sqy", s, tc))
                        if g >= 1:
                            em.wait(v, L("lny", (g - 1) // NCH, (g - 1) % NCH))
                        elif fp > 0:
                            em.wait(v, ("F", 0, "lny", NSTR - 1, NCH - 1))
                        v.tensor_add(red_sb[:, 12:13], red_sb[:, 12:13], red_sb[:, 13:14])
                        em.inc("s_ve2", v.tensor_add(red_sb[:, 14:15], red_sb[:, 14:15],
                                                     red_sb[:, 15:16]), L("yp2", s, tc))
                        em.wait(v, L("yp2", s, tc))
                        em.inc("s_ve2", v.tensor_add(red_sb[:, 4:5], red_sb[:, 12:13],
                                                     red_sb[:, 14:15]), L("yps", s, tc))
                        em.wait(v, L("irms", s, tc))
                        em.inc("s_ve2", v.tensor_scalar(y_sb[:, :], y_sb[:, :], red_sb[:, 6:7],
                                                        0.5, ALU.mult, ALU.mult),
                               L("contrib1", s, tc))
                        em.wait(v, L("contrib1", s, tc))
                        em.inc("s_ve2", v.tensor_tensor(y_sb[:, :], y_sb[:, :], nwb_sb[:, :],
                                                        ALU.mult), L("contrib2", s, tc))
                        em.wait(v, L("contrib2", s, tc))
                        em.inc("s_ve2", v.tensor_add(y_sb[:, :], y_sb[:, :], xch[:, 2 * D:3 * D]),
                               L("ep_done", s, tc))
                        em.inc("s_xfb", v.engine_nop(), L("xfb", s, tc))

                v_proln(0)
                v_prort(0)
                for s in range(nstr):
                    if tap == "comb" and fp == 0 and s == 1:
                        em.wait(v, ("tapc", 7))
                    v_comb(s)
                    if s + 1 < nstr:
                        v_proln(s + 1)
                    v_ep(s)
                    if s + 1 < nstr:
                        v_prort(s + 1)

                if tap == "logit" and fp == 0:
                    pass

        def phase_qkvT(which, eng):
            if which != "sync":
                return
            sy = eng
            em.wait(sy, ("cast_qkv",))
            slots = [qo2_sb[:, :], qp2_sb[:, :]]
            nidx = 0
            for orange in range(12):
                for cc in range(DC):
                    sl = slots[nidx % 2]
                    if nidx >= 2:
                        em.wait(sy, ("qkvT_st", nidx - 2))
                    em.inc("s_ld", sy.dma_start_transpose(
                        sl, d_qkvb[orange * 512:(orange + 1) * 512, cc * P:(cc + 1) * P]),
                        ("qkvT_ld", nidx), 16)
                    em.wait(sy, ("qkvT_ld", nidx))
                    em.inc("s_st", sy.dma_start(
                        out=d_qkvT[cc * P:(cc + 1) * P, orange * 512:(orange + 1) * 512],
                        in_=sl), ("qkvT_st", nidx), 16)
                    nidx += 1

        def phase_A1(which, eng):
            L = lambda *a: ("A1",) + a

            if which == "sync":
                sy = eng
                em.wait(sy, ("F", 0, "proj_mm", NSTR - 1, 31))
                em.wait(sy, ("F", 0, "comb", NSTR - 1, 255))
                em.inc("s_ld", sy.dma_start(out=wbc_sb[:, COS_OFF:COS_OFF + TOK], in_=d_cos),
                       None, 16)
                em.inc("s_ld", sy.dma_start(out=wbc_sb[:, SIN_OFF:SIN_OFF + TOK], in_=d_sin),
                       L("cs"), 16)
                for s in range(NSTR):
                    for tc in range(NCH):
                        g = s * NCH + tc
                        if g >= 2:
                            em.wait(sy, L("xfa", (g - 2) // NCH, (g - 2) % NCH))
                            em.wait(sy, L("sq", (g - 2) // NCH, (g - 2) % NCH))
                        else:
                            em.wait(sy, ("F", 0, "xfa", NSTR - 1, NCH - 2 + g))
                            em.wait(sy, ("F", 0, "sq", NSTR - 1, NCH - 2 + g))
                        em.wait(sy, ("F", 0, "st", s, tc))
                        em.inc("s_xl", sy.dma_start(
                            out=xch[:, (tc % 2) * D:(tc % 2 + 1) * D],
                            in_=d_x1[(s * NCH + tc) * P:(s * NCH + tc + 1) * P, :]),
                            L("xl", s, tc), 16)
                    def a1_store(orange):
                        if orange < 8:
                            for oc in range(4):
                                for th in range(2):
                                    o_abs = orange * 512 + oc * P
                                    is_q = o_abs < D
                                    hh = (o_abs % D) // P
                                    dst = d_qt if is_q else d_kt
                                    em.wait(sy, L("qkout", s, orange, oc, th))
                                    i = QK_IDX[(s, orange, oc, th)]
                                    srcb = tmp_bf[:, 1536:2048] if i % 2 == 0 else qo2_sb[:, :]
                                    em.inc("s_st", sy.dma_start(
                                        out=dst[hh * P:(hh + 1) * P,
                                                s * STRIPE + th * 512:s * STRIPE + (th + 1) * 512],
                                        in_=srcb), L("qkst", s, orange, oc, th), 16)
                        else:
                            for tc in range(NCH):
                                em.wait(sy, L("vout", s, orange, tc))
                                em.inc("s_st", sy.dma_start(
                                    out=d_v[(s * NCH + tc) * P:(s * NCH + tc + 1) * P,
                                            (orange - 8) * 512:(orange - 7) * 512],
                                    in_=tmp_bf[:, (tc % 2) * 512:(tc % 2 + 1) * 512]),
                                    L("vst", s, orange, tc), 16)

                    for orange in range(12):
                        gs = s * 12 + orange
                        if gs == 0:
                            em.wait(sy, ("qkvT_st", 191))
                        if gs >= 4:
                            em.wait(sy, L("wfree", (gs - 4) // 12, (gs - 4) % 12))
                        elif gs in (0, 1):
                            em.wait(sy, ("F", 0, "wfree", NSTR - 1, 16))
                        else:
                            em.wait(sy, ("F", 0, "wfree", NSTR - 1, 17))
                        em.inc("s_ld", sy.dma_start(
                            out=a1slot(orange % 4).rearrange("p (c o) -> p c o", o=512),
                            in_=d_qkvT[:, orange * 512:(orange + 1) * 512]
                            .rearrange("(c p) o -> p c o", p=P)), L("wl", s, orange), 16)
                        if orange >= 1:
                            a1_store(orange - 1)
                    a1_store(11)

            elif which == "tensor":
                t = eng
                em.wait(t, ("F", 0, "pfree", NSTR - 1, 31))
                em.wait(t, ("F", 0, "silu", NSTR - 1, 255))
                em.wait(t, ("F", 0, "xtc", NSTR - 1, NCH * DC - 1))
                em.wait(t, ("F", 0, "rtfree", NSTR - 1, NCH - 1))
                em.wait(t, ("F", 0, "w8free", NSTR - 1, NCH - 1))
                em.wait(t, ("F", 0, "wbcc", NSTR - 1, NCH - 1, E - 1))
                for s in range(NSTR):
                    for tc in range(NCH):
                        em.wait(t, L("xln", s, tc))
                        if tc == 0 and s >= 1:
                            em.wait(t, L("qkout", *QK_TILES[s * 64 - 1]))
                        for dc in range(DC):
                            k = tc * DC + dc
                            kk = s * NCH * DC + k
                            if kk >= 2:
                                em.wait(t, L("xtc", (kk - 2) // (NCH * DC), (kk - 2) % (NCH * DC)))
                            else:
                                em.wait(t, ("F", 0, "xtc", NSTR - 1, NCH * DC - 2 + kk))
                            em.inc("s_pe2", t.transpose(
                                pbb(2 + k % 2)[:, 0:P],
                                xln_sb[:, (tc % 2) * D + dc * P:(tc % 2) * D + (dc + 1) * P],
                                ident[:, :]), L("tp", s, k))
                    qk_local = [(o_, oc_, th_) for o_ in range(8)
                                for oc_ in range(4) for th_ in range(2)]

                    def a1_stage1(tile):
                        orange, oc, th = tile
                        i = QK_IDX[(s, orange, oc, th)]
                        bank = i % 3
                        if oc == 0 and th == 0:
                            em.wait(t, L("wl", s, orange))
                        if i - s * 64 < 3 and bank == 2:
                            em.wait(t, L("xtc", s, NCH * DC - 2))
                        if i - s * 64 < 1:
                            em.wait(t, L("xtc", s, NCH * DC - 1))
                        if i >= 3:
                            em.wait(t, L("qb", *QK_TILES[i - 3]))
                        for dc in range(DC):
                            ins = t.matmul(
                                pbk[bank][:, :],
                                a1slot(orange % 4)[:, dc * 512 + oc * P:dc * 512 + (oc + 1) * P],
                                xt_sb[:, dc * STRIPE + th * 512:dc * STRIPE + (th + 1) * 512],
                                start=(dc == 0), stop=(dc == DC - 1))
                        em.inc("s_mm", ins, L("qk_mm", s, orange, oc, th))
                        if oc == 3 and th == 1:
                            em.inc("s_wfree", t.nop(), L("wfree", s, orange))

                    def a1_stage2(tile):
                        orange, oc, th = tile
                        i = QK_IDX[(s, orange, oc, th)]
                        em.wait(t, L("qbc", s, orange, oc, th))
                        if i - s * 64 < 1:
                            em.wait(t, L("xtc", s, NCH * DC - 1))
                        if i >= 1:
                            em.wait(t, L("qsin", *QK_TILES[i - 1]))
                        em.inc("s_pe2", t.matmul(pbk[3][:, :], pm_sb[:, :],
                                                 tmp_bf[:, 0:512], start=True, stop=True),
                               L("rot_mm", s, orange, oc, th))

                    def a1_stage3(tile):
                        orange, oc, th = tile
                        i = QK_IDX[(s, orange, oc, th)]
                        em.wait(t, L("qsq", s, orange, oc, th))
                        if i >= 1:
                            em.wait(t, L("rowb", *QK_TILES[i - 1]))
                        em.inc("s_pe2", t.matmul(pbk[4][0:1, :], ones_col[:, :],
                                                 sq_sb[:, 0:512], start=True, stop=True),
                               L("rms_mm", s, orange, oc, th))

                    def a1_stage4(tile):
                        orange, oc, th = tile
                        i = QK_IDX[(s, orange, oc, th)]
                        em.wait(t, L("rowb", s, orange, oc, th))
                        if i >= 1:
                            em.wait(t, L("qkout", *QK_TILES[i - 1]))
                        em.inc("s_pe2", t.matmul(pbk[5][:, :], ones_row[:, :],
                                                 rms_rowb[0:1, :], start=True, stop=True),
                               L("bc_mm", s, orange, oc, th))

                    for step in range(len(qk_local) + 3):
                        if 3 <= step and step - 3 < len(qk_local):
                            a1_stage4(qk_local[step - 3])
                        if 2 <= step and step - 2 < len(qk_local):
                            a1_stage3(qk_local[step - 2])
                        if 1 <= step and step - 1 < len(qk_local):
                            a1_stage2(qk_local[step - 1])
                        if step < len(qk_local):
                            a1_stage1(qk_local[step])
                    for orange in range(8, 12):
                        em.wait(t, L("wl", s, orange))
                        if True:
                            for tc in range(NCH):
                                iv = V_IDX[(s, orange, tc)]
                                bank = iv % 2 + 6
                                if iv >= 2:
                                    em.wait(t, L("vc", *V_TILES[iv - 2]))
                                elif fpv0 := (iv < 2):
                                    em.wait(t, ("F", 0, "wbcc", NSTR - 1, NCH - 1, E - 2 + iv))
                                for dc in range(DC):
                                    ins = t.matmul(
                                        pbk[bank][:, :],
                                        xt_sb[:, dc * STRIPE + tc * P:dc * STRIPE + (tc + 1) * P],
                                        a1slot(orange % 4)[:, dc * 512:(dc + 1) * 512],
                                        start=(dc == 0), stop=(dc == DC - 1))
                                em.inc("s_mm", ins, L("v_mm", s, orange, tc))
                                if tc == NCH - 1:
                                    em.inc("s_wfree", t.nop(), L("wfree", s, orange))

            elif which == "scalar":
                sc = eng
                for s in range(NSTR):
                    for tc in range(NCH):
                        g = s * NCH + tc
                        pv = tc % 2
                        em.wait(sc, L("xl", s, tc))
                        if s >= 1 and tc == 0:
                            em.wait(sc, L("rms_mm", s - 1, 7, 3, 1))
                        if g >= 1:
                            em.wait(sc, L("var", (g - 1) // NCH, (g - 1) % NCH))
                        else:
                            em.wait(sc, ("F", 0, "var", NSTR - 1, NCH - 1))
                        for q4 in range(4):
                            ins = sc.activation(
                                sq_sb[:, :],
                                xch[:, (tc % 2) * D + q4 * 512:(tc % 2) * D + (q4 + 1) * 512],
                                AF.Square, accum_out=red_sb[:, 8 + q4:9 + q4])
                        em.inc("s_act2", ins, L("sq", s, tc))
                        em.wait(sc, L("var", s, tc))
                        em.inc("s_act2", sc.activation(red_sb[:, 2 + pv:3 + pv], red_sb[:, pv:pv + 1],
                                                       AF.Ln, bias=f32(LN_EPS)), L("lnv", s, tc))
                        em.wait(sc, L("lnv", s, tc))
                        em.inc("s_act2", sc.activation(stats[:, 2 * tc + 1:2 * tc + 2],
                                                       red_sb[:, 2 + pv:3 + pv], AF.Exp, scale=f32(-0.5)),
                               L("istd", s, tc))
                    qk_local = [(o_, oc_, th_) for o_ in range(8)
                                for oc_ in range(4) for th_ in range(2)]

                    def a1s_front(tile):
                        orange, oc, th = tile
                        i = QK_IDX[(s, orange, oc, th)]
                        em.wait(sc, L("qrope", s, orange, oc, th))
                        if i >= 1:
                            em.wait(sc, L("rms_mm", *QK_TILES[i - 1]))
                        qsrc = tmp_bf[:, 512:1024] if i % 2 == 0 else qp2_sb[:, :]
                        em.inc("s_act2", sc.activation(sq_sb[:, 0:512], qsrc,
                                                       AF.Square), L("qsq", s, orange, oc, th))

                    def a1s_back(tile):
                        orange, oc, th = tile
                        o_abs = orange * 512 + oc * P
                        is_q = o_abs < D
                        i = QK_IDX[(s, orange, oc, th)]
                        em.wait(sc, L("rms_mm", s, orange, oc, th))
                        if i >= 1:
                            em.wait(sc, L("rowb", *QK_TILES[i - 1]))
                        em.inc("s_act2", sc.activation(
                            pbk[4][0:1, :], pbk[4][0:1, :], AF.Ln,
                            scale=f32(1.0 / P), bias=f32(RMS_EPS_QK)),
                            L("lnr", s, orange, oc, th))
                        em.wait(sc, L("lnr", s, orange, oc, th))
                        em.inc("s_act2", sc.activation(
                            pbk[4][0:1, :], pbk[4][0:1, :], AF.Exp, scale=f32(-0.5),
                            bias=f32(-HALF_LN128) if is_q else f32(0.0)),
                            L("expr", s, orange, oc, th))

                    for step in range(len(qk_local) + 1):
                        if step >= 1:
                            a1s_back(qk_local[step - 1])
                        if step < len(qk_local):
                            a1s_front(qk_local[step])

            elif which == "vector":
                v = eng
                em.wait(v, L("cs"))
                for s in range(NSTR):
                    for tc in range(NCH):
                        slot = (tc % 2) * D
                        g = s * NCH + tc
                        pv = tc % 2
                        em.wait(v, L("xl", s, tc))
                        em.inc("s_ve2", v.reduce_sum(red_sb[:, 7:8], xch[:, slot:slot + D],
                                                     axis=AX, negate=True), L("nsum", s, tc))
                        em.wait(v, L("nsum", s, tc))
                        em.inc("s_ve2", v.tensor_scalar(stats[:, 2 * tc:2 * tc + 1],
                                                        red_sb[:, 7:8], 1.0 / D, None, ALU.mult),
                               L("nmean", s, tc))
                        em.wait(v, L("sq", s, tc))
                        em.wait(v, L("nmean", s, tc))
                        v.tensor_add(red_sb[:, 8:9], red_sb[:, 8:9], red_sb[:, 9:10])
                        em.inc("s_ve2", v.tensor_add(red_sb[:, 10:11], red_sb[:, 10:11],
                                                     red_sb[:, 11:12]), L("p2", s, tc))
                        em.wait(v, L("p2", s, tc))
                        em.inc("s_ve2", v.tensor_add(red_sb[:, 8:9], red_sb[:, 8:9],
                                                     red_sb[:, 10:11]), L("p4", s, tc))
                        if g >= 2:
                            em.wait(v, L("lnv", (g - 2) // NCH, (g - 2) % NCH))
                        else:
                            em.wait(v, ("F", 0, "lnv", NSTR - 1, NCH - 2 + g))
                        em.inc("s_ve2", v.tensor_tensor(red_sb[:, pv:pv + 1], stats[:, 2 * tc:2 * tc + 1],
                                                        stats[:, 2 * tc:2 * tc + 1], ALU.mult),
                               L("m2done", s, tc))
                        em.wait(v, L("p4", s, tc))
                        em.wait(v, L("m2done", s, tc))
                        em.inc("s_ve2", v.scalar_tensor_tensor(
                            red_sb[:, pv:pv + 1], red_sb[:, 8:9], 1.0 / D,
                            red_sb[:, pv:pv + 1], ALU.mult, ALU.subtract), L("var", s, tc))
                        em.wait(v, L("istd", s, tc))
                        if g >= 2:
                            em.wait(v, L("tp", (g - 2) // NCH, ((g - 2) % NCH) * DC + DC - 1))
                        else:
                            em.wait(v, ("F", 0, "tp", NSTR - 1, (NCH - 2 + g) * DC + DC - 1))
                        em.inc("s_ve2", v.tensor_scalar(
                            xln_sb[:, slot:slot + D], xch[:, slot:slot + D],
                            stats[:, 2 * tc:2 * tc + 1], stats[:, 2 * tc + 1:2 * tc + 2],
                            ALU.add, ALU.mult), L("xln", s, tc))
                        em.inc("s_xfa", v.engine_nop(), L("xfa", s, tc))
                        for dc in range(DC):
                            k = tc * DC + dc
                            em.wait(v, L("tp", s, k))
                            em.inc("s_ve", v.tensor_copy(
                                xt_sb[:, dc * STRIPE + tc * P:dc * STRIPE + (tc + 1) * P],
                                pbb(2 + k % 2)[:, 0:P]), L("xtc", s, k))
                    qk_local = [(o_, oc_, th_) for o_ in range(8)
                                for oc_ in range(4) for th_ in range(2)]

                    def qp(i):
                        return tmp_bf[:, 512:1024] if i % 2 == 0 else qp2_sb[:, :]

                    def a1v_front(tile):
                        orange, oc, th = tile
                        tok0 = s * STRIPE + th * 512
                        i = QK_IDX[(s, orange, oc, th)]
                        bank = i % 3
                        em.wait(v, L("qk_mm", s, orange, oc, th))
                        if i >= 1:
                            em.wait(v, L("rot_mm", *QK_TILES[i - 1]))
                        em.inc("s_ve", v.tensor_copy(tmp_bf[:, 0:512], pbk[bank][:, :]),
                               L("qb", s, orange, oc, th))
                        em.inc("s_ve", v.engine_nop(), L("qbc", s, orange, oc, th))
                        em.wait(v, L("qbc", s, orange, oc, th))
                        if i >= 2:
                            em.wait(v, L("qsq", *QK_TILES[i - 2]))
                        em.inc("s_ve", v.tensor_tensor(
                            qp(i), tmp_bf[:, 0:512],
                            wbc_sb[:, COS_OFF + tok0:COS_OFF + tok0 + 512], ALU.mult),
                            L("qcos", s, orange, oc, th))
                        em.wait(v, L("rot_mm", s, orange, oc, th))
                        em.inc("s_ve", v.tensor_tensor(
                            tmp_bf[:, 1024:1536], pbk[3][:, :],
                            wbc_sb[:, SIN_OFF + tok0:SIN_OFF + tok0 + 512], ALU.mult),
                            L("qsin", s, orange, oc, th))
                        em.wait(v, L("qcos", s, orange, oc, th))
                        em.wait(v, L("qsin", s, orange, oc, th))
                        em.inc("s_ve", v.tensor_add(
                            qp(i), qp(i),
                            tmp_bf[:, 1024:1536]), L("qrope", s, orange, oc, th))

                    def a1v_back(tile):
                        orange, oc, th = tile
                        i = QK_IDX[(s, orange, oc, th)]
                        em.wait(v, L("expr", s, orange, oc, th))
                        if i >= 1:
                            em.wait(v, L("bc_mm", *QK_TILES[i - 1]))
                        em.inc("s_ve", v.tensor_copy(rms_rowb[0:1, :], pbk[4][0:1, :]),
                               L("rowb", s, orange, oc, th))
                        em.wait(v, L("bc_mm", s, orange, oc, th))
                        if i == 1:
                            em.wait(v, ("qkvT_st", 191))
                        if i >= 2:
                            em.wait(v, L("qkst", *QK_TILES[i - 2]))
                        dstq = tmp_bf[:, 1536:2048] if i % 2 == 0 else qo2_sb[:, :]
                        em.inc("s_ve", v.tensor_tensor(
                            dstq, qp(i), pbk[5][:, :], ALU.mult),
                            L("qkout", s, orange, oc, th))

                    for step in range(len(qk_local) + 2):
                        if step >= 2:
                            a1v_back(qk_local[step - 2])
                        if step < len(qk_local):
                            a1v_front(qk_local[step])
                    for orange in range(8, 12):
                        if True:
                            for tc in range(NCH):
                                iv = V_IDX[(s, orange, tc)]
                                bank = iv % 2 + 6
                                em.wait(v, L("v_mm", s, orange, tc))
                                if iv >= 2:
                                    em.wait(v, L("vst", *V_TILES[iv - 2]))
                                em.inc("s_ve", v.tensor_copy(
                                    tmp_bf[:, (tc % 2) * 512:(tc % 2 + 1) * 512], pbk[bank][:, :]),
                                    L("vout", s, orange, tc))
                                em.inc("s_ve", v.engine_nop(), L("vc", s, orange, tc))

        def phase_A2(which, eng):
            L = lambda *a: ("A2",) + a

            ST = []
            SIDX0 = {}
            for kcb in range(DC):
                nq = T - kcb * P
                SIDX0[kcb] = len(ST)
                for t2 in range((nq + 1023) // 1024):
                    ST.append((kcb, t2, min(1024, nq - t2 * 1024)))
            NST = len(ST)

            def qtbuf(b):
                return xt_sb[:, b * 2048:(b + 1) * 2048]

            def ktbuf(b):
                return xt_sb[:, 4096 + b * 2048:4096 + (b + 1) * 2048]

            def vbuf(b):
                return comb_sb[:, b * 2112:(b + 1) * 2112]

            def stbuf(sidx):
                return pb2[0] if sidx % 2 == 0 else pb2[3]

            if which == "sync":
                sy = eng
                for sh in range(NSH):
                    s, hh = sh // H, sh % H
                    b = sh % 2
                    if sh >= 2:
                        em.wait(sy, L("shdone", sh - 2))
                    else:
                        em.wait(sy, L("vones", b))
                        em.wait(sy, ("A1", "v_mm", NSTR - 1, 11, NCH - 1))
                        em.wait(sy, ("A1", "qkst", 2 * s + 1, 7, 3, 1))
                        em.wait(sy, ("A1", "vst", 2 * s + 1, 11, NCH - 1))
                    em.inc("s_xl", sy.dma_start(
                        out=qtbuf(b), in_=d_qt[hh * P:(hh + 1) * P, s * T:(s + 1) * T]), None, 16)
                    em.inc("s_xl", sy.dma_start(
                        out=ktbuf(b), in_=d_kt[hh * P:(hh + 1) * P, s * T:(s + 1) * T]), None, 16)
                    em.inc("s_xl", sy.dma_start(
                        out=vbuf(b).rearrange("p (kc c) -> p kc c", c=132)[:, :, 0:P],
                        in_=d_v[s * T:(s + 1) * T, hh * P:(hh + 1) * P]
                        .rearrange("(kc p) c -> p kc c", p=P)), L("qkv", sh), 16)

            elif which == "tensor":
                t = eng
                em.wait(t, ("A1", "qb", NSTR - 1, 7, 3, 1))
                em.wait(t, ("A1", "qsin", NSTR - 1, 7, 3, 1))
                em.wait(t, ("A1", "rowb", NSTR - 1, 7, 3, 1))
                em.wait(t, ("A1", "qkout", NSTR - 1, 7, 3, 1))
                em.wait(t, ("A1", "vc", NSTR - 1, 11, NCH - 1))
                for sh in range(NSH):
                    b = sh % 2
                    em.wait(t, L("qkv", sh))
                    for step in range(DC + 4):
                        if step < DC:
                            kcb = step
                            q0 = kcb * P
                            for t2 in range((T - q0 + 1023) // 1024):
                                sidx = SIDX0[kcb] + t2
                                w2 = ST[sidx][2]
                                gsid = sh * NST + sidx
                                if gsid >= 2:
                                    gp = gsid - 2
                                    em.wait(t, L("exp", gp // NST, gp % NST))
                                nsub = (w2 + 511) // 512
                                for sub in range(nsub):
                                    ws = min(512, w2 - sub * 512)
                                    ins = t.matmul(
                                        stbuf(sidx)[:, sub * 512:sub * 512 + ws],
                                        ktbuf(b)[:, kcb * P:(kcb + 1) * P],
                                        qtbuf(b)[:, q0 + t2 * 1024 + sub * 512:
                                                 q0 + t2 * 1024 + sub * 512 + ws],
                                        start=True, stop=True)
                                em.inc("s_mm", ins, L("sc_mm", sh, sidx))
                        if step >= 2 and step - 2 < DC:
                            j = step - 2
                            avoff = (j % 2) * 512
                            em.wait(t, L("exp", sh, SIDX0[j]))
                            gav = sh * DC + j
                            if gav >= 2:
                                gp = gav - 2
                                em.wait(t, L("avdone", gp // DC, gp % DC))
                            for kcb2 in range(j + 1):
                                ins = t.matmul(
                                    pb2[1][:, avoff:avoff + 129],
                                    wbuf[:, kcb2 * T + j * P:kcb2 * T + (j + 1) * P],
                                    vbuf(b)[:, kcb2 * 132:kcb2 * 132 + 129],
                                    start=(kcb2 == 0), stop=(kcb2 == j))
                            em.inc("s_mm", ins, L("av_mm", sh, j))
                        if step >= 4 and step - 4 < DC:
                            j2 = step - 4
                            em.wait(t, L("avsc", sh, j2))
                            gav = sh * DC + j2
                            if gav >= 2:
                                gp = gav - 2
                                em.wait(t, L("avst", gp // DC, gp % DC))
                            em.inc("s_pe2", t.transpose(
                                pb2[2].bitcast(BF16)[:, (j2 % 2) * 1024:(j2 % 2) * 1024 + P],
                                qo2_sb[:, (j2 % 2) * P:(j2 % 2 + 1) * P],
                                ident[:, :]), L("avtp", sh, j2))

            elif which == "scalar":
                sc = eng
                em.wait(sc, ("A1", "qkst", NSTR - 1, 7, 3, 1))
                def at_store(sh_, j2_):
                    s2, hh2 = sh_ // H, sh_ % H
                    em.wait(sc, L("avst", sh_, j2_))
                    em.inc("s_ast", sc.dma_start(
                        out=d_at[hh2 * P:(hh2 + 1) * P,
                                 s2 * T + j2_ * P:s2 * T + (j2_ + 1) * P],
                        in_=tmp_bf[:, j2_ * P:(j2_ + 1) * P]),
                        L("atst", sh_, j2_), 16)

                for sh in range(NSH):
                    for sidx in range(NST):
                        kcb, t2, w2 = ST[sidx]
                        q0 = kcb * P
                        if sh >= 1 and 4 <= sidx < 4 + DC:
                            at_store(sh - 1, sidx - 4)
                        em.wait(sc, L("sc_mm", sh, sidx))
                        if t2 == 0:
                            em.wait(sc, L("mask", sh, kcb))
                        if sh >= 1:
                            j_last = (kcb * P + t2 * 1024 + w2 - 1) // P
                            em.wait(sc, L("avdone", sh - 1, min(j_last, DC - 1)))
                        em.inc("s_act", sc.activation(
                            wbuf[:, kcb * T + q0 + t2 * 1024:kcb * T + q0 + t2 * 1024 + w2],
                            stbuf(sidx)[:, 0:w2], AF.Exp, bias=f32(CSHIFT)),
                            L("exp", sh, sidx))
                for j2_ in range(DC):
                    at_store(NSH - 1, j2_)

            elif which == "vector":
                v = eng
                for b in range(2):
                    em.wait(v, ("F", 0, "proj_mm", NSTR - 1, 31))
                    em.inc("s_ve2", v.memset(vbuf(b)[:, :], 1.0), L("vones", b))
                em.wait(v, ("A1", "vst", NSTR - 1, 11, NCH - 1))
                em.wait(v, ("A1", "qkst", NSTR - 1, 7, 3, 1))
                for sh in range(NSH):
                    b = sh % 2
                    for step in range(DC + 4):
                        if step < DC:
                            kcb = step
                            sidx = SIDX0[kcb]
                            em.wait(v, L("sc_mm", sh, sidx))
                            em.inc("s_ve", v.tensor_add(stbuf(sidx)[:, 0:P], stbuf(sidx)[:, 0:P],
                                                        maskt_sb[:, :]), L("mask", sh, kcb))
                        if step >= 2 and step - 2 < DC:
                            j = step - 2
                            avoff = (j % 2) * 512
                            em.wait(v, L("av_mm", sh, j))
                            em.inc("s_ve", v.reciprocal(rinv_sb[:, j % 2:j % 2 + 1],
                                                        pb2[1][:, avoff + 128:avoff + 129]),
                                   L("rinv", sh, j))
                            em.wait(v, L("rinv", sh, j))
                            gav = sh * DC + j
                            if gav >= 2:
                                gp = gav - 2
                                em.wait(v, L("avtp", gp // DC, gp % DC))
                            em.inc("s_ve", v.tensor_scalar(
                                qo2_sb[:, (j % 2) * P:(j % 2 + 1) * P],
                                pb2[1][:, avoff:avoff + P], rinv_sb[:, j % 2:j % 2 + 1], None,
                                ALU.mult), L("avsc", sh, j))
                            em.inc("s_ve", v.engine_nop(), L("avdone", sh, j))
                        if step >= 4 and step - 4 < DC:
                            j2 = step - 4
                            em.wait(v, L("avtp", sh, j2))
                            if sh >= 1:
                                em.wait(v, L("atst", sh - 1, j2))
                            em.inc("s_ve", v.tensor_copy(
                                tmp_bf[:, j2 * P:(j2 + 1) * P],
                                pb2[2].bitcast(BF16)[:, (j2 % 2) * 1024:(j2 % 2) * 1024 + P]),
                                L("avst", sh, j2))
                            em.inc("s_ve", v.engine_nop(), L("avtpc", sh, j2))
                    em.inc("s_ve", v.engine_nop(), L("shdone", sh))

        def phase_A3(which, eng, xo):
            L = lambda *a: ("A3",) + a

            if which == "sync":
                sy = eng
                em.wait(sy, ("cast_wo",))
                em.wait(sy, ("A2", "shdone", NSH - 1))
                n = 0
                for oc in range(DC):
                    for cr in range(4):
                        em.inc("s_ld", sy.dma_start_transpose(
                            wbuf[:, oc * 2048 + cr * 512:oc * 2048 + (cr + 1) * 512],
                            d_wob[cr * 512:(cr + 1) * 512, oc * P:(oc + 1) * P]),
                            L("wo", n), 16)
                        n += 1
                for tc in range(NTC):
                    if tc >= 2:
                        em.wait(sy, L("atfree", tc - 2))
                    else:
                        em.wait(sy, ("A2", "av_mm", NSH - 1, DC - 1))
                    sh_last = (tc * P) // T * H + H - 1
                    em.wait(sy, ("A2", "atst", sh_last, tc % (T // P)))
                    em.inc("s_xl", sy.dma_start(
                        out=xt_sb[:, (tc % 2) * 2048:(tc % 2 + 1) * 2048]
                        .rearrange("p (c t) -> p c t", t=P),
                        in_=d_at[:, tc * P:(tc + 1) * P].rearrange("(c p) t -> p c t", p=P)),
                        L("at", tc), 16)
                    if tc >= 2:
                        em.wait(sy, L("x1free", tc - 2))
                    else:
                        em.wait(sy, ("A1", "xfa", NSTR - 1, NCH - 2 + tc))
                        em.wait(sy, ("A1", "sq", NSTR - 1, NCH - 2 + tc))
                    em.inc("s_xl", sy.dma_start(
                        out=xch[:, (tc % 2) * D:(tc % 2 + 1) * D],
                        in_=d_x1[tc * P:(tc + 1) * P, :]), L("x1", tc), 16)
                    em.wait(sy, L("x2done", tc))
                    em.inc("s_st", sy.dma_start(
                        out=xo[tc * P:(tc + 1) * P, :], in_=y_sb[:, :]), L("st", tc), 16)

            elif which == "tensor":
                t = eng
                for n in range(64):
                    em.wait(t, L("wo", n))
                em.wait(t, ("A2", "avtpc", NSH - 1, DC - 1))
                for tc in range(NTC):
                    em.wait(t, L("at", tc))
                    for foq in range(4):
                        jj = tc * 4 + foq
                        bank = 4 + jj % 4
                        if jj >= 4:
                            em.wait(t, L("pfree", jj - 4))
                        for oc in range(DC):
                            ins = t.matmul(
                                pbk[bank][:, :],
                                xt_sb[:, (tc % 2) * 2048 + oc * P:(tc % 2) * 2048 + (oc + 1) * P],
                                wbuf[:, oc * 2048 + foq * 512:oc * 2048 + (foq + 1) * 512],
                                start=(oc == 0), stop=(oc == DC - 1))
                        em.inc("s_mm", ins, L("mm", tc, foq))
                        if foq == 3:
                            em.inc("s_pe2", t.nop(), L("atfree", tc))

            elif which == "vector":
                v = eng
                for tc in range(NTC):
                    em.wait(v, L("x1", tc))
                    if tc >= 1:
                        em.wait(v, L("st", tc - 1))
                    else:
                        em.wait(v, ("F", 0, "st", NSTR - 1, NCH - 1))
                    for foq in range(4):
                        em.wait(v, L("mm", tc, foq))
                        em.inc("s_ve2", v.tensor_add(
                            y_sb[:, foq * 512:(foq + 1) * 512],
                            xch[:, (tc % 2) * D + foq * 512:(tc % 2) * D + (foq + 1) * 512],
                            pbk[4 + (tc * 4 + foq) % 4][:, :]), L("pfree", tc * 4 + foq))
                    em.inc("s_ve2", v.engine_nop(), L("x2done", tc))
                    em.inc("s_ve2", v.engine_nop(), L("x1free", tc))
                em.inc("s_ve2", eng.engine_nop(), ("A3", "done_ve"))

        def phase_final(which, eng):
            if which != "sync" or em.mode != "emit":
                return
            for sn in SEM_NAMES:
                cnt = em.cnt.get(sn, 0)
                if cnt > 0:
                    eng.wait_ge(em.sems[sn].h, cnt)

        def run_phases(which, eng):
            phase_prep(which, eng)
            if upto.startswith("dbg_"):
                m = upto[4:]
                ns = NSTR
                if "_s" in m:
                    m, _, nss = m.partition("_s")
                    ns = int(nss)
                pmap = {"ln": (), "tp": ("tp",), "rt": ("tp", "rt"),
                        "w8": ("tp", "rt", "w8c"), "wbc": ("tp", "rt", "w8c", "wbc"),
                        "pro": ("pro",), "exp": ("pro", "exp")}
                phase_F(which, eng, 0, d_x, d_out, parts=pmap[m], nstr=ns)
                return
            if upto == "f1":
                phase_F(which, eng, 0, d_x, d_out)
                return
            phase_F(which, eng, 0, d_x, d_x1,
                    sync_hook=(lambda: phase_qkvT(which, eng)) if which == "sync" else None)
            phase_A1(which, eng)
            phase_A2(which, eng)
            if upto == "a3":
                phase_A3(which, eng, d_out)
                return
            phase_A3(which, eng, d_x2)
            phase_F(which, eng, 1, d_x2, d_out)

        em.mode = "plan"
        fake = _FakeEng()
        for which in ENGINES:
            run_phases(which, fake)
        total_plan = dict(em.cnt)

        em.mode = "emit"
        em.cnt = total_plan
        global _LAST_EM
        _LAST_EM = em

        block = ctx.enter_context(nc.Block())

        @block.sync
        def _(sy):
            run_phases("sync", sy)
            phase_final("sync", sy)

        @block.gpsimd
        def _(g):
            run_phases("gpsimd", g)

        @block.tensor
        def _(t):
            run_phases("tensor", t)

        @block.scalar
        def _(sc):
            run_phases("scalar", sc)

        @block.vector
        def _(v):
            run_phases("vector", v)

    return nc



def _host_constants():
    ident = np.eye(P, dtype=bf16)
    pm = np.zeros((P, P), dtype=np.float32)
    for i in range(P // 2):
        pm[2 * i + 1, 2 * i] = -1.0
        pm[2 * i, 2 * i + 1] = 1.0
    pm = pm.astype(bf16)
    inv_freq = (1.0 / 10000.0 ** (np.arange(0, P, 2, dtype=np.float64) / P))
    pos = (np.arange(TOK) % T).astype(np.float64)
    ang = np.repeat(inv_freq, 2)[:, None] * pos[None, :]
    cos = np.cos(ang).astype(np.float32).astype(bf16)
    sin = np.sin(ang).astype(np.float32).astype(bf16)
    kl = np.arange(P)[:, None]
    ql = np.arange(P)[None, :]
    maskt = np.where(kl <= ql, 0.0, NEG).astype(np.float32)
    return ident, pm, cos, sin, maskt


_UPTO = "full"
_TAP = None
_LAST_EM = None


def kernel(L0_attn_x, L0_attn_W_qkv, L0_attn_W_o, L0_ffn_router_w, L0_ffn_router_b,
           L0_ffn_expert_w, L0_ffn_expert_b, L0_ffn_proj_w, L0_ffn_proj_b, L0_ffn_norm_w):
    x = np.ascontiguousarray(np.asarray(L0_attn_x, dtype=np.float32))
    wqkv = np.ascontiguousarray(np.asarray(L0_attn_W_qkv, dtype=np.float32))
    wo = np.ascontiguousarray(np.asarray(L0_attn_W_o, dtype=np.float32))
    rw = np.ascontiguousarray(np.asarray(L0_ffn_router_w, dtype=np.float32))
    rb = np.ascontiguousarray(np.asarray(L0_ffn_router_b, dtype=np.float32)).reshape(1, E)
    ew = np.ascontiguousarray(np.asarray(L0_ffn_expert_w, dtype=np.float32))
    eb = np.ascontiguousarray(np.asarray(L0_ffn_expert_b, dtype=np.float32))
    pw = np.ascontiguousarray(np.asarray(L0_ffn_proj_w, dtype=np.float32))
    pb = np.ascontiguousarray(np.asarray(L0_ffn_proj_b, dtype=np.float32)).reshape(1, D)
    nw = np.ascontiguousarray(np.asarray(L0_ffn_norm_w, dtype=np.float32)).reshape(1, D)

    ebl = np.ascontiguousarray(eb.reshape(E, DC, P).transpose(2, 1, 0).reshape(P, DC * E))
    ident, pm, cos, sin, maskt = _host_constants()

    nc = build_program(_UPTO, _TAP)
    in_maps = []
    for c in range(8):
        xs = np.ascontiguousarray(x[SEQ * c:SEQ * (c + 1)].reshape(TOK, D))
        in_maps.append({
            "x_in": xs, "w_qkv": wqkv, "w_o": wo, "router_w": rw, "router_b": rb,
            "expert_w": ew, "expert_b_laid": ebl, "proj_w": pw, "proj_b": pb,
            "norm_w": nw, "c_ident": ident, "c_rope_pm": pm, "c_cos": cos,
            "c_sin": sin, "c_maskt": maskt,
        })
    res = run_bass_kernel_spmd(nc, in_maps, core_ids=list(range(8)))
    out = np.stack([np.asarray(r["x_out"]).reshape(SEQ, T, D) for r in res.results], axis=0)
    return out.reshape(16, T, D)


# version 17
# speedup vs baseline: 1.0883x; 1.0617x over previous
import numpy as np
from contextlib import ExitStack

import concourse.bass as bass
import concourse.mybir as mybir
from concourse.bass_utils import run_bass_kernel_spmd
import ml_dtypes

F32 = mybir.dt.float32
BF16 = mybir.dt.bfloat16
AF = mybir.ActivationFunctionType
ALU = mybir.AluOpType
AX = mybir.AxisListType.X
f32 = np.float32
bf16 = ml_dtypes.bfloat16

P = 128
D = 2048
DC = D // P
E = 8
H = 16
T = 2048
SEQ = 2
TOK = SEQ * T
STRIPE = 1024
NSTR = TOK // STRIPE
NCH = STRIPE // P
NTC = TOK // P
OQKV = 3 * D
NSH = SEQ * H
LN_EPS = 1e-5
RMS_EPS_QK = 1.1920929e-07
RMS_EPS_FFN = 1e-6
CSHIFT = -12.0
NEG = -1e9
HALF_LN128 = 0.5 * float(np.log(128.0))

ENGINES = ("sync", "gpsimd", "tensor", "scalar", "vector")


class _FakeInstr:
    def then_inc(self, *a, **k):
        return self


class _FakeEng:
    def __getattr__(self, name):
        return lambda *a, **k: _FakeInstr()


class EM:
    def __init__(self):
        self.mode = "plan"
        self.cnt = {}
        self.plan = {}
        self.sems = {}

    def inc(self, sem, instr, label=None, k=1):
        if self.mode == "plan":
            self.cnt[sem] = self.cnt.get(sem, 0) + k
            if label is not None:
                assert label not in self.plan, f"dup label {label}"
                self.plan[label] = (sem, self.cnt[sem])
        else:
            s = self.sems[sem]
            instr.then_inc(s.h, k)
            s.n += k
            if label is not None:
                assert self.plan[label] == (sem, s.n), (label, self.plan[label], sem, s.n)

    def wait(self, eng, label):
        if self.mode == "emit":
            sem, cnt = self.plan[label]
            eng.wait_ge(self.sems[sem].h, cnt)


class Sem:
    def __init__(self, ctx, nc, name):
        self.h = ctx.enter_context(nc.semaphore(name))
        self.n = 0


def build_program(upto="full", tap=None):
    nc = bass.Bass("TRN2", target_bir_lowering=False)

    d_x = nc.dram_tensor("x_in", [TOK, D], F32, kind="ExternalInput").ap()
    d_wqkv = nc.dram_tensor("w_qkv", [OQKV, D], F32, kind="ExternalInput").ap()
    d_wo = nc.dram_tensor("w_o", [D, D], F32, kind="ExternalInput").ap()
    d_rw = nc.dram_tensor("router_w", [D, E], F32, kind="ExternalInput").ap()
    d_rb = nc.dram_tensor("router_b", [1, E], F32, kind="ExternalInput").ap()
    d_ew = nc.dram_tensor("expert_w", [E, D, D], F32, kind="ExternalInput").ap()
    d_ebl = nc.dram_tensor("expert_b_laid", [P, DC * E], F32, kind="ExternalInput").ap()
    d_pw = nc.dram_tensor("proj_w", [D, D], F32, kind="ExternalInput").ap()
    d_pb = nc.dram_tensor("proj_b", [1, D], F32, kind="ExternalInput").ap()
    d_nw = nc.dram_tensor("norm_w", [1, D], F32, kind="ExternalInput").ap()
    d_ident = nc.dram_tensor("c_ident", [P, P], BF16, kind="ExternalInput").ap()
    d_pm = nc.dram_tensor("c_rope_pm", [P, P], BF16, kind="ExternalInput").ap()
    d_cos = nc.dram_tensor("c_cos", [P, TOK], BF16, kind="ExternalInput").ap()
    d_sin = nc.dram_tensor("c_sin", [P, TOK], BF16, kind="ExternalInput").ap()
    d_maskt = nc.dram_tensor("c_maskt", [P, P], F32, kind="ExternalInput").ap()
    d_out = nc.dram_tensor("x_out", [TOK, D], F32, kind="ExternalOutput").ap()

    d_ewb = nc.dram_tensor("s_ewb", [E, D, D], BF16, kind="Internal").ap()
    d_pwb = nc.dram_tensor("s_pwb", [D, D], BF16, kind="Internal").ap()
    d_qkvb = nc.dram_tensor("s_qkvb", [OQKV, D], BF16, kind="Internal").ap()
    d_qkvT = nc.dram_tensor("s_qkvT", [D, OQKV], BF16, kind="Internal").ap()
    d_wob = nc.dram_tensor("s_wob", [D, D], BF16, kind="Internal").ap()
    d_x1 = nc.dram_tensor("s_x1", [TOK, D], F32, kind="Internal").ap()
    d_x2 = nc.dram_tensor("s_x2", [TOK, D], F32, kind="Internal").ap()
    d_qt = nc.dram_tensor("s_qt", [D, TOK], BF16, kind="Internal").ap()
    d_kt = nc.dram_tensor("s_kt", [D, TOK], BF16, kind="Internal").ap()
    d_v = nc.dram_tensor("s_v", [TOK, D], BF16, kind="Internal").ap()
    d_at = nc.dram_tensor("s_at", [D, TOK], BF16, kind="Internal").ap()

    ctx = ExitStack()
    with ctx:
        def sbuf(name, shape, dt=F32):
            return ctx.enter_context(nc.sbuf_tensor(name, shape, dt))

        xt_sb = sbuf("xt_sb", [P, DC * STRIPE], BF16)
        comb_sb = sbuf("comb_sb", [P, DC * STRIPE], BF16)
        wbuf = sbuf("wbuf", [P, 2 * DC * STRIPE], BF16)
        xch = sbuf("xch", [P, 3 * D])
        xln_sb = sbuf("xln_sb", [P, 2 * D], BF16)
        wbc_sb = sbuf("wbc_sb", [P, E * STRIPE], BF16)
        nwb_sb = sbuf("nwb_sb", [P, D], BF16)
        sq_sb = sbuf("sq_sb", [P, 512], BF16)
        y_sb = sbuf("y_sb", [P, D])
        tmp_bf = sbuf("tmp_bf", [P, 2048], BF16)
        qo2_sb = sbuf("qo2_sb", [P, 512], BF16)
        qp2_sb = sbuf("qp2_sb", [P, 512], BF16)
        ident = sbuf("ident", [P, P], BF16)
        pm_sb = sbuf("pm_sb", [P, P], BF16)
        maskt_sb = sbuf("maskt_sb", [P, P])
        ones_col = sbuf("ones_col", [P, 1], BF16)
        ones_row = sbuf("ones_row", [1, P], BF16)
        bias_sb = sbuf("bias_sb", [P, DC * E])
        rb_bc = sbuf("rb_bc", [P, E])
        rw_sb = sbuf("rw_sb", [P, DC * E], BF16)
        pbrow = sbuf("pbrow", [1, D], BF16)
        wrow_sb = sbuf("wrow_sb", [1, E * P], BF16)
        rms_rowb = sbuf("rms_rowb", [1, 512], BF16)
        stats = sbuf("stats", [P, 4 * NCH])
        logit_sb = sbuf("logit_sb", [P, E])
        m1_sb = sbuf("m1_sb", [P, 1])
        l2_sb = sbuf("l2_sb", [P, E])
        m2_sb = sbuf("m2_sb", [P, 1])
        nm1_sb = sbuf("nm1_sb", [P, 1])
        keep_sb = sbuf("keep_sb", [P, E])
        es_sb = sbuf("es_sb", [P, E])
        sk_sb = sbuf("sk_sb", [P, E])
        w8_sb = sbuf("w8_sb", [P, E], BF16)
        red_sb = sbuf("red_sb", [P, 16])
        rinv_sb = sbuf("rinv_sb", [P, 2])

        pb2 = [ctx.enter_context(nc.psum_tensor(f"pb2_{i}", [P, 1024], F32)) for i in range(4)]

        class _PB:
            def __getitem__(self, i):
                return pb2[i // 2][:, (i % 2) * 512:(i % 2 + 1) * 512]

        pbk = _PB()

        def pbb(i):
            return pb2[i // 2].bitcast(BF16)[:, (i % 2) * 1024:(i % 2 + 1) * 1024]

        em = EM()
        SEM_NAMES = ["s_cast", "s_ld", "s_xl", "s_st", "s_ast", "s_mm", "s_pe2", "s_wfree",
                     "s_act", "s_act2", "s_ve", "s_ve2", "s_pfree", "s_xfa", "s_xfb"]
        for sn in SEM_NAMES:
            em.sems[sn] = Sem(ctx, nc, sn)

        def wslot(j2):
            return wbuf[:, j2 * 16384:(j2 + 1) * 16384]

        def a1slot(j4):
            return wbuf[:, j4 * 8192:(j4 + 1) * 8192]

        COS_OFF = 0
        SIN_OFF = TOK

        QK_TILES = []
        V_TILES = []
        for _s in range(NSTR):
            for _o in range(12):
                if _o < 8:
                    for _oc in range(4):
                        for _th in range(2):
                            QK_TILES.append((_s, _o, _oc, _th))
                else:
                    for _tc in range(NCH):
                        V_TILES.append((_s, _o, _tc))
        QK_IDX = {t: i for i, t in enumerate(QK_TILES)}
        V_IDX = {t: i for i, t in enumerate(V_TILES)}

        def phase_prep(which, eng):
            if which != "gpsimd":
                return
            g = eng
            em.inc("s_cast", g.dma_start(
                out=rw_sb[:, :].rearrange("p (c e) -> p c e", e=E),
                in_=d_rw.rearrange("(c p) e -> p c e", p=P)), None, 16)
            em.inc("s_cast", g.dma_start(out=pbrow[:, :], in_=d_pb), None, 16)
            em.inc("s_cast", g.dma_start(
                out=nwb_sb[:, :], in_=d_nw[:, :].partition_broadcast(P)[:, 0, :]), None, 16)
            g.memset(ones_col[:, :], 1.0)
            g.memset(ones_row[:, :], 1.0)
            em.inc("s_cast", g.dma_start(out=ident[:, :], in_=d_ident), None, 16)
            em.inc("s_cast", g.dma_start(out=pm_sb[:, :], in_=d_pm), None, 16)
            em.inc("s_cast", g.dma_start(out=maskt_sb[:, :], in_=d_maskt), None, 16)
            em.inc("s_cast", g.dma_start(out=bias_sb[:, :], in_=d_ebl), None, 16)
            em.inc("s_cast", g.dma_start(
                out=rb_bc[:, :], in_=d_rb[:, :].partition_broadcast(P)[:, 0, :]),
                ("prep_done",), 16)
            em.inc("s_cast", g.dma_start(out=d_qkvb, in_=d_wqkv), ("cast_qkv",), 16)
            em.inc("s_cast", g.dma_start(out=d_wob, in_=d_wo), ("cast_wo",), 16)

        def phase_F(which, eng, fp, xin, xout, parts=("pro", "exp", "proj"), nstr=NSTR,
                    sync_hook=None):
            L = lambda *a: ("F", fp) + a
            NSLOT = 18

            def SO(s):
                return (s % 2) * 2 * NCH

            if which == "gpsimd":
                g_ = eng
                for s_ in range(nstr):
                    for e_ in range(E):
                        for fh_ in range(2):
                            j_ = e_ * 2 + fh_
                            prev_ = s_ * NSLOT + j_ - 2
                            if prev_ >= 0:
                                em.wait(g_, L("wfree", prev_ // NSLOT, prev_ % NSLOT))
                            elif fp > 0:
                                em.wait(g_, ("A3", "mm", NTC - 1, 3))
                            em.inc("s_cast", g_.dma_start(
                                out=wslot(j_ % 2).rearrange("p (c f) -> p c f", f=1024),
                                in_=d_ew[e_][:, fh_ * 1024:(fh_ + 1) * 1024]
                                .rearrange("(c p) f -> p c f", p=P)), L("wl", s_, j_), 16)
                    for fh_ in range(2):
                        j_ = 16 + fh_
                        prev_ = s_ * NSLOT + j_ - 2
                        em.wait(g_, L("wfree", prev_ // NSLOT, prev_ % NSLOT))
                        em.inc("s_cast", g_.dma_start(
                            out=wslot(j_ % 2).rearrange("p (c f) -> p c f", f=1024),
                            in_=d_pw[:, fh_ * 1024:(fh_ + 1) * 1024]
                            .rearrange("(c p) f -> p c f", p=P)), L("wl", s_, j_), 16)
                if tap == "comb" and fp == 0:
                    em.wait(g_, ("F", 0, "comb", 0, 255))
                    em.wait(g_, ("F", 0, "st", 0, NCH - 1))
                    for r_ in range(8):
                        em.inc("s_cast", g_.dma_start(
                            out=xout[r_ * P:(r_ + 1) * P, :],
                            in_=comb_sb[:, r_ * 2048:(r_ + 1) * 2048]), ("tapc", r_), 16)
                return

            if which == "sync":
                sy = eng

                def sy_xl(s):
                    for tc in range(NCH):
                        g = s * NCH + tc
                        if g >= 2:
                            em.wait(sy, L("xfa", (g - 2) // NCH, (g - 2) % NCH))
                            em.wait(sy, L("sq", (g - 2) // NCH, (g - 2) % NCH))
                        elif fp > 0:
                            em.wait(sy, ("A3", "x1free", NTC - 1))
                        em.inc("s_xl", sy.dma_start(
                            out=xch[:, (tc % 2) * D:(tc % 2 + 1) * D],
                            in_=xin[(s * NCH + tc) * P:(s * NCH + tc + 1) * P, :]),
                            L("xl", s, tc), 16)

                def sy_ep(s):
                    for tc in range(NCH):
                        g = s * NCH + tc
                        if g >= 1:
                            em.wait(sy, L("ep_done", (g - 1) // NCH, (g - 1) % NCH))
                        elif fp > 0:
                            em.wait(sy, ("F", 0, "ep_done", NSTR - 1, NCH - 1))
                        em.inc("s_xl", sy.dma_start(
                            out=xch[:, 2 * D:3 * D],
                            in_=xin[(s * NCH + tc) * P:(s * NCH + tc + 1) * P, :]),
                            L("xe", s, tc), 16)
                        em.wait(sy, L("ep_done", s, tc))
                        em.inc("s_st", sy.dma_start(
                            out=xout[(s * NCH + tc) * P:(s * NCH + tc + 1) * P, :],
                            in_=y_sb[:, :]), L("st", s, tc), 16)

                for s in range(nstr):
                    sy_xl(s)
                    if s == 0 and sync_hook is not None:
                        sync_hook()
                    if s >= 1:
                        sy_ep(s - 1)
                sy_ep(nstr - 1)

            elif which == "tensor":
                t = eng
                if fp == 0:
                    em.wait(t, ("prep_done",))
                else:
                    em.wait(t, ("A3", "done_ve"))
                    em.wait(t, ("A2", "exp", NSH - 1, 23))
                    em.wait(t, ("A2", "avdone", NSH - 1, DC - 1))
                    em.wait(t, ("A2", "avtpc", NSH - 1, DC - 1))

                def t_tp(s):
                    for tc in range(NCH):
                        em.wait(t, L("xln", s, tc))
                        for dc in range(DC):
                            k = tc * DC + dc
                            kk = s * NCH * DC + k
                            if kk >= 2:
                                em.wait(t, L("xtc", (kk - 2) // (NCH * DC), (kk - 2) % (NCH * DC)))
                            elif fp > 0:
                                em.wait(t, ("A1", "xtc", NSTR - 1, NCH * DC - 2 + kk))
                            em.inc("s_pe2", t.transpose(
                                pbb(2 + k % 2)[:, 0:P],
                                xln_sb[:, (tc % 2) * D + dc * P:(tc % 2) * D + (dc + 1) * P],
                                ident[:, :]), L("tp", s, k))

                def t_rt(s):
                    if s >= 1:
                        em.wait(t, L("pfree", s - 1, 31))
                    for tc in range(NCH):
                        em.wait(t, L("xtc", s, tc * DC + DC - 1))
                        gg = s * NCH + tc - 1
                        if gg >= 0:
                            em.wait(t, L("rtfree", gg // NCH, gg % NCH))
                        for dc in range(DC):
                            ins = t.matmul(pbk[4][:, 0:8],
                                           xt_sb[:, dc * STRIPE + tc * P:dc * STRIPE + (tc + 1) * P],
                                           rw_sb[:, dc * E:(dc + 1) * E],
                                           start=(dc == 0), stop=(dc == DC - 1))
                        em.inc("s_mm", ins, L("rt_mm", s, tc))
                        em.wait(t, L("w8", s, tc))
                        if gg >= 0:
                            em.wait(t, L("w8free", gg // NCH, gg % NCH))
                        for e in range(E):
                            ins = t.transpose(pbb(5)[0:1, e * P:(e + 1) * P], w8_sb[:, e:e + 1],
                                              ident[:, :])
                        em.inc("s_pe2", ins, L("w8tp", s, tc))
                        em.wait(t, L("wrow", s, tc))
                        for e in range(E):
                            iw = (s * NCH + tc) * E + e
                            if iw >= 2:
                                iwp = iw - 2
                                em.wait(t, L("wbcc", iwp // (NCH * E), (iwp % (NCH * E)) // E,
                                             iwp % E))
                            em.inc("s_pe2", t.matmul(
                                pbk[6 + e % 2][:, 0:P], ones_row[:, :],
                                wrow_sb[0:1, e * P:(e + 1) * P], start=True, stop=True),
                                L("wbc_mm", s, tc, e))

                def t_exp(s):
                    for e in range(E):
                        for fh in range(2):
                            j = e * 2 + fh
                            em.wait(t, L("wl", s, j))
                            for fc in range(NCH):
                                for th in range(2):
                                    i = (j * NCH + fc) * 2 + th
                                    gi = s * 256 + i
                                    bank = gi % 2
                                    if i == 0:
                                        em.wait(t, L("xtc", s, NCH * DC - 1))
                                        em.wait(t, L("wbcc", s, NCH - 1, E - 1))
                                    if gi >= 2:
                                        gp = gi - 2
                                        em.wait(t, L("silu", gp // 256, gp % 256))
                                    for dc in range(DC):
                                        ins = t.matmul(
                                            pbk[bank][:, :],
                                            wslot(j % 2)[:, dc * 1024 + fc * P:dc * 1024 + (fc + 1) * P],
                                            xt_sb[:, dc * STRIPE + th * 512:dc * STRIPE + (th + 1) * 512],
                                            start=(dc == 0), stop=(dc == DC - 1))
                                    em.inc("s_mm", ins, L("exp_mm", s, i))
                                    if fc == NCH - 1 and th == 1:
                                        em.inc("s_wfree", t.nop(), L("wfree", s, j))

                def t_proj(s):
                    em.wait(t, L("wl", s, 16))
                    em.wait(t, L("wl", s, 17))
                    em.wait(t, L("comb", s, 255))
                    for tc in range(NCH):
                        for foq in range(4):
                            jj = tc * 4 + foq
                            gj = s * 32 + jj
                            bank = 4 + jj % 4
                            gp = gj - 4
                            if gp >= 0:
                                em.wait(t, L("pfree", gp // 32, gp % 32))
                            slot = foq // 2
                            for fgc in range(DC):
                                t.matmul(pbk[bank][:, :],
                                         comb_sb[:, fgc * STRIPE + tc * P:fgc * STRIPE + (tc + 1) * P],
                                         wslot(slot)[:, fgc * 1024 + (foq % 2) * 512:
                                                     fgc * 1024 + (foq % 2) * 512 + 512],
                                         start=(fgc == 0), stop=False)
                            ins = t.matmul(pbk[bank][:, :], ones_row[:, :],
                                           pbrow[0:1, foq * 512:(foq + 1) * 512],
                                           start=False, stop=True)
                            em.inc("s_mm", ins, L("proj_mm", s, jj))
                            if jj == 30:
                                em.inc("s_wfree", t.nop(), L("wfree", s, 16))
                            if jj == 31:
                                em.inc("s_wfree", t.nop(), L("wfree", s, 17))

                t_tp(0)
                t_rt(0)
                for s in range(nstr):
                    t_exp(s)
                    if s + 1 < nstr:
                        t_tp(s + 1)
                    t_proj(s)
                    if s + 1 < nstr:
                        t_rt(s + 1)

            elif which == "scalar":
                sc = eng
                if fp > 0:
                    em.wait(sc, ("A2", "atst", NSH - 1, 3))
                    em.wait(sc, ("A2", "avtp", NSH - 1, DC - 1))

                def sc_ln(s):
                    for tc in range(NCH):
                        g = s * NCH + tc
                        pv = tc % 2
                        em.wait(sc, L("xl", s, tc))
                        if g >= 1:
                            em.wait(sc, L("var", (g - 1) // NCH, (g - 1) % NCH))
                        elif fp > 0:
                            em.wait(sc, ("A1", "var", NSTR - 1, NCH - 1))
                        for q4 in range(4):
                            ins = sc.activation(
                                sq_sb[:, :],
                                xch[:, (tc % 2) * D + q4 * 512:(tc % 2) * D + (q4 + 1) * 512],
                                AF.Square, accum_out=red_sb[:, 8 + q4:9 + q4])
                        em.inc("s_act2", ins, L("sq", s, tc))
                        em.wait(sc, L("var", s, tc))
                        em.inc("s_act2", sc.activation(red_sb[:, 2 + pv:3 + pv], red_sb[:, pv:pv + 1],
                                                       AF.Ln, bias=f32(LN_EPS)), L("lnv", s, tc))
                        em.wait(sc, L("lnv", s, tc))
                        em.inc("s_act2", sc.activation(stats[:, SO(s) + 2 * tc + 1:SO(s) + 2 * tc + 2],
                                                       red_sb[:, 2 + pv:3 + pv], AF.Exp, scale=f32(-0.5)),
                               L("istd", s, tc))
                        for dc in range(DC):
                            k = tc * DC + dc
                            em.wait(sc, L("tp", s, k))
                            em.inc("s_act2", sc.activation(
                                xt_sb[:, dc * STRIPE + tc * P:dc * STRIPE + (tc + 1) * P],
                                pbb(2 + k % 2)[:, 0:P], AF.Copy), L("xtc", s, k))

                def sc_es(s):
                    for tc in range(NCH):
                        gg = s * NCH + tc - 1
                        if gg >= 0:
                            em.wait(sc, L("sk", gg // NCH, gg % NCH))
                        em.wait(sc, L("nm1", s, tc))
                        em.inc("s_act2", sc.activation(es_sb[:, :], logit_sb[:, :], AF.Exp,
                                                       bias=nm1_sb[:, :]), L("es", s, tc))

                def sc_silu(s):
                    for e in range(E):
                        for fh in range(2):
                            for fc in range(NCH):
                                for th in range(2):
                                    i = ((e * 2 + fh) * NCH + fc) * 2 + th
                                    gi = s * 256 + i
                                    bank = gi % 2
                                    fgc = fh * NCH + fc
                                    em.wait(sc, L("exp_mm", s, i))
                                    gp = gi - 2
                                    if gp >= 0:
                                        em.wait(sc, L("comb", gp // 256, gp % 256))
                                    em.inc("s_act", sc.activation(
                                        tmp_bf[:, (gi % 2) * 512:(gi % 2 + 1) * 512],
                                        pbk[bank][:, :], AF.Silu,
                                        bias=bias_sb[:, fgc * E + e:fgc * E + e + 1]),
                                        L("silu", s, i))

                def sc_ep(s):
                    for tc in range(NCH):
                        g = s * NCH + tc
                        em.wait(sc, L("ycomp", s, tc))
                        if g >= 1:
                            em.wait(sc, L("yps", (g - 1) // NCH, (g - 1) % NCH))
                        elif fp > 0:
                            em.wait(sc, ("F", 0, "yps", NSTR - 1, NCH - 1))
                        for q4 in range(4):
                            ins = sc.activation(sq_sb[:, :], y_sb[:, q4 * 512:(q4 + 1) * 512],
                                                AF.Square, accum_out=red_sb[:, 12 + q4:13 + q4])
                        em.inc("s_act2", ins, L("sqy", s, tc))
                        em.wait(sc, L("yps", s, tc))
                        em.inc("s_act2", sc.activation(red_sb[:, 5:6], red_sb[:, 4:5], AF.Ln,
                                                       scale=f32(1.0 / D), bias=f32(RMS_EPS_FFN)),
                               L("lny", s, tc))
                        em.wait(sc, L("lny", s, tc))
                        if g >= 1:
                            em.wait(sc, L("contrib1", (g - 1) // NCH, (g - 1) % NCH))
                        elif fp > 0:
                            em.wait(sc, ("F", 0, "contrib1", NSTR - 1, NCH - 1))
                        em.inc("s_act2", sc.activation(red_sb[:, 6:7], red_sb[:, 5:6], AF.Exp,
                                                       scale=f32(-0.5)), L("irms", s, tc))

                sc_ln(0)
                sc_es(0)
                for s in range(nstr):
                    sc_silu(s)
                    if s + 1 < nstr:
                        sc_ln(s + 1)
                    sc_ep(s)
                    if s + 1 < nstr:
                        sc_es(s + 1)

            elif which == "vector":
                v = eng

                def v_proln(s):
                    for tc in range(NCH):
                        slot = (tc % 2) * D
                        g = s * NCH + tc
                        pv = tc % 2
                        if fp > 0 and s == 0 and tc < 2:
                            em.wait(v, ("A1", "tp", NSTR - 1, NCH * DC - 1))
                            if tc == 0:
                                em.wait(v, ("A3", "mm", NTC - 1, 3))
                        em.wait(v, L("xl", s, tc))
                        em.inc("s_ve2", v.reduce_sum(red_sb[:, 7:8], xch[:, slot:slot + D],
                                                     axis=AX, negate=True), L("nsum", s, tc))
                        em.wait(v, L("nsum", s, tc))
                        em.inc("s_ve2", v.tensor_scalar(stats[:, SO(s) + 2 * tc:SO(s) + 2 * tc + 1],
                                                        red_sb[:, 7:8], 1.0 / D, None, ALU.mult),
                               L("nmean", s, tc))
                        em.wait(v, L("sq", s, tc))
                        em.wait(v, L("nmean", s, tc))
                        v.tensor_add(red_sb[:, 8:9], red_sb[:, 8:9], red_sb[:, 9:10])
                        em.inc("s_ve2", v.tensor_add(red_sb[:, 10:11], red_sb[:, 10:11],
                                                     red_sb[:, 11:12]), L("p2", s, tc))
                        em.wait(v, L("p2", s, tc))
                        em.inc("s_ve2", v.tensor_add(red_sb[:, 8:9], red_sb[:, 8:9],
                                                     red_sb[:, 10:11]), L("p4", s, tc))
                        if g >= 2:
                            em.wait(v, L("lnv", (g - 2) // NCH, (g - 2) % NCH))
                        elif fp > 0:
                            em.wait(v, ("A1", "lnv", NSTR - 1, NCH - 2 + g))
                        em.inc("s_ve2", v.tensor_tensor(red_sb[:, pv:pv + 1],
                                                        stats[:, SO(s) + 2 * tc:SO(s) + 2 * tc + 1],
                                                        stats[:, SO(s) + 2 * tc:SO(s) + 2 * tc + 1],
                                                        ALU.mult),
                               L("m2done", s, tc))
                        em.wait(v, L("p4", s, tc))
                        em.wait(v, L("m2done", s, tc))
                        em.inc("s_ve2", v.scalar_tensor_tensor(
                            red_sb[:, pv:pv + 1], red_sb[:, 8:9], 1.0 / D,
                            red_sb[:, pv:pv + 1], ALU.mult, ALU.subtract), L("var", s, tc))
                        em.wait(v, L("istd", s, tc))
                        if g >= 2:
                            em.wait(v, L("tp", (g - 2) // NCH, ((g - 2) % NCH) * DC + DC - 1))
                        elif fp > 0:
                            em.wait(v, ("A1", "tp", NSTR - 1, (NCH - 2 + g) * DC + DC - 1))
                        em.inc("s_ve2", v.tensor_scalar(
                            xln_sb[:, slot:slot + D], xch[:, slot:slot + D],
                            stats[:, SO(s) + 2 * tc:SO(s) + 2 * tc + 1],
                            stats[:, SO(s) + 2 * tc + 1:SO(s) + 2 * tc + 2],
                            ALU.add, ALU.mult), L("xln", s, tc))
                        em.inc("s_xfa", v.engine_nop(), L("xfa", s, tc))

                def v_prort(s):
                    for tc in range(NCH):
                        em.wait(v, L("rt_mm", s, tc))
                        gg = s * NCH + tc - 1
                        if gg >= 0:
                            em.wait(v, L("es", gg // NCH, gg % NCH))
                        elif fp > 0:
                            em.wait(v, ("F", 0, "es", NSTR - 1, NCH - 1))
                        em.inc("s_ve", v.tensor_add(logit_sb[:, :], pbk[4][:, 0:8], rb_bc[:, :]),
                               L("rtfree", s, tc))
                        em.wait(v, L("rtfree", s, tc))
                        em.inc("s_ve", v.reduce_max(m1_sb[:, :], logit_sb[:, :], axis=AX),
                               L("m1", s, tc))
                        em.wait(v, L("m1", s, tc))
                        em.inc("s_ve", v.tensor_scalar_mul(nm1_sb[:, :], m1_sb[:, :], -1.0),
                               L("nm1", s, tc))
                        em.inc("s_ve", v.tensor_scalar(l2_sb[:, :], logit_sb[:, :], m1_sb[:, :],
                                                       None, ALU.is_ge), L("lmask", s, tc))
                        em.wait(v, L("lmask", s, tc))
                        em.inc("s_ve", v.scalar_tensor_tensor(l2_sb[:, :], l2_sb[:, :], NEG,
                                                              logit_sb[:, :], ALU.mult, ALU.add),
                               L("l2", s, tc))
                        em.wait(v, L("l2", s, tc))
                        em.inc("s_ve", v.reduce_max(m2_sb[:, :], l2_sb[:, :], axis=AX),
                               L("m2", s, tc))
                        em.wait(v, L("m2", s, tc))
                        em.inc("s_ve", v.tensor_scalar(keep_sb[:, :], logit_sb[:, :], m2_sb[:, :],
                                                       None, ALU.is_ge), L("keep", s, tc))
                        em.wait(v, L("es", s, tc))
                        em.wait(v, L("keep", s, tc))
                        em.inc("s_ve", v.tensor_mul(sk_sb[:, :], es_sb[:, :], keep_sb[:, :]),
                               L("sk", s, tc))
                        em.wait(v, L("sk", s, tc))
                        em.inc("s_ve", v.reduce_sum(red_sb[:, 7:8], sk_sb[:, :], axis=AX),
                               L("den", s, tc))
                        em.wait(v, L("den", s, tc))
                        em.inc("s_ve", v.reciprocal(red_sb[:, 7:8], red_sb[:, 7:8]),
                               L("rden", s, tc))
                        em.wait(v, L("rden", s, tc))
                        em.inc("s_ve", v.tensor_scalar(w8_sb[:, :], sk_sb[:, :], red_sb[:, 7:8],
                                                       None, ALU.mult), L("w8", s, tc))
                        em.wait(v, L("w8tp", s, tc))
                        em.inc("s_ve", v.tensor_copy(wrow_sb[0:1, :], pbb(5)[0:1, 0:E * P]),
                               L("wrow", s, tc), k=1)
                        em.inc("s_ve", v.engine_nop(), L("w8free", s, tc))
                        for e in range(E):
                            em.wait(v, L("wbc_mm", s, tc, e))
                            em.inc("s_ve", v.tensor_copy(
                                wbc_sb[:, e * STRIPE + tc * P:e * STRIPE + (tc + 1) * P],
                                pbk[6 + e % 2][:, 0:P]), L("wbcc", s, tc, e))

                def v_comb(s):
                    if s >= 1:
                        em.wait(v, L("proj_mm", s - 1, 31))
                    elif fp > 0:
                        em.wait(v, ("A2", "av_mm", NSH - 1, DC - 1))
                    for e in range(E):
                        for fh in range(2):
                            for fc in range(NCH):
                                for th in range(2):
                                    i = ((e * 2 + fh) * NCH + fc) * 2 + th
                                    gi = s * 256 + i
                                    fgc = fh * NCH + fc
                                    ts = (gi % 2) * 512
                                    em.wait(v, L("silu", s, i))
                                    if e == 0:
                                        em.inc("s_ve", v.tensor_tensor(
                                            comb_sb[:, fgc * STRIPE + th * 512:
                                                    fgc * STRIPE + (th + 1) * 512],
                                            tmp_bf[:, ts:ts + 512],
                                            wbc_sb[:, th * 512:(th + 1) * 512],
                                            ALU.mult), L("comb", s, i))
                                    else:
                                        em.inc("s_ve", v.tensor_mul(
                                            tmp_bf[:, ts:ts + 512], tmp_bf[:, ts:ts + 512],
                                            wbc_sb[:, e * STRIPE + th * 512:
                                                   e * STRIPE + (th + 1) * 512]),
                                            L("wmul", s, i))
                                        em.wait(v, L("wmul", s, i))
                                        em.inc("s_ve", v.tensor_add(
                                            comb_sb[:, fgc * STRIPE + th * 512:
                                                    fgc * STRIPE + (th + 1) * 512],
                                            comb_sb[:, fgc * STRIPE + th * 512:
                                                    fgc * STRIPE + (th + 1) * 512],
                                            tmp_bf[:, ts:ts + 512]), L("comb", s, i))

                def v_ep(s):
                    for tc in range(NCH):
                        em.wait(v, L("xe", s, tc))
                        g = s * NCH + tc
                        if fp == 0 and g == 0:
                            em.wait(v, ("prep_done",))
                        if g >= 1:
                            em.wait(v, L("st", (g - 1) // NCH, (g - 1) % NCH))
                        elif fp > 0:
                            em.wait(v, ("A3", "st", NTC - 1))
                        em.inc("s_ve2", v.tensor_scalar(
                            y_sb[:, :], xch[:, 2 * D:3 * D],
                            stats[:, SO(s) + 2 * tc:SO(s) + 2 * tc + 1],
                            stats[:, SO(s) + 2 * tc + 1:SO(s) + 2 * tc + 2],
                            ALU.add, ALU.mult),
                            L("yln", s, tc))
                        for foq in range(4):
                            jj = tc * 4 + foq
                            bank = 4 + jj % 4
                            em.wait(v, L("proj_mm", s, jj))
                            if foq == 0:
                                em.wait(v, L("yln", s, tc))
                            em.inc("s_pfree", v.tensor_add(y_sb[:, foq * 512:(foq + 1) * 512],
                                                           y_sb[:, foq * 512:(foq + 1) * 512],
                                                           pbk[bank][:, :]), L("pfree", s, jj))
                        em.inc("s_ve2", v.engine_nop(), L("ycomp", s, tc))
                        em.wait(v, L("sqy", s, tc))
                        if g >= 1:
                            em.wait(v, L("lny", (g - 1) // NCH, (g - 1) % NCH))
                        elif fp > 0:
                            em.wait(v, ("F", 0, "lny", NSTR - 1, NCH - 1))
                        v.tensor_add(red_sb[:, 12:13], red_sb[:, 12:13], red_sb[:, 13:14])
                        em.inc("s_ve2", v.tensor_add(red_sb[:, 14:15], red_sb[:, 14:15],
                                                     red_sb[:, 15:16]), L("yp2", s, tc))
                        em.wait(v, L("yp2", s, tc))
                        em.inc("s_ve2", v.tensor_add(red_sb[:, 4:5], red_sb[:, 12:13],
                                                     red_sb[:, 14:15]), L("yps", s, tc))
                        em.wait(v, L("irms", s, tc))
                        em.inc("s_ve2", v.tensor_scalar(y_sb[:, :], y_sb[:, :], red_sb[:, 6:7],
                                                        0.5, ALU.mult, ALU.mult),
                               L("contrib1", s, tc))
                        em.wait(v, L("contrib1", s, tc))
                        em.inc("s_ve2", v.tensor_tensor(y_sb[:, :], y_sb[:, :], nwb_sb[:, :],
                                                        ALU.mult), L("contrib2", s, tc))
                        em.wait(v, L("contrib2", s, tc))
                        em.inc("s_ve2", v.tensor_add(y_sb[:, :], y_sb[:, :], xch[:, 2 * D:3 * D]),
                               L("ep_done", s, tc))
                        em.inc("s_xfb", v.engine_nop(), L("xfb", s, tc))

                v_proln(0)
                v_prort(0)
                for s in range(nstr):
                    if tap == "comb" and fp == 0 and s == 1:
                        em.wait(v, ("tapc", 7))
                    v_comb(s)
                    if s + 1 < nstr:
                        v_proln(s + 1)
                    v_ep(s)
                    if s + 1 < nstr:
                        v_prort(s + 1)

                if tap == "logit" and fp == 0:
                    pass

        def phase_qkvT(which, eng):
            if which != "sync":
                return
            sy = eng
            em.wait(sy, ("cast_qkv",))
            slots = [qo2_sb[:, :], qp2_sb[:, :]]
            nidx = 0
            for orange in range(12):
                for cc in range(DC):
                    sl = slots[nidx % 2]
                    if nidx >= 2:
                        em.wait(sy, ("qkvT_st", nidx - 2))
                    em.inc("s_ld", sy.dma_start_transpose(
                        sl, d_qkvb[orange * 512:(orange + 1) * 512, cc * P:(cc + 1) * P]),
                        ("qkvT_ld", nidx), 16)
                    em.wait(sy, ("qkvT_ld", nidx))
                    em.inc("s_st", sy.dma_start(
                        out=d_qkvT[cc * P:(cc + 1) * P, orange * 512:(orange + 1) * 512],
                        in_=sl), ("qkvT_st", nidx), 16)
                    nidx += 1

        def phase_A1(which, eng):
            L = lambda *a: ("A1",) + a

            if which == "sync":
                sy = eng
                em.wait(sy, ("F", 0, "proj_mm", NSTR - 1, 31))
                em.wait(sy, ("F", 0, "comb", NSTR - 1, 255))
                em.inc("s_ld", sy.dma_start(out=wbc_sb[:, COS_OFF:COS_OFF + TOK], in_=d_cos),
                       None, 16)
                em.inc("s_ld", sy.dma_start(out=wbc_sb[:, SIN_OFF:SIN_OFF + TOK], in_=d_sin),
                       L("cs"), 16)
                for s in range(NSTR):
                    for tc in range(NCH):
                        g = s * NCH + tc
                        if g >= 2:
                            em.wait(sy, L("xfa", (g - 2) // NCH, (g - 2) % NCH))
                            em.wait(sy, L("sq", (g - 2) // NCH, (g - 2) % NCH))
                        else:
                            em.wait(sy, ("F", 0, "xfa", NSTR - 1, NCH - 2 + g))
                            em.wait(sy, ("F", 0, "sq", NSTR - 1, NCH - 2 + g))
                        em.wait(sy, ("F", 0, "st", s, tc))
                        em.inc("s_xl", sy.dma_start(
                            out=xch[:, (tc % 2) * D:(tc % 2 + 1) * D],
                            in_=d_x1[(s * NCH + tc) * P:(s * NCH + tc + 1) * P, :]),
                            L("xl", s, tc), 16)
                    def a1_store(orange):
                        if orange < 8:
                            for oc in range(4):
                                for th in range(2):
                                    o_abs = orange * 512 + oc * P
                                    is_q = o_abs < D
                                    hh = (o_abs % D) // P
                                    dst = d_qt if is_q else d_kt
                                    em.wait(sy, L("qkout", s, orange, oc, th))
                                    i = QK_IDX[(s, orange, oc, th)]
                                    srcb = tmp_bf[:, 1536:2048] if i % 2 == 0 else qo2_sb[:, :]
                                    em.inc("s_st", sy.dma_start(
                                        out=dst[hh * P:(hh + 1) * P,
                                                s * STRIPE + th * 512:s * STRIPE + (th + 1) * 512],
                                        in_=srcb), L("qkst", s, orange, oc, th), 16)
                        else:
                            for tc in range(NCH):
                                em.wait(sy, L("vout", s, orange, tc))
                                em.inc("s_st", sy.dma_start(
                                    out=d_v[(s * NCH + tc) * P:(s * NCH + tc + 1) * P,
                                            (orange - 8) * 512:(orange - 7) * 512],
                                    in_=tmp_bf[:, (tc % 2) * 512:(tc % 2 + 1) * 512]),
                                    L("vst", s, orange, tc), 16)

                    for orange in range(12):
                        gs = s * 12 + orange
                        if gs == 0:
                            em.wait(sy, ("qkvT_st", 191))
                        if gs >= 4:
                            em.wait(sy, L("wfree", (gs - 4) // 12, (gs - 4) % 12))
                        elif gs in (0, 1):
                            em.wait(sy, ("F", 0, "wfree", NSTR - 1, 16))
                        else:
                            em.wait(sy, ("F", 0, "wfree", NSTR - 1, 17))
                        em.inc("s_ld", sy.dma_start(
                            out=a1slot(orange % 4).rearrange("p (c o) -> p c o", o=512),
                            in_=d_qkvT[:, orange * 512:(orange + 1) * 512]
                            .rearrange("(c p) o -> p c o", p=P)), L("wl", s, orange), 16)
                        if orange >= 1:
                            a1_store(orange - 1)
                    a1_store(11)

            elif which == "tensor":
                t = eng
                em.wait(t, ("F", 0, "pfree", NSTR - 1, 31))
                em.wait(t, ("F", 0, "silu", NSTR - 1, 255))
                em.wait(t, ("F", 0, "xtc", NSTR - 1, NCH * DC - 1))
                em.wait(t, ("F", 0, "rtfree", NSTR - 1, NCH - 1))
                em.wait(t, ("F", 0, "w8free", NSTR - 1, NCH - 1))
                em.wait(t, ("F", 0, "wbcc", NSTR - 1, NCH - 1, E - 1))
                for s in range(NSTR):
                    for tc in range(NCH):
                        em.wait(t, L("xln", s, tc))
                        if tc == 0 and s >= 1:
                            em.wait(t, L("qkout", *QK_TILES[s * 64 - 1]))
                        for dc in range(DC):
                            k = tc * DC + dc
                            kk = s * NCH * DC + k
                            if kk >= 2:
                                em.wait(t, L("xtc", (kk - 2) // (NCH * DC), (kk - 2) % (NCH * DC)))
                            else:
                                em.wait(t, ("F", 0, "xtc", NSTR - 1, NCH * DC - 2 + kk))
                            em.inc("s_pe2", t.transpose(
                                pbb(2 + k % 2)[:, 0:P],
                                xln_sb[:, (tc % 2) * D + dc * P:(tc % 2) * D + (dc + 1) * P],
                                ident[:, :]), L("tp", s, k))
                    qk_local = [(o_, oc_, th_) for o_ in range(8)
                                for oc_ in range(4) for th_ in range(2)]

                    def a1_stage1(tile):
                        orange, oc, th = tile
                        i = QK_IDX[(s, orange, oc, th)]
                        bank = i % 3
                        if oc == 0 and th == 0:
                            em.wait(t, L("wl", s, orange))
                        if i - s * 64 < 3 and bank == 2:
                            em.wait(t, L("xtc", s, NCH * DC - 2))
                        if i - s * 64 < 1:
                            em.wait(t, L("xtc", s, NCH * DC - 1))
                        if i >= 3:
                            em.wait(t, L("qb", *QK_TILES[i - 3]))
                        for dc in range(DC):
                            ins = t.matmul(
                                pbk[bank][:, :],
                                a1slot(orange % 4)[:, dc * 512 + oc * P:dc * 512 + (oc + 1) * P],
                                xt_sb[:, dc * STRIPE + th * 512:dc * STRIPE + (th + 1) * 512],
                                start=(dc == 0), stop=(dc == DC - 1))
                        em.inc("s_mm", ins, L("qk_mm", s, orange, oc, th))
                        if oc == 3 and th == 1:
                            em.inc("s_wfree", t.nop(), L("wfree", s, orange))

                    def a1_stage2(tile):
                        orange, oc, th = tile
                        i = QK_IDX[(s, orange, oc, th)]
                        em.wait(t, L("qbc", s, orange, oc, th))
                        if i - s * 64 < 1:
                            em.wait(t, L("xtc", s, NCH * DC - 1))
                        if i >= 1:
                            em.wait(t, L("qsin", *QK_TILES[i - 1]))
                        em.inc("s_pe2", t.matmul(pbk[3][:, :], pm_sb[:, :],
                                                 tmp_bf[:, 0:512], start=True, stop=True),
                               L("rot_mm", s, orange, oc, th))

                    def a1_stage3(tile):
                        orange, oc, th = tile
                        i = QK_IDX[(s, orange, oc, th)]
                        em.wait(t, L("qsq", s, orange, oc, th))
                        if i >= 1:
                            em.wait(t, L("rowb", *QK_TILES[i - 1]))
                        em.inc("s_pe2", t.matmul(pbk[4][0:1, :], ones_col[:, :],
                                                 sq_sb[:, 0:512], start=True, stop=True),
                               L("rms_mm", s, orange, oc, th))

                    def a1_stage4(tile):
                        orange, oc, th = tile
                        i = QK_IDX[(s, orange, oc, th)]
                        em.wait(t, L("rowb", s, orange, oc, th))
                        if i >= 1:
                            em.wait(t, L("qkout", *QK_TILES[i - 1]))
                        em.inc("s_pe2", t.matmul(pbk[5][:, :], ones_row[:, :],
                                                 rms_rowb[0:1, :], start=True, stop=True),
                               L("bc_mm", s, orange, oc, th))

                    for step in range(len(qk_local) + 3):
                        if 3 <= step and step - 3 < len(qk_local):
                            a1_stage4(qk_local[step - 3])
                        if 2 <= step and step - 2 < len(qk_local):
                            a1_stage3(qk_local[step - 2])
                        if 1 <= step and step - 1 < len(qk_local):
                            a1_stage2(qk_local[step - 1])
                        if step < len(qk_local):
                            a1_stage1(qk_local[step])
                    for orange in range(8, 12):
                        em.wait(t, L("wl", s, orange))
                        if True:
                            for tc in range(NCH):
                                iv = V_IDX[(s, orange, tc)]
                                bank = iv % 2 + 6
                                if iv >= 2:
                                    em.wait(t, L("vc", *V_TILES[iv - 2]))
                                elif fpv0 := (iv < 2):
                                    em.wait(t, ("F", 0, "wbcc", NSTR - 1, NCH - 1, E - 2 + iv))
                                for dc in range(DC):
                                    ins = t.matmul(
                                        pbk[bank][:, :],
                                        xt_sb[:, dc * STRIPE + tc * P:dc * STRIPE + (tc + 1) * P],
                                        a1slot(orange % 4)[:, dc * 512:(dc + 1) * 512],
                                        start=(dc == 0), stop=(dc == DC - 1))
                                em.inc("s_mm", ins, L("v_mm", s, orange, tc))
                                if tc == NCH - 1:
                                    em.inc("s_wfree", t.nop(), L("wfree", s, orange))

            elif which == "scalar":
                sc = eng
                for s in range(NSTR):
                    for tc in range(NCH):
                        g = s * NCH + tc
                        pv = tc % 2
                        em.wait(sc, L("xl", s, tc))
                        if s >= 1 and tc == 0:
                            em.wait(sc, L("rms_mm", s - 1, 7, 3, 1))
                        if g >= 1:
                            em.wait(sc, L("var", (g - 1) // NCH, (g - 1) % NCH))
                        else:
                            em.wait(sc, ("F", 0, "var", NSTR - 1, NCH - 1))
                        for q4 in range(4):
                            ins = sc.activation(
                                sq_sb[:, :],
                                xch[:, (tc % 2) * D + q4 * 512:(tc % 2) * D + (q4 + 1) * 512],
                                AF.Square, accum_out=red_sb[:, 8 + q4:9 + q4])
                        em.inc("s_act2", ins, L("sq", s, tc))
                        em.wait(sc, L("var", s, tc))
                        em.inc("s_act2", sc.activation(red_sb[:, 2 + pv:3 + pv], red_sb[:, pv:pv + 1],
                                                       AF.Ln, bias=f32(LN_EPS)), L("lnv", s, tc))
                        em.wait(sc, L("lnv", s, tc))
                        em.inc("s_act2", sc.activation(stats[:, 2 * tc + 1:2 * tc + 2],
                                                       red_sb[:, 2 + pv:3 + pv], AF.Exp, scale=f32(-0.5)),
                               L("istd", s, tc))
                    qk_local = [(o_, oc_, th_) for o_ in range(8)
                                for oc_ in range(4) for th_ in range(2)]

                    def a1s_front(tile):
                        orange, oc, th = tile
                        i = QK_IDX[(s, orange, oc, th)]
                        em.wait(sc, L("qrope", s, orange, oc, th))
                        if i >= 1:
                            em.wait(sc, L("rms_mm", *QK_TILES[i - 1]))
                        qsrc = tmp_bf[:, 512:1024] if i % 2 == 0 else qp2_sb[:, :]
                        em.inc("s_act2", sc.activation(sq_sb[:, 0:512], qsrc,
                                                       AF.Square), L("qsq", s, orange, oc, th))

                    def a1s_back(tile):
                        orange, oc, th = tile
                        o_abs = orange * 512 + oc * P
                        is_q = o_abs < D
                        i = QK_IDX[(s, orange, oc, th)]
                        em.wait(sc, L("rms_mm", s, orange, oc, th))
                        if i >= 1:
                            em.wait(sc, L("rowb", *QK_TILES[i - 1]))
                        em.inc("s_act2", sc.activation(
                            pbk[4][0:1, :], pbk[4][0:1, :], AF.Ln,
                            scale=f32(1.0 / P), bias=f32(RMS_EPS_QK)),
                            L("lnr", s, orange, oc, th))
                        em.wait(sc, L("lnr", s, orange, oc, th))
                        em.inc("s_act2", sc.activation(
                            pbk[4][0:1, :], pbk[4][0:1, :], AF.Exp, scale=f32(-0.5),
                            bias=f32(-HALF_LN128) if is_q else f32(0.0)),
                            L("expr", s, orange, oc, th))

                    for step in range(len(qk_local) + 1):
                        if step >= 1:
                            a1s_back(qk_local[step - 1])
                        if step < len(qk_local):
                            a1s_front(qk_local[step])

            elif which == "vector":
                v = eng
                em.wait(v, L("cs"))
                for s in range(NSTR):
                    for tc in range(NCH):
                        slot = (tc % 2) * D
                        g = s * NCH + tc
                        pv = tc % 2
                        em.wait(v, L("xl", s, tc))
                        em.inc("s_ve2", v.reduce_sum(red_sb[:, 7:8], xch[:, slot:slot + D],
                                                     axis=AX, negate=True), L("nsum", s, tc))
                        em.wait(v, L("nsum", s, tc))
                        em.inc("s_ve2", v.tensor_scalar(stats[:, 2 * tc:2 * tc + 1],
                                                        red_sb[:, 7:8], 1.0 / D, None, ALU.mult),
                               L("nmean", s, tc))
                        em.wait(v, L("sq", s, tc))
                        em.wait(v, L("nmean", s, tc))
                        v.tensor_add(red_sb[:, 8:9], red_sb[:, 8:9], red_sb[:, 9:10])
                        em.inc("s_ve2", v.tensor_add(red_sb[:, 10:11], red_sb[:, 10:11],
                                                     red_sb[:, 11:12]), L("p2", s, tc))
                        em.wait(v, L("p2", s, tc))
                        em.inc("s_ve2", v.tensor_add(red_sb[:, 8:9], red_sb[:, 8:9],
                                                     red_sb[:, 10:11]), L("p4", s, tc))
                        if g >= 2:
                            em.wait(v, L("lnv", (g - 2) // NCH, (g - 2) % NCH))
                        else:
                            em.wait(v, ("F", 0, "lnv", NSTR - 1, NCH - 2 + g))
                        em.inc("s_ve2", v.tensor_tensor(red_sb[:, pv:pv + 1], stats[:, 2 * tc:2 * tc + 1],
                                                        stats[:, 2 * tc:2 * tc + 1], ALU.mult),
                               L("m2done", s, tc))
                        em.wait(v, L("p4", s, tc))
                        em.wait(v, L("m2done", s, tc))
                        em.inc("s_ve2", v.scalar_tensor_tensor(
                            red_sb[:, pv:pv + 1], red_sb[:, 8:9], 1.0 / D,
                            red_sb[:, pv:pv + 1], ALU.mult, ALU.subtract), L("var", s, tc))
                        em.wait(v, L("istd", s, tc))
                        if g >= 2:
                            em.wait(v, L("tp", (g - 2) // NCH, ((g - 2) % NCH) * DC + DC - 1))
                        else:
                            em.wait(v, ("F", 0, "tp", NSTR - 1, (NCH - 2 + g) * DC + DC - 1))
                        em.inc("s_ve2", v.tensor_scalar(
                            xln_sb[:, slot:slot + D], xch[:, slot:slot + D],
                            stats[:, 2 * tc:2 * tc + 1], stats[:, 2 * tc + 1:2 * tc + 2],
                            ALU.add, ALU.mult), L("xln", s, tc))
                        em.inc("s_xfa", v.engine_nop(), L("xfa", s, tc))
                        for dc in range(DC):
                            k = tc * DC + dc
                            em.wait(v, L("tp", s, k))
                            em.inc("s_ve", v.tensor_copy(
                                xt_sb[:, dc * STRIPE + tc * P:dc * STRIPE + (tc + 1) * P],
                                pbb(2 + k % 2)[:, 0:P]), L("xtc", s, k))
                    qk_local = [(o_, oc_, th_) for o_ in range(8)
                                for oc_ in range(4) for th_ in range(2)]

                    def qp(i):
                        return tmp_bf[:, 512:1024] if i % 2 == 0 else qp2_sb[:, :]

                    def a1v_front(tile):
                        orange, oc, th = tile
                        tok0 = s * STRIPE + th * 512
                        i = QK_IDX[(s, orange, oc, th)]
                        bank = i % 3
                        em.wait(v, L("qk_mm", s, orange, oc, th))
                        if i >= 1:
                            em.wait(v, L("rot_mm", *QK_TILES[i - 1]))
                        em.inc("s_ve", v.tensor_copy(tmp_bf[:, 0:512], pbk[bank][:, :]),
                               L("qb", s, orange, oc, th))
                        em.inc("s_ve", v.engine_nop(), L("qbc", s, orange, oc, th))
                        em.wait(v, L("qbc", s, orange, oc, th))
                        if i >= 2:
                            em.wait(v, L("qsq", *QK_TILES[i - 2]))
                        em.inc("s_ve", v.tensor_tensor(
                            qp(i), tmp_bf[:, 0:512],
                            wbc_sb[:, COS_OFF + tok0:COS_OFF + tok0 + 512], ALU.mult),
                            L("qcos", s, orange, oc, th))
                        em.wait(v, L("rot_mm", s, orange, oc, th))
                        em.inc("s_ve", v.tensor_tensor(
                            tmp_bf[:, 1024:1536], pbk[3][:, :],
                            wbc_sb[:, SIN_OFF + tok0:SIN_OFF + tok0 + 512], ALU.mult),
                            L("qsin", s, orange, oc, th))
                        em.wait(v, L("qcos", s, orange, oc, th))
                        em.wait(v, L("qsin", s, orange, oc, th))
                        em.inc("s_ve", v.tensor_add(
                            qp(i), qp(i),
                            tmp_bf[:, 1024:1536]), L("qrope", s, orange, oc, th))

                    def a1v_back(tile):
                        orange, oc, th = tile
                        i = QK_IDX[(s, orange, oc, th)]
                        em.wait(v, L("expr", s, orange, oc, th))
                        if i >= 1:
                            em.wait(v, L("bc_mm", *QK_TILES[i - 1]))
                        em.inc("s_ve", v.tensor_copy(rms_rowb[0:1, :], pbk[4][0:1, :]),
                               L("rowb", s, orange, oc, th))
                        em.wait(v, L("bc_mm", s, orange, oc, th))
                        if i == 1:
                            em.wait(v, ("qkvT_st", 191))
                        if i >= 2:
                            em.wait(v, L("qkst", *QK_TILES[i - 2]))
                        dstq = tmp_bf[:, 1536:2048] if i % 2 == 0 else qo2_sb[:, :]
                        em.inc("s_ve", v.tensor_tensor(
                            dstq, qp(i), pbk[5][:, :], ALU.mult),
                            L("qkout", s, orange, oc, th))

                    for step in range(len(qk_local) + 2):
                        if step >= 2:
                            a1v_back(qk_local[step - 2])
                        if step < len(qk_local):
                            a1v_front(qk_local[step])
                    for orange in range(8, 12):
                        if True:
                            for tc in range(NCH):
                                iv = V_IDX[(s, orange, tc)]
                                bank = iv % 2 + 6
                                em.wait(v, L("v_mm", s, orange, tc))
                                if iv >= 2:
                                    em.wait(v, L("vst", *V_TILES[iv - 2]))
                                em.inc("s_ve", v.tensor_copy(
                                    tmp_bf[:, (tc % 2) * 512:(tc % 2 + 1) * 512], pbk[bank][:, :]),
                                    L("vout", s, orange, tc))
                                em.inc("s_ve", v.engine_nop(), L("vc", s, orange, tc))

        def phase_A2(which, eng):
            L = lambda *a: ("A2",) + a

            ST = []
            SIDX0 = {}
            for kcb in range(DC):
                nq = T - kcb * P
                SIDX0[kcb] = len(ST)
                for t2 in range((nq + 1023) // 1024):
                    ST.append((kcb, t2, min(1024, nq - t2 * 1024)))
            NST = len(ST)

            def qtbuf(b):
                return xt_sb[:, b * 2048:(b + 1) * 2048]

            def ktbuf(b):
                return xt_sb[:, 4096 + b * 2048:4096 + (b + 1) * 2048]

            def vbuf(b):
                return comb_sb[:, b * 2112:(b + 1) * 2112]

            def stbuf(sidx):
                return pb2[0] if sidx % 2 == 0 else pb2[3]

            if which == "sync":
                sy = eng
                for sh in range(NSH):
                    s, hh = sh // H, sh % H
                    b = sh % 2
                    if sh >= 2:
                        em.wait(sy, L("shdone", sh - 2))
                    else:
                        em.wait(sy, L("vones", b))
                        em.wait(sy, ("A1", "v_mm", NSTR - 1, 11, NCH - 1))
                        em.wait(sy, ("A1", "qkst", 2 * s + 1, 7, 3, 1))
                        em.wait(sy, ("A1", "vst", 2 * s + 1, 11, NCH - 1))
                    em.inc("s_xl", sy.dma_start(
                        out=qtbuf(b), in_=d_qt[hh * P:(hh + 1) * P, s * T:(s + 1) * T]), None, 16)
                    em.inc("s_xl", sy.dma_start(
                        out=ktbuf(b), in_=d_kt[hh * P:(hh + 1) * P, s * T:(s + 1) * T]), None, 16)
                    em.inc("s_xl", sy.dma_start(
                        out=vbuf(b).rearrange("p (kc c) -> p kc c", c=132)[:, :, 0:P],
                        in_=d_v[s * T:(s + 1) * T, hh * P:(hh + 1) * P]
                        .rearrange("(kc p) c -> p kc c", p=P)), L("qkv", sh), 16)

            elif which == "tensor":
                t = eng
                em.wait(t, ("A1", "qb", NSTR - 1, 7, 3, 1))
                em.wait(t, ("A1", "qsin", NSTR - 1, 7, 3, 1))
                em.wait(t, ("A1", "rowb", NSTR - 1, 7, 3, 1))
                em.wait(t, ("A1", "qkout", NSTR - 1, 7, 3, 1))
                em.wait(t, ("A1", "vc", NSTR - 1, 11, NCH - 1))
                for sh in range(NSH):
                    b = sh % 2
                    em.wait(t, L("qkv", sh))
                    for step in range(DC + 4):
                        if step < DC:
                            kcb = step
                            q0 = kcb * P
                            for t2 in range((T - q0 + 1023) // 1024):
                                sidx = SIDX0[kcb] + t2
                                w2 = ST[sidx][2]
                                gsid = sh * NST + sidx
                                if gsid >= 2:
                                    gp = gsid - 2
                                    em.wait(t, L("exp", gp // NST, gp % NST))
                                nsub = (w2 + 511) // 512
                                for sub in range(nsub):
                                    ws = min(512, w2 - sub * 512)
                                    ins = t.matmul(
                                        stbuf(sidx)[:, sub * 512:sub * 512 + ws],
                                        ktbuf(b)[:, kcb * P:(kcb + 1) * P],
                                        qtbuf(b)[:, q0 + t2 * 1024 + sub * 512:
                                                 q0 + t2 * 1024 + sub * 512 + ws],
                                        start=True, stop=True)
                                em.inc("s_mm", ins, L("sc_mm", sh, sidx))
                        if step >= 2 and step - 2 < DC:
                            j = step - 2
                            avoff = (j % 2) * 512
                            em.wait(t, L("exp", sh, SIDX0[j]))
                            gav = sh * DC + j
                            if gav >= 2:
                                gp = gav - 2
                                em.wait(t, L("avdone", gp // DC, gp % DC))
                            for kcb2 in range(j + 1):
                                ins = t.matmul(
                                    pb2[1][:, avoff:avoff + 129],
                                    wbuf[:, kcb2 * T + j * P:kcb2 * T + (j + 1) * P],
                                    vbuf(b)[:, kcb2 * 132:kcb2 * 132 + 129],
                                    start=(kcb2 == 0), stop=(kcb2 == j))
                            em.inc("s_mm", ins, L("av_mm", sh, j))
                        if step >= 4 and step - 4 < DC:
                            j2 = step - 4
                            em.wait(t, L("avsc", sh, j2))
                            gav = sh * DC + j2
                            if gav >= 2:
                                gp = gav - 2
                                em.wait(t, L("avst", gp // DC, gp % DC))
                            em.inc("s_pe2", t.transpose(
                                pb2[2].bitcast(BF16)[:, (j2 % 2) * 1024:(j2 % 2) * 1024 + P],
                                qo2_sb[:, (j2 % 2) * P:(j2 % 2 + 1) * P],
                                ident[:, :]), L("avtp", sh, j2))

            elif which == "scalar":
                sc = eng
                em.wait(sc, ("A1", "qkst", NSTR - 1, 7, 3, 1))
                def at_store(sh_, g_):
                    s2, hh2 = sh_ // H, sh_ % H
                    em.wait(sc, L("avst", sh_, 4 * g_ + 3))
                    em.inc("s_ast", sc.dma_start(
                        out=d_at[hh2 * P:(hh2 + 1) * P,
                                 s2 * T + g_ * 512:s2 * T + (g_ + 1) * 512],
                        in_=tmp_bf[:, g_ * 512:(g_ + 1) * 512]),
                        L("atst", sh_, g_), 16)

                for sh in range(NSH):
                    for sidx in range(NST):
                        kcb, t2, w2 = ST[sidx]
                        q0 = kcb * P
                        if sh >= 1 and sidx in (7, 11, 15, 19):
                            at_store(sh - 1, (sidx - 7) // 4)
                        em.wait(sc, L("sc_mm", sh, sidx))
                        if t2 == 0:
                            em.wait(sc, L("mask", sh, kcb))
                        if sh >= 1:
                            j_last = (kcb * P + t2 * 1024 + w2 - 1) // P
                            em.wait(sc, L("avdone", sh - 1, min(j_last, DC - 1)))
                        em.inc("s_act", sc.activation(
                            wbuf[:, kcb * T + q0 + t2 * 1024:kcb * T + q0 + t2 * 1024 + w2],
                            stbuf(sidx)[:, 0:w2], AF.Exp, bias=f32(CSHIFT)),
                            L("exp", sh, sidx))
                for g_ in range(4):
                    at_store(NSH - 1, g_)

            elif which == "vector":
                v = eng
                for b in range(2):
                    em.wait(v, ("F", 0, "proj_mm", NSTR - 1, 31))
                    em.inc("s_ve2", v.memset(vbuf(b)[:, :], 1.0), L("vones", b))
                em.wait(v, ("A1", "vst", NSTR - 1, 11, NCH - 1))
                em.wait(v, ("A1", "qkst", NSTR - 1, 7, 3, 1))
                for sh in range(NSH):
                    b = sh % 2
                    for step in range(DC + 4):
                        if step < DC:
                            kcb = step
                            sidx = SIDX0[kcb]
                            em.wait(v, L("sc_mm", sh, sidx))
                            em.inc("s_ve", v.tensor_add(stbuf(sidx)[:, 0:P], stbuf(sidx)[:, 0:P],
                                                        maskt_sb[:, :]), L("mask", sh, kcb))
                        if step >= 2 and step - 2 < DC:
                            j = step - 2
                            avoff = (j % 2) * 512
                            em.wait(v, L("av_mm", sh, j))
                            em.inc("s_ve", v.reciprocal(rinv_sb[:, j % 2:j % 2 + 1],
                                                        pb2[1][:, avoff + 128:avoff + 129]),
                                   L("rinv", sh, j))
                            em.wait(v, L("rinv", sh, j))
                            gav = sh * DC + j
                            if gav >= 2:
                                gp = gav - 2
                                em.wait(v, L("avtp", gp // DC, gp % DC))
                            em.inc("s_ve", v.tensor_scalar(
                                qo2_sb[:, (j % 2) * P:(j % 2 + 1) * P],
                                pb2[1][:, avoff:avoff + P], rinv_sb[:, j % 2:j % 2 + 1], None,
                                ALU.mult), L("avsc", sh, j))
                            em.inc("s_ve", v.engine_nop(), L("avdone", sh, j))
                        if step >= 4 and step - 4 < DC:
                            j2 = step - 4
                            em.wait(v, L("avtp", sh, j2))
                            if sh >= 1:
                                em.wait(v, L("atst", sh - 1, j2 // 4))
                            em.inc("s_ve", v.tensor_copy(
                                tmp_bf[:, j2 * P:(j2 + 1) * P],
                                pb2[2].bitcast(BF16)[:, (j2 % 2) * 1024:(j2 % 2) * 1024 + P]),
                                L("avst", sh, j2))
                            em.inc("s_ve", v.engine_nop(), L("avtpc", sh, j2))
                    em.inc("s_ve", v.engine_nop(), L("shdone", sh))

        def phase_A3(which, eng, xo):
            L = lambda *a: ("A3",) + a

            if which == "sync":
                sy = eng
                em.wait(sy, ("cast_wo",))
                em.wait(sy, ("A2", "shdone", NSH - 1))
                n = 0
                for oc in range(DC):
                    for cr in range(4):
                        em.inc("s_ld", sy.dma_start_transpose(
                            wbuf[:, oc * 2048 + cr * 512:oc * 2048 + (cr + 1) * 512],
                            d_wob[cr * 512:(cr + 1) * 512, oc * P:(oc + 1) * P]),
                            L("wo", n), 16)
                        n += 1
                for tc in range(NTC):
                    if tc >= 2:
                        em.wait(sy, L("atfree", tc - 2))
                    else:
                        em.wait(sy, ("A2", "av_mm", NSH - 1, DC - 1))
                    sh_last = (tc * P) // T * H + H - 1
                    em.wait(sy, ("A2", "atst", sh_last, (tc % (T // P)) // 4))
                    em.inc("s_xl", sy.dma_start(
                        out=xt_sb[:, (tc % 2) * 2048:(tc % 2 + 1) * 2048]
                        .rearrange("p (c t) -> p c t", t=P),
                        in_=d_at[:, tc * P:(tc + 1) * P].rearrange("(c p) t -> p c t", p=P)),
                        L("at", tc), 16)
                    if tc >= 2:
                        em.wait(sy, L("x1free", tc - 2))
                    else:
                        em.wait(sy, ("A1", "xfa", NSTR - 1, NCH - 2 + tc))
                        em.wait(sy, ("A1", "sq", NSTR - 1, NCH - 2 + tc))
                    em.inc("s_xl", sy.dma_start(
                        out=xch[:, (tc % 2) * D:(tc % 2 + 1) * D],
                        in_=d_x1[tc * P:(tc + 1) * P, :]), L("x1", tc), 16)
                    em.wait(sy, L("x2done", tc))
                    em.inc("s_st", sy.dma_start(
                        out=xo[tc * P:(tc + 1) * P, :], in_=y_sb[:, :]), L("st", tc), 16)

            elif which == "tensor":
                t = eng
                for n in range(64):
                    em.wait(t, L("wo", n))
                em.wait(t, ("A2", "avtpc", NSH - 1, DC - 1))
                for tc in range(NTC):
                    em.wait(t, L("at", tc))
                    for foq in range(4):
                        jj = tc * 4 + foq
                        bank = 4 + jj % 4
                        if jj >= 4:
                            em.wait(t, L("pfree", jj - 4))
                        for oc in range(DC):
                            ins = t.matmul(
                                pbk[bank][:, :],
                                xt_sb[:, (tc % 2) * 2048 + oc * P:(tc % 2) * 2048 + (oc + 1) * P],
                                wbuf[:, oc * 2048 + foq * 512:oc * 2048 + (foq + 1) * 512],
                                start=(oc == 0), stop=(oc == DC - 1))
                        em.inc("s_mm", ins, L("mm", tc, foq))
                        if foq == 3:
                            em.inc("s_pe2", t.nop(), L("atfree", tc))

            elif which == "vector":
                v = eng
                for tc in range(NTC):
                    em.wait(v, L("x1", tc))
                    if tc >= 1:
                        em.wait(v, L("st", tc - 1))
                    else:
                        em.wait(v, ("F", 0, "st", NSTR - 1, NCH - 1))
                    for foq in range(4):
                        em.wait(v, L("mm", tc, foq))
                        em.inc("s_ve2", v.tensor_add(
                            y_sb[:, foq * 512:(foq + 1) * 512],
                            xch[:, (tc % 2) * D + foq * 512:(tc % 2) * D + (foq + 1) * 512],
                            pbk[4 + (tc * 4 + foq) % 4][:, :]), L("pfree", tc * 4 + foq))
                    em.inc("s_ve2", v.engine_nop(), L("x2done", tc))
                    em.inc("s_ve2", v.engine_nop(), L("x1free", tc))
                em.inc("s_ve2", eng.engine_nop(), ("A3", "done_ve"))

        def phase_final(which, eng):
            if which != "sync" or em.mode != "emit":
                return
            for sn in SEM_NAMES:
                cnt = em.cnt.get(sn, 0)
                if cnt > 0:
                    eng.wait_ge(em.sems[sn].h, cnt)

        def run_phases(which, eng):
            phase_prep(which, eng)
            if upto.startswith("dbg_"):
                m = upto[4:]
                ns = NSTR
                if "_s" in m:
                    m, _, nss = m.partition("_s")
                    ns = int(nss)
                pmap = {"ln": (), "tp": ("tp",), "rt": ("tp", "rt"),
                        "w8": ("tp", "rt", "w8c"), "wbc": ("tp", "rt", "w8c", "wbc"),
                        "pro": ("pro",), "exp": ("pro", "exp")}
                phase_F(which, eng, 0, d_x, d_out, parts=pmap[m], nstr=ns)
                return
            if upto == "f1":
                phase_F(which, eng, 0, d_x, d_out)
                return
            phase_F(which, eng, 0, d_x, d_x1,
                    sync_hook=(lambda: phase_qkvT(which, eng)) if which == "sync" else None)
            phase_A1(which, eng)
            phase_A2(which, eng)
            if upto == "a3":
                phase_A3(which, eng, d_out)
                return
            phase_A3(which, eng, d_x2)
            phase_F(which, eng, 1, d_x2, d_out)

        em.mode = "plan"
        fake = _FakeEng()
        for which in ENGINES:
            run_phases(which, fake)
        total_plan = dict(em.cnt)

        em.mode = "emit"
        em.cnt = total_plan
        global _LAST_EM
        _LAST_EM = em

        block = ctx.enter_context(nc.Block())

        @block.sync
        def _(sy):
            run_phases("sync", sy)
            phase_final("sync", sy)

        @block.gpsimd
        def _(g):
            run_phases("gpsimd", g)

        @block.tensor
        def _(t):
            run_phases("tensor", t)

        @block.scalar
        def _(sc):
            run_phases("scalar", sc)

        @block.vector
        def _(v):
            run_phases("vector", v)

    return nc



def _host_constants():
    ident = np.eye(P, dtype=bf16)
    pm = np.zeros((P, P), dtype=np.float32)
    for i in range(P // 2):
        pm[2 * i + 1, 2 * i] = -1.0
        pm[2 * i, 2 * i + 1] = 1.0
    pm = pm.astype(bf16)
    inv_freq = (1.0 / 10000.0 ** (np.arange(0, P, 2, dtype=np.float64) / P))
    pos = (np.arange(TOK) % T).astype(np.float64)
    ang = np.repeat(inv_freq, 2)[:, None] * pos[None, :]
    cos = np.cos(ang).astype(np.float32).astype(bf16)
    sin = np.sin(ang).astype(np.float32).astype(bf16)
    kl = np.arange(P)[:, None]
    ql = np.arange(P)[None, :]
    maskt = np.where(kl <= ql, 0.0, NEG).astype(np.float32)
    return ident, pm, cos, sin, maskt


_UPTO = "full"
_TAP = None
_LAST_EM = None


def kernel(L0_attn_x, L0_attn_W_qkv, L0_attn_W_o, L0_ffn_router_w, L0_ffn_router_b,
           L0_ffn_expert_w, L0_ffn_expert_b, L0_ffn_proj_w, L0_ffn_proj_b, L0_ffn_norm_w):
    x = np.ascontiguousarray(np.asarray(L0_attn_x, dtype=np.float32))
    wqkv = np.ascontiguousarray(np.asarray(L0_attn_W_qkv, dtype=np.float32))
    wo = np.ascontiguousarray(np.asarray(L0_attn_W_o, dtype=np.float32))
    rw = np.ascontiguousarray(np.asarray(L0_ffn_router_w, dtype=np.float32))
    rb = np.ascontiguousarray(np.asarray(L0_ffn_router_b, dtype=np.float32)).reshape(1, E)
    ew = np.ascontiguousarray(np.asarray(L0_ffn_expert_w, dtype=np.float32))
    eb = np.ascontiguousarray(np.asarray(L0_ffn_expert_b, dtype=np.float32))
    pw = np.ascontiguousarray(np.asarray(L0_ffn_proj_w, dtype=np.float32))
    pb = np.ascontiguousarray(np.asarray(L0_ffn_proj_b, dtype=np.float32)).reshape(1, D)
    nw = np.ascontiguousarray(np.asarray(L0_ffn_norm_w, dtype=np.float32)).reshape(1, D)

    ebl = np.ascontiguousarray(eb.reshape(E, DC, P).transpose(2, 1, 0).reshape(P, DC * E))
    ident, pm, cos, sin, maskt = _host_constants()

    nc = build_program(_UPTO, _TAP)
    in_maps = []
    for c in range(8):
        xs = np.ascontiguousarray(x[SEQ * c:SEQ * (c + 1)].reshape(TOK, D))
        in_maps.append({
            "x_in": xs, "w_qkv": wqkv, "w_o": wo, "router_w": rw, "router_b": rb,
            "expert_w": ew, "expert_b_laid": ebl, "proj_w": pw, "proj_b": pb,
            "norm_w": nw, "c_ident": ident, "c_rope_pm": pm, "c_cos": cos,
            "c_sin": sin, "c_maskt": maskt,
        })
    res = run_bass_kernel_spmd(nc, in_maps, core_ids=list(range(8)))
    out = np.stack([np.asarray(r["x_out"]).reshape(SEQ, T, D) for r in res.results], axis=0)
    return out.reshape(16, T, D)
